# version 04
# baseline (speedup 1.0000x reference)
import numpy as np
from contextlib import ExitStack
import ml_dtypes
import concourse.bass as bass
import concourse.mybir as mybir
from concourse.bass_utils import run_bass_kernel_spmd

bfdt = ml_dtypes.bfloat16
F32 = mybir.dt.float32
BF16 = mybir.dt.bfloat16
AF = mybir.ActivationFunctionType
ALU = mybir.AluOpType
AX = mybir.AxisListType

NCORE = 8
B, L, D = 4, 8192, 1024
TOK = B * L
TPC = TOK // NCORE
NDT = D // 128
LN_EPS = 1e-5


def _f32(x):
    return np.ascontiguousarray(np.asarray(x, np.float32))


def _bfa(x):
    return np.ascontiguousarray(np.asarray(x).astype(bfdt))



ENGS = ("t", "v", "s", "y", "g")
NDMASEM = 8


class Sched:

    def __init__(self, nc):
        self.nc = nc
        self.steps = {e: [] for e in ENGS}
        self.count = {e: 0 for e in ENGS}
        for q in ("y", "g"):
            for i in range(NDMASEM):
                self.count[f"{q}{i}"] = 0
        self.rr = {"y": 0, "g": 0}
        self.eng_clock = {e: {} for e in ENGS}
        self.ev_clock = {}
        self.own_wait = {e: 0 for e in ENGS}

    def em(self, eng, fn, deps=(), dma=False):
        if dma and eng in ("y", "g"):
            key = f"{eng}{self.rr[eng]}"
            self.rr[eng] = (self.rr[eng] + 1) % NDMASEM
            inc = 16
        else:
            key = eng
            inc = 1
        req = {}
        if inc == 16 and self.count[key] > 0:
            req[key] = self.count[key]
            for k, v in self.ev_clock.get((key, self.count[key]), {}).items():
                if req.get(k, 0) < v:
                    req[k] = v
        for d in deps:
            if d is None:
                continue
            (k, v) = d
            if req.get(k, 0) < v:
                req[k] = v
            for k2, v2 in self.ev_clock.get((k, v), {}).items():
                if req.get(k2, 0) < v2:
                    req[k2] = v2
        ec = self.eng_clock[eng]
        waits = {}
        for k, v in req.items():
            if k == eng:
                if k == "t":
                    continue
                if v > self.own_wait[eng]:
                    waits[k] = v
                    self.own_wait[eng] = v
                continue
            if ec.get(k, 0) >= v:
                continue
            waits[k] = v
        self.count[key] += inc
        self.steps[eng].append((fn, waits, inc, key))
        for k, v in req.items():
            if ec.get(k, 0) < v:
                ec[k] = v
        if inc != 16:
            ec[key] = self.count[key]
        if inc == 16:
            evc = dict(req)
            evc[key] = self.count[key]
            self.ev_clock[(key, self.count[key])] = evc
        else:
            self.ev_clock[(key, self.count[key])] = dict(ec)
        return (key, self.count[key])

    def barrier(self):
        return [(k, v) for k, v in self.count.items() if v > 0]

    def build(self, block):
        nc = self.nc
        sems = {k: nc.alloc_semaphore(f"sem_{k}") for k in self.count}
        self.sems = sems
        final = self.barrier()

        def run_stream(engobj, eng):
            for (fn, waits, inc, key) in self.steps[eng]:
                for (wk, wv) in waits.items():
                    engobj.wait_ge(sems[wk], wv)
                inst = fn(engobj)
                inst.then_inc(sems[key], inc)
            for (wk, wv) in final:
                if self.eng_clock[eng].get(wk, 0) >= wv and wk != eng:
                    pass
                engobj.wait_ge(sems[wk], wv)

        @block.tensor
        def _(e):
            run_stream(e, "t")

        @block.vector
        def _(e):
            run_stream(e, "v")

        @block.scalar
        def _(e):
            run_stream(e, "s")

        @block.sync
        def _(e):
            run_stream(e, "y")

        @block.gpsimd
        def _(e):
            run_stream(e, "g")



def DMA(sch, eng, dst, src, deps=()):
    return sch.em(eng, lambda e: e.dma_start(out=dst, in_=src), deps, dma=True)


def DMAS(sch, eng, dst, src, deps=()):
    return sch.em(eng, lambda e: e.dma_start(out=dst, in_=src,
                                             allow_slow_non_contiguous=True),
                  deps, dma=True)


def MM(sch, out, lhsT, rhs, start, stop, deps=()):
    return sch.em("t", lambda e: e.matmul(out, lhsT, rhs, start=start, stop=stop), deps)


def TRANS(sch, out, in_, ident, deps=()):
    return sch.em("t", lambda e: e.transpose(out, in_, ident), deps)


def ACT(sch, out, in_, func, bias=0.0, scale=1.0, deps=(), accum_out=None):
    return sch.em("s", lambda e: e.activation(out, in_, func, bias=bias, scale=scale,
                                              accum_out=accum_out), deps)


def COPY(sch, out, in_, deps=(), eng="v"):
    return sch.em(eng, lambda e: e.tensor_copy(out, in_), deps)


def TT(sch, out, a, b, op, deps=(), eng="v"):
    return sch.em(eng, lambda e: e.tensor_tensor(out, a, b, op), deps)


def TS(sch, out, in_, s1, s2, op0, op1=ALU.bypass, deps=(), eng="v"):
    if s2 is None:
        return sch.em(eng, lambda e: e.tensor_scalar(out, in_, s1, None, op0), deps)
    return sch.em(eng, lambda e: e.tensor_scalar(out, in_, s1, s2, op0, op1), deps)


def STT(sch, out, in0, scalar, in1, op0, op1, deps=(), eng="v"):
    return sch.em(eng, lambda e: e.scalar_tensor_tensor(out, in0, scalar, in1, op0, op1), deps)


def RED(sch, out, in_, op, deps=(), negate=False, eng="v"):
    if op == "max":
        return sch.em(eng, lambda e: e.reduce_max(out, in_, axis=AX.X, negate=negate), deps)
    return sch.em(eng, lambda e: e.reduce_sum(out, in_, axis=AX.X, negate=negate), deps)


def RECIP(sch, out, in_, deps=(), eng="v"):
    return sch.em(eng, lambda e: e.reciprocal(out, in_), deps)


def COLL(sch, kind, op, ins, outs, rg, deps=()):
    return sch.em("g", lambda e: e.collective_compute(kind, op, replica_groups=rg,
                                                      ins=ins, outs=outs), deps)



def consts(nc):
    c = {}
    c["ones_col"] = nc.inline_tensor(_bfa(np.ones((128, 1))), "c_ones_col").ap()
    c["ones_row"] = nc.inline_tensor(_bfa(np.ones((1, 128))), "c_ones_row").ap()
    c["ident"] = nc.inline_tensor(_bfa(np.eye(128)), "c_ident").ap()
    return c


def emit_ln_stats(sch, sb, blk_xf_tiles, xp_tiles, TB, ones_col_sb, after):
    evs = []
    cast_evs = []
    for i in range(NDT):
        cast_evs.append(COPY(sch, xp_tiles[i], blk_xf_tiles[i], deps=after))
    ev_mm1 = None
    for i in range(NDT):
        ev_mm1 = MM(sch, sb["r1"], ones_col_sb, xp_tiles[i], start=(i == 0),
                    stop=(i == NDT - 1), deps=[cast_evs[i]])
    prev_mm = None
    for i in range(NDT):
        d = [cast_evs[i]] + ([prev_mm] if prev_mm else [])
        ev_sq = ACT(sch, sb["sq"], xp_tiles[i], AF.Square, deps=d)
        prev_mm = MM(sch, sb["r2"], ones_col_sb, sb["sq"], start=(i == 0),
                     stop=(i == NDT - 1), deps=[ev_sq])
    e1 = TS(sch, sb["row_m"], sb["r1"], 1.0 / D, None, ALU.mult, deps=[ev_mm1])
    e2 = TS(sch, sb["row_v"], sb["r2"], 1.0 / D, None, ALU.mult, deps=[prev_mm])
    e3 = TT(sch, sb["row_t"], sb["row_m"], sb["row_m"], ALU.mult, deps=[e1])
    e4 = TT(sch, sb["row_v"], sb["row_v"], sb["row_t"], ALU.subtract, deps=[e2, e3])
    e4b = TS(sch, sb["row_v"], sb["row_v"], LN_EPS, None, ALU.add, deps=[e4])
    e5 = ACT(sch, sb["row_t"], sb["row_v"], AF.Sqrt, deps=[e4b])
    e6 = RECIP(sch, sb["row_r"], sb["row_t"], deps=[e5])
    e7 = TT(sch, sb["row_t"], sb["row_m"], sb["row_r"], ALU.mult, deps=[e6])
    e8 = TS(sch, sb["row_uf"], sb["row_t"], -1.0, None, ALU.mult, deps=[e7])
    e9 = COPY(sch, sb["row_rb"], sb["row_r"], deps=[e6])
    e10 = COPY(sch, sb["row_ub"], sb["row_uf"], deps=[e8])
    return [e9, e10], cast_evs



def build_l0_proj():
    TB = 512
    NB = TPC // TB
    nc = bass.Bass("TRN2", target_bir_lowering=False)
    xT = nc.dram_tensor("xT", [D, TPC], F32, kind="ExternalInput").ap()
    Wq = nc.dram_tensor("Wq", [D, D], F32, kind="ExternalInput").ap()
    Wk = nc.dram_tensor("Wk", [D, D], F32, kind="ExternalInput").ap()
    Wv = nc.dram_tensor("Wv", [D, D], F32, kind="ExternalInput").ap()
    Wb = nc.dram_tensor("Wb", [D, 8], F32, kind="ExternalInput").ap()
    qT = nc.dram_tensor("qT", [D, TPC], BF16, kind="ExternalOutput").ap()
    kT = nc.dram_tensor("kT", [D, TPC], BF16, kind="ExternalOutput").ap()
    vT = nc.dram_tensor("vT", [D, TPC], BF16, kind="ExternalOutput").ap()
    betaT = nc.dram_tensor("betaT", [8, TPC], F32, kind="ExternalOutput").ap()
    scr = nc.dram_tensor("scr", [8, TB], BF16, kind="Internal")

    cs = consts(nc)
    colsel = []
    for h in range(8):
        m = np.zeros((128, 8), np.float32)
        m[:, h] = 1.0
        colsel.append(nc.inline_tensor(_bfa(m), f"c_colsel{h}").ap())

    sch = Sched(nc)
    with ExitStack() as _st:
        xblk = _st.enter_context(nc.sbuf_tensor("xblk", [128, NDT * TB], F32))
        w_sb = _st.enter_context(nc.sbuf_tensor("w_sb", [128, 3 * NDT * D // 128 * 128 + NDT * 8], BF16))
        xp = _st.enter_context(nc.sbuf_tensor("xp", [128, NDT * TB], BF16))
        xb = _st.enter_context(nc.sbuf_tensor("xb", [128, NDT * TB], BF16))
        sq = _st.enter_context(nc.sbuf_tensor("sq", [128, TB], BF16))
        rows = _st.enter_context(nc.sbuf_tensor("rows", [1, 6 * TB], F32))
        rowsb = _st.enter_context(nc.sbuf_tensor("rowsb", [1, 2 * TB], BF16))
        sw = _st.enter_context(nc.sbuf_tensor("sw", [1, 3 * D + 8], BF16))
        qout = _st.enter_context(nc.sbuf_tensor("qout", [128, NDT * TB], BF16))
        kout = _st.enter_context(nc.sbuf_tensor("kout", [128, NDT * TB], BF16))
        vout = _st.enter_context(nc.sbuf_tensor("vout", [128, NDT * TB], BF16))
        bout = _st.enter_context(nc.sbuf_tensor("bout", [8, TB], F32))
        ksq = _st.enter_context(nc.sbuf_tensor("ksq", [128, TB], BF16))
        nrm = _st.enter_context(nc.sbuf_tensor("nrm", [8, TB], F32))
        nrmb = _st.enter_context(nc.sbuf_tensor("nrmb", [8, TB], BF16))
        nrow = _st.enter_context(nc.sbuf_tensor("nrow", [1, 8 * TB], BF16))
        onesc = _st.enter_context(nc.sbuf_tensor("onesc", [128, 1], BF16))
        onesr = _st.enter_context(nc.sbuf_tensor("onesr", [1, 128], BF16))
        colsel_sb = _st.enter_context(nc.sbuf_tensor("colsel_sb", [128, 64], BF16))
        ps_p = _st.enter_context(nc.psum_tensor("ps_p", [128, TB], F32))
        ps_r1 = _st.enter_context(nc.psum_tensor("ps_r1", [1, TB], F32))
        ps_r2 = _st.enter_context(nc.psum_tensor("ps_r2", [1, TB], F32))
        ps_n = _st.enter_context(nc.psum_tensor("ps_n", [8, TB], F32))
        ps_sw = _st.enter_context(nc.psum_tensor("ps_sw", [1, 128], F32))
        ps_bc = _st.enter_context(nc.psum_tensor("ps_bc", [128, TB], F32))
        block = _st.enter_context(nc.Block())
        WQ0, WK0, WV0, WB0 = 0, NDT * D, 2 * NDT * D, 3 * NDT * D

        def wt(base, k, j):
            o = base + (k * NDT + j) * 128
            return w_sb[:, o:o + 128]

        def wbt(k):
            return w_sb[:, WB0 + k * 8:WB0 + k * 8 + 8]

        e_c1 = DMA(sch, "y", onesc[:], cs["ones_col"])
        e_c2 = DMA(sch, "y", onesr[:], cs["ones_row"])
        e_cs = [DMA(sch, "y", colsel_sb[:, h * 8:(h + 1) * 8], colsel[h]) for h in range(8)]
        e_wq = DMA(sch, "g", w_sb[:, WQ0:WQ0 + NDT * D].rearrange("p (k j n) -> p k j n", j=NDT, n=128),
                   Wq.rearrange("(k p) (j n) -> p k j n", p=128, n=128))
        e_wk = DMA(sch, "g", w_sb[:, WK0:WK0 + NDT * D].rearrange("p (k j n) -> p k j n", j=NDT, n=128),
                   Wk.rearrange("(k p) (j n) -> p k j n", p=128, n=128))
        e_wv = DMA(sch, "g", w_sb[:, WV0:WV0 + NDT * D].rearrange("p (k j n) -> p k j n", j=NDT, n=128),
                   Wv.rearrange("(k p) (j n) -> p k j n", p=128, n=128))
        e_wb = DMA(sch, "g", w_sb[:, WB0:WB0 + NDT * 8].rearrange("p (k n) -> p k n", n=8),
                   Wb.rearrange("(k p) n -> p k n", p=128))
        e_w = [e_wq, e_wk, e_wv, e_wb, e_c1, e_c2] + e_cs

        e_sw = []
        for (base, nj, wtf) in ((WQ0, NDT, None), (WK0, NDT, None), (WV0, NDT, None)):
            for j in range(NDT):
                ev = None
                for k in range(NDT):
                    d = list(e_w) + ([e_sw[-1]] if e_sw else [])
                    ev = MM(sch, ps_sw[:], onesc[:], wt(base, k, j), start=(k == 0),
                            stop=(k == NDT - 1), deps=d)
                e_sw.append(COPY(sch, sw[:, base // NDT + j * 128: base // NDT + j * 128 + 128], ps_sw[:], deps=[ev]))
        ev = None
        for k in range(NDT):
            ev = MM(sch, ps_sw[:, 0:8], onesc[:], wbt(k), start=(k == 0),
                    stop=(k == NDT - 1), deps=list(e_w) + [e_sw[-1]])
        e_sw.append(COPY(sch, sw[:, 3 * D:3 * D + 8], ps_sw[:, 0:8], deps=[ev]))

        sb = {
            "sq": sq[:], "r1": ps_r1[:], "r2": ps_r2[:],
            "row_m": rows[:, 0:TB], "row_v": rows[:, TB:2 * TB],
            "row_t": rows[:, 2 * TB:3 * TB], "row_r": rows[:, 3 * TB:4 * TB],
            "row_uf": rows[:, 4 * TB:5 * TB],
            "row_rb": rowsb[:, 0:TB], "row_ub": rowsb[:, TB:2 * TB],
        }

        prev_block_done = sch.barrier()
        for blk in range(NB):
            xf_t = [xblk[:, i * TB:(i + 1) * TB] for i in range(NDT)]
            xp_t = [xp[:, i * TB:(i + 1) * TB] for i in range(NDT)]
            xb_t = [xb[:, i * TB:(i + 1) * TB] for i in range(NDT)]
            e_x = DMA(sch, "y", xblk[:].rearrange("p (a n) -> p a n", n=TB),
                      xT.rearrange("(a p) n -> p a n", p=128)[:, :, blk * TB:(blk + 1) * TB],
                      deps=prev_block_done)
            ev_rows, _ = emit_ln_stats(sch, sb, xf_t, xp_t, TB, onesc[:], [e_x])
            e_rbc = MM(sch, ps_bc[:], onesr[:], sb["row_rb"], start=True, stop=True,
                       deps=ev_rows)
            e_xb = [TT(sch, xb_t[i], xf_t[i], ps_bc[:], ALU.mult, deps=[e_rbc])
                    for i in range(NDT)]
            e_xb_all = e_xb

            prev_evac = [None]

            def proj(base, j, outtile, act, extra_deps=(), swoff=None):
                ev = None
                for k in range(NDT):
                    d = list(extra_deps) + [e_xb[k]]
                    if k == 0 and prev_evac[0] is not None:
                        d.append(prev_evac[0])
                    ev = MM(sch, ps_p[:], wt(base, k, j), xb_t[k], start=(k == 0),
                            stop=False, deps=d)
                so = swoff if swoff is not None else base // NDT + j * 128
                ev = MM(sch, ps_p[:], sw[:, so:so + 128], sb["row_ub"], start=False,
                        stop=True, deps=[ev] + list(ev_rows))
                if act == "copy":
                    e = COPY(sch, outtile, ps_p[:], deps=[ev])
                elif act == "silu":
                    e = ACT(sch, outtile, ps_p[:], AF.Silu, deps=[ev])
                else:
                    raise ValueError(act)
                prev_evac[0] = e
                return e

            e_q = [proj(WQ0, j, qout[:, j * TB:(j + 1) * TB], "copy") for j in range(NDT)]
            e_k = [proj(WK0, j, kout[:, j * TB:(j + 1) * TB], "copy") for j in range(NDT)]
            e_v = [proj(WV0, j, vout[:, j * TB:(j + 1) * TB], "silu") for j in range(NDT)]
            ev = None
            for k in range(NDT):
                ev = MM(sch, ps_n[:], wbt(k), xb_t[k], start=(k == 0), stop=False,
                        deps=[e_xb[k]])
            ev = MM(sch, ps_n[:], sw[:, 3 * D:3 * D + 8], sb["row_ub"], start=False,
                    stop=True, deps=[ev] + list(ev_rows))
            e_b = ACT(sch, bout[:], ps_n[:], AF.Sigmoid, deps=[ev])

            evn = e_b
            prev = None
            for h in range(8):
                d = [e_k[h], e_b] + ([prev] if prev else [])
                e_ks = ACT(sch, ksq[:], kout[:, h * TB:(h + 1) * TB], AF.Square, deps=d)
                prev = MM(sch, ps_n[:], colsel_sb[:, h * 8:(h + 1) * 8], ksq[:],
                          start=(h == 0), stop=(h == 7), deps=[e_ks])
            e_n1 = ACT(sch, nrm[:], ps_n[:], AF.Sqrt, deps=[prev])
            e_n2 = TS(sch, nrm[:], nrm[:], 1e-12, None, ALU.max, deps=[e_n1])
            e_n3 = RECIP(sch, nrm[:], nrm[:], deps=[e_n2])
            e_n3 = COPY(sch, nrmb[:], nrm[:], deps=[e_n3])
            e_n4 = DMA(sch, "y", scr.ap(), nrmb[:], deps=[e_n3])
            e_n5 = DMA(sch, "y", nrow[:].rearrange("p (a n) -> p a n", n=TB),
                       scr.ap().rearrange("(u a) n -> u a n", u=1), deps=[e_n4])
            e_kn = []
            for h in range(8):
                e_bc = MM(sch, ps_bc[:], onesr[:],
                          nrow[:, h * TB:(h + 1) * TB], start=True, stop=True,
                          deps=[e_n5] + ([e_kn[-1]] if e_kn else []))
                e_kn.append(TT(sch, kout[:, h * TB:(h + 1) * TB],
                               kout[:, h * TB:(h + 1) * TB], ps_bc[:], ALU.mult,
                               deps=[e_bc]))

            sl = slice(blk * TB, (blk + 1) * TB)
            e_s = [
                DMA(sch, "y", qT.rearrange("(a p) n -> p a n", p=128)[:, :, sl],
                    qout[:].rearrange("p (a n) -> p a n", n=TB), deps=e_q),
                DMA(sch, "y", vT.rearrange("(a p) n -> p a n", p=128)[:, :, sl],
                    vout[:].rearrange("p (a n) -> p a n", n=TB), deps=e_v),
                DMA(sch, "y", betaT[:, sl], bout[:], deps=[e_b]),
                DMA(sch, "y", kT.rearrange("(a p) n -> p a n", p=128)[:, :, sl],
                    kout[:].rearrange("p (a n) -> p a n", n=TB), deps=e_kn),
            ]
            prev_block_done = e_s + [e_kn[-1]] + e_q + e_v + [e_b]
        sch.build(block)
    return nc, {}



def kernel(**inputs):
    raise NotImplementedError("under construction")



def MMg(sch, out, lhsT, rhs, start, stop, deps=()):
    return sch.em("t", lambda e: e.matmul(out, lhsT, rhs, start=start, stop=stop,
                                          skip_group_check=True), deps)


def build_l0_scan():
    C = 128
    NCH = L // C
    NBH = 4
    NJ = 7
    nc = bass.Bass("TRN2", target_bir_lowering=False)
    kf = nc.dram_tensor("kf", [NBH, 128, L], BF16, kind="ExternalInput").ap()
    qf = nc.dram_tensor("qf", [NBH, 128, L], BF16, kind="ExternalInput").ap()
    krm = nc.dram_tensor("krm", [NBH, L, 128], BF16, kind="ExternalInput").ap()
    vrm = nc.dram_tensor("vrm", [NBH, L, 128], BF16, kind="ExternalInput").ap()
    bcol = nc.dram_tensor("bcol", [NBH, 128, NCH], F32, kind="ExternalInput").ap()
    brow = nc.dram_tensor("brow", [NBH, 1, L], F32, kind="ExternalInput").ap()
    Oout = nc.dram_tensor("O", [NBH, L, 128], BF16, kind="ExternalOutput").ap()

    ml = -np.tril(np.ones((C, C), np.float32), -1)
    mu = -np.triu(np.ones((C, C), np.float32), 1)
    mui = np.triu(np.ones((C, C), np.float32))
    c_ml = nc.inline_tensor(_bfa(ml), "c_ml").ap()
    c_mu = nc.inline_tensor(_bfa(mu), "c_mu").ap()
    c_mui = nc.inline_tensor(_bfa(mui), "c_mui").ap()
    c_onesr = nc.inline_tensor(np.ones((1, 128), np.float32), "c_onesr_f").ap()

    sch = Sched(nc)
    with ExitStack() as _st:
        sb = {}
        for n, sh, dt in [
            ("ml", [C, C], BF16), ("mu", [C, C], BF16), ("mui", [C, C], BF16),
            ("onesr", [1, 128], F32),
            ("S", [128, 128], F32), ("S0b", [128, 128], BF16),
            ("X", [C, 128], BF16), ("N", [C, C], BF16), ("NT", [C, C], BF16),
            ("Lq", [C, C], BF16), ("bu", [C, C], BF16), ("tmpX", [C, 128], F32),
            ("bcolS", [128, NCH], F32), ("browS", [1, L], F32),
            ("kf0", [128, C], BF16), ("kf1", [128, C], BF16),
            ("qf0", [128, C], BF16), ("qf1", [128, C], BF16),
            ("krm0", [C, 128], BF16), ("krm1", [C, 128], BF16),
            ("vrm0", [C, 128], BF16), ("vrm1", [C, 128], BF16),
            ("Oo0", [C, 128], BF16), ("Oo1", [C, 128], BF16),
        ]:
            sb[n] = _st.enter_context(nc.sbuf_tensor(n, sh, dt))
        ps = {}
        for n in ["G", "T1", "KQ", "O", "A", "N2", "NT2", "S"]:
            ps[n] = _st.enter_context(nc.psum_tensor(f"ps{n}", [128, 128], F32))
        block = _st.enter_context(nc.Block())

        e_c = [DMA(sch, "y", sb["ml"][:], c_ml), DMA(sch, "y", sb["mu"][:], c_mu),
               DMA(sch, "y", sb["mui"][:], c_mui), DMA(sch, "y", sb["onesr"][:], c_onesr)]

        for b in range(NBH):
            bh_start = sch.barrier()
            eb1 = DMA(sch, "y", sb["bcolS"][:], bcol[b], deps=bh_start)
            eb2 = DMA(sch, "y", sb["browS"][:], brow[b], deps=bh_start)
            ez1 = sch.em("v", lambda e: e.memset(sb["S"][:], 0.0), deps=bh_start)
            ez2 = sch.em("v", lambda e: e.memset(sb["S0b"][:], 0.0), deps=bh_start)
            e0 = [eb1, eb2, ez1, ez2] + e_c

            def bufs(c):
                p = c % 2
                return (sb[f"kf{p}"], sb[f"qf{p}"], sb[f"krm{p}"], sb[f"vrm{p}"],
                        sb[f"Oo{p}"], p)

            def load_chunk(c, deps):
                kfb, qfb, krmb, vrmb, _, _ = bufs(c)
                return [
                    DMA(sch, "y", kfb[:], kf[b][:, c * C:(c + 1) * C], deps=deps),
                    DMA(sch, "y", qfb[:], qf[b][:, c * C:(c + 1) * C], deps=deps),
                    DMA(sch, "y", krmb[:], krm[b][c * C:(c + 1) * C, :], deps=deps),
                    DMA(sch, "y", vrmb[:], vrm[b][c * C:(c + 1) * C, :], deps=deps),
                ]

            st_loaded = [None, None]
            st_freed = [e0, e0]
            st_store = [None, None]
            e_prevS = e0
            e_prevO = None
            st_loaded[0] = load_chunk(0, e0)

            for c in range(NCH):
                kfb, qfb, krmb, vrmb, Oob, p = bufs(c)
                eL = st_loaded[p]
                bc = sb["bcolS"][:, c:c + 1]
                eG = MM(sch, ps["G"][:], kfb[:], kfb[:], True, True, deps=eL)
                eT1 = MM(sch, ps["T1"][:], kfb[:], sb["S0b"][:], True, True,
                         deps=eL + e_prevS)
                eKQ = MM(sch, ps["KQ"][:], kfb[:], qfb[:], True, True, deps=eL)
                eO1 = MMg(sch, ps["O"][:], qfb[:], sb["S0b"][:], True, False,
                          deps=eL + e_prevS + ([e_prevO] if e_prevO else []))
                eN0 = sch.em("v", lambda e, bc=bc: e.tensor_scalar(
                    sb["tmpX"][:], ps["G"][:], bc, None, ALU.mult), deps=[eG])
                eN = TT(sch, sb["N"][:], sb["tmpX"][:], sb["ml"][:], ALU.mult, deps=[eN0])
                ebr = MM(sch, ps["A"][:], sb["onesr"][:],
                         sb["browS"][:, c * C:(c + 1) * C], True, True,
                         deps=e0 + [st_freed[p][0]] if False else e0)
                ebu = TT(sch, sb["bu"][:], ps["A"][:], sb["mu"][:], ALU.mult, deps=[ebr])
                eNT = TT(sch, sb["NT"][:], ps["G"][:], sb["bu"][:], ALU.mult,
                         deps=[eG, ebu])
                eX0 = STT(sch, sb["tmpX"][:], ps["T1"][:], -1.0, vrmb[:], ALU.mult,
                          ALU.add, deps=[eT1, eN])
                eX = sch.em("v", lambda e, bc=bc: e.tensor_scalar(
                    sb["X"][:], sb["tmpX"][:], bc, None, ALU.mult), deps=[eX0])
                eLq = TT(sch, sb["Lq"][:], ps["KQ"][:], sb["mui"][:], ALU.mult,
                         deps=[eKQ])
                eXc, eNc, eNTc = eX, eN, eNT
                for j in range(NJ):
                    eA = MM(sch, ps["A"][:], sb["NT"][:], sb["X"][:], True, True,
                            deps=[eXc, eNTc, ebu])
                    eXc = TT(sch, sb["X"][:], sb["X"][:], ps["A"][:], ALU.add, deps=[eA])
                    if j < NJ - 1:
                        eN2 = MM(sch, ps["N2"][:], sb["NT"][:], sb["N"][:], True, True,
                                 deps=[eNc, eNTc])
                        eNT2 = MM(sch, ps["NT2"][:], sb["N"][:], sb["NT"][:], True, True,
                                  deps=[eNc, eNTc])
                        eNc = COPY(sch, sb["N"][:], ps["N2"][:], deps=[eN2, eNT2, eA])
                        eNTc = COPY(sch, sb["NT"][:], ps["NT2"][:], deps=[eNT2, eA, eXc])
                eO2 = MMg(sch, ps["O"][:], sb["Lq"][:], sb["X"][:], False, True,
                          deps=[eO1, eLq, eXc])
                dO = [eO2] + ([st_store[p]] if st_store[p] else [])
                eOe = COPY(sch, Oob[:], ps["O"][:], deps=dO)
                e_prevO = eOe
                st_store[p] = DMA(sch, "y", Oout[b][c * C:(c + 1) * C, :], Oob[:],
                                  deps=[eOe])
                eS = MM(sch, ps["S"][:], krmb[:], sb["X"][:], True, True,
                        deps=[eXc] + eL + e_prevS)
                eS1 = TT(sch, sb["S"][:], sb["S"][:], ps["S"][:], ALU.add, deps=[eS])
                eS2 = COPY(sch, sb["S0b"][:], sb["S"][:], deps=[eS1])
                e_prevS = [eS1, eS2]
                st_freed[p] = [eG, eKQ, eX0, eS, eOe]
                if c + 1 < NCH:
                    st_loaded[(c + 1) % 2] = load_chunk(c + 1, st_freed[(c + 1) % 2])
        sch.build(block)
    return nc, {}



TWO_PI = float(2 * np.pi)


def emit_range_reduce(sch, t_f32, t_i32, scr_f32, deps):
    e1 = TS(sch, scr_f32, t_f32, 1.0 / TWO_PI, None, ALU.mult, deps=deps)
    e2 = COPY(sch, t_i32, scr_f32, deps=[e1])
    e3 = COPY(sch, scr_f32, t_i32, deps=[e2])
    e4 = STT(sch, t_f32, scr_f32, -TWO_PI, t_f32, ALU.mult, ALU.add, deps=[e3])
    e5 = TS(sch, scr_f32, t_f32, float(np.pi), None, ALU.is_gt, deps=[e4])
    e6 = STT(sch, t_f32, scr_f32, -TWO_PI, t_f32, ALU.mult, ALU.add, deps=[e5])
    e7 = TS(sch, scr_f32, t_f32, float(-np.pi), None, ALU.is_lt, deps=[e6])
    e8 = STT(sch, t_f32, scr_f32, TWO_PI, t_f32, ALU.mult, ALU.add, deps=[e7])
    return e8


def build_l0_postffn():
    TB = 256
    NB = TPC // TB
    DF = 4096
    NJ = DF // 128
    nc = bass.Bass("TRN2", target_bir_lowering=False)
    xT = nc.dram_tensor("xT", [D, TPC], F32, kind="ExternalInput").ap()
    oT = nc.dram_tensor("oT", [D, TPC], BF16, kind="ExternalInput").ap()
    lnw = nc.dram_tensor("lnw", [128, NDT], F32, kind="ExternalInput").ap()
    lnb = nc.dram_tensor("lnb", [128, NDT], F32, kind="ExternalInput").ap()
    W1 = nc.dram_tensor("W1", [D, DF], F32, kind="ExternalInput").ap()
    b1c = nc.dram_tensor("b1c", [128, NJ], F32, kind="ExternalInput").ap()
    thc = nc.dram_tensor("thc", [128, NJ], F32, kind="ExternalInput").ap()
    wrc = nc.dram_tensor("wrc", [128, NJ], F32, kind="ExternalInput").ap()
    wic = nc.dram_tensor("wic", [128, NJ], F32, kind="ExternalInput").ap()
    W2 = nc.dram_tensor("W2", [DF, D], F32, kind="ExternalInput").ap()
    b2c = nc.dram_tensor("b2c", [128, NDT], F32, kind="ExternalInput").ap()
    yT = nc.dram_tensor("yT", [D, TPC], F32, kind="ExternalOutput").ap()

    cs = consts(nc)
    sch = Sched(nc)
    with ExitStack() as _st:
        S = {}
        for n, sh, dt in [
            ("w1", [128, NDT * DF], BF16), ("w2", [128, NJ * D], BF16),
            ("x", [128, NDT * TB], F32), ("o", [128, NDT * TB], BF16),
            ("xp", [128, NDT * TB], BF16), ("xb", [128, NDT * TB], BF16),
            ("hid", [128, NJ * TB], BF16), ("sq", [128, TB], BF16),
            ("rows", [1, 6 * TB], F32), ("rowsb", [1, 2 * TB], BF16),
            ("sw1", [1, DF], BF16),
            ("s_lnw", [128, NDT], F32), ("s_lnb", [128, NDT], F32),
            ("s_b1", [128, NJ], F32), ("s_th", [128, NJ], F32),
            ("coef", [128, NJ], F32), ("s_b2", [128, NDT], F32),
            ("t1", [128, TB], F32), ("t2", [128, TB], F32), ("t3", [128, TB], F32),
            ("ti", [128, TB], mybir.dt.int32),
            ("c1", [128, NJ], F32), ("c2", [128, NJ], F32),
            ("ci", [128, NJ], mybir.dt.int32),
            ("onesc", [128, 1], BF16), ("onesr", [1, 128], BF16),
        ]:
            S[n] = _st.enter_context(nc.sbuf_tensor(n, sh, dt))
        P = {}
        for n, sh in [("p", [128, TB]), ("r1", [1, TB]), ("r2", [1, TB]),
                      ("bc1", [128, TB]), ("bc2", [128, TB]), ("sw", [1, 128])]:
            P[n] = _st.enter_context(nc.psum_tensor("ps" + n, sh, F32))
        block = _st.enter_context(nc.Block())

        e_c = [DMA(sch, "y", S["onesc"][:], cs["ones_col"]),
               DMA(sch, "y", S["onesr"][:], cs["ones_row"])]
        for nm, src in [("s_lnw", lnw), ("s_lnb", lnb), ("s_b1", b1c), ("s_th", thc),
                        ("s_b2", b2c)]:
            e_c.append(DMA(sch, "y", S[nm][:], src))
        e_wr = DMA(sch, "y", S["c1"][:], wrc)
        e_wi = DMA(sch, "y", S["c2"][:], wic)
        e_w1 = DMA(sch, "g", S["w1"][:].rearrange("p (k j n) -> p k j n", j=NJ, n=128),
                   W1.rearrange("(k p) (j n) -> p k j n", p=128, n=128))
        e_w2 = DMA(sch, "g", S["w2"][:].rearrange("p (k j n) -> p k j n", j=NDT, n=128),
                   W2.rearrange("(k p) (j n) -> p k j n", p=128, n=128))
        e_c += [e_w1, e_w2]

        etmp = COPY(sch, S["coef"][:], S["s_th"][:], deps=e_c + [e_wr, e_wi])
        err = emit_range_reduce(sch, S["coef"][:], S["ci"][:], S["t1"][:, 0:NJ], [etmp])
        esin = ACT(sch, S["t2"][:, 0:NJ], S["coef"][:], AF.Sin, deps=[err])
        ec1 = STT(sch, S["coef"][:], S["coef"][:], -1.0, S["coef"][:], ALU.mult,
                  ALU.bypass, deps=[esin])
        ec2 = TS(sch, S["coef"][:], S["coef"][:], float(np.pi / 2), None, ALU.add,
                 deps=[ec1])
        ec3 = emit_range_reduce(sch, S["coef"][:], S["ci"][:], S["t1"][:, 0:NJ], [ec2])
        ecos = ACT(sch, S["t3"][:, 0:NJ], S["coef"][:], AF.Sin, deps=[ec3])
        eq1 = TT(sch, S["t3"][:, 0:NJ], S["t3"][:, 0:NJ], S["c1"][:], ALU.mult,
                 deps=[ecos])
        eq2 = TT(sch, S["t2"][:, 0:NJ], S["t2"][:, 0:NJ], S["c2"][:], ALU.mult,
                 deps=[esin, ecos])
        eq3 = TT(sch, S["t3"][:, 0:NJ], S["t3"][:, 0:NJ], S["t2"][:, 0:NJ], ALU.add,
                 deps=[eq1, eq2])
        eq4 = TT(sch, S["coef"][:], S["t3"][:, 0:NJ], S["c2"][:], ALU.mult, deps=[eq3])
        e_coef = TS(sch, S["coef"][:], S["coef"][:], 0.1, None, ALU.mult, deps=[eq4])
        et1 = STT(sch, S["s_th"][:], S["s_b1"][:], 0.1, S["s_th"][:], ALU.mult, ALU.add,
                  deps=e_c + [ecos])
        e_th = emit_range_reduce(sch, S["s_th"][:], S["ci"][:], S["t1"][:, 0:NJ], [et1])

        e_sw = []
        prev = None
        for j in range(NJ):
            ev = None
            for k in range(NDT):
                d = list(e_c) + ([prev] if prev else [])
                ev = MM(sch, P["sw"][:], S["onesc"][:],
                        S["w1"][:, (k * NJ + j) * 128:(k * NJ + j) * 128 + 128],
                        start=(k == 0), stop=(k == NDT - 1), deps=d)
            prev = COPY(sch, S["sw1"][:, j * 128:(j + 1) * 128], P["sw"][:], deps=[ev])
            e_sw.append(prev)

        sb = {
            "sq": S["sq"][:], "r1": P["r1"][:], "r2": P["r2"][:],
            "row_m": S["rows"][:, 0:TB], "row_v": S["rows"][:, TB:2 * TB],
            "row_t": S["rows"][:, 2 * TB:3 * TB], "row_r": S["rows"][:, 3 * TB:4 * TB],
            "row_uf": S["rows"][:, 4 * TB:5 * TB],
            "row_rb": S["rowsb"][:, 0:TB], "row_ub": S["rowsb"][:, TB:2 * TB],
        }

        prev_blk = sch.barrier()
        for blk in range(NB):
            sl = slice(blk * TB, (blk + 1) * TB)
            x_t = [S["x"][:, i * TB:(i + 1) * TB] for i in range(NDT)]
            o_t = [S["o"][:, i * TB:(i + 1) * TB] for i in range(NDT)]
            xp_t = [S["xp"][:, i * TB:(i + 1) * TB] for i in range(NDT)]
            xb_t = [S["xb"][:, i * TB:(i + 1) * TB] for i in range(NDT)]
            e_x = DMA(sch, "y", S["x"][:].rearrange("p (a n) -> p a n", n=TB),
                      xT.rearrange("(a p) n -> p a n", p=128)[:, :, sl], deps=prev_blk)
            e_o = DMA(sch, "y", S["o"][:].rearrange("p (a n) -> p a n", n=TB),
                      oT.rearrange("(a p) n -> p a n", p=128)[:, :, sl], deps=prev_blk)

            ev_mm1 = None
            for i in range(NDT):
                ev_mm1 = MM(sch, sb["r1"], S["onesc"][:], o_t[i], start=(i == 0),
                            stop=(i == NDT - 1), deps=[e_o] + prev_blk)
            prev_mm = None
            for i in range(NDT):
                d = [e_o] + ([prev_mm] if prev_mm else []) + prev_blk
                ev_sq = ACT(sch, sb["sq"], o_t[i], AF.Square, deps=d)
                prev_mm = MM(sch, sb["r2"], S["onesc"][:], sb["sq"], start=(i == 0),
                             stop=(i == NDT - 1), deps=[ev_sq])
            e1 = TS(sch, sb["row_m"], sb["r1"], 1.0 / D, None, ALU.mult, deps=[ev_mm1])
            e2 = TS(sch, sb["row_v"], sb["r2"], 1.0 / D, None, ALU.mult, deps=[prev_mm])
            e3 = TT(sch, sb["row_t"], sb["row_m"], sb["row_m"], ALU.mult, deps=[e1])
            e4 = TT(sch, sb["row_v"], sb["row_v"], sb["row_t"], ALU.subtract, deps=[e2, e3])
            e4b = TS(sch, sb["row_v"], sb["row_v"], LN_EPS, None, ALU.add, deps=[e4])
            e5 = ACT(sch, sb["row_t"], sb["row_v"], AF.Sqrt, deps=[e4b])
            e6 = RECIP(sch, sb["row_r"], sb["row_t"], deps=[e5])
            e7 = TT(sch, sb["row_t"], sb["row_m"], sb["row_r"], ALU.mult, deps=[e6])
            e8 = TS(sch, sb["row_uf"], sb["row_t"], -1.0, None, ALU.mult, deps=[e7])
            e9 = COPY(sch, sb["row_rb"], sb["row_r"], deps=[e6])
            e10 = COPY(sch, sb["row_ub"], sb["row_uf"], deps=[e8])
            eb1 = MM(sch, P["bc1"][:], S["onesr"][:], sb["row_rb"], True, True,
                     deps=[e9] + prev_blk)
            eb2 = MM(sch, P["bc2"][:], S["onesr"][:], sb["row_ub"], True, True,
                     deps=[e10] + prev_blk)
            e_xn = []
            for i in range(NDT):
                dprev = [e_xn[-1]] if e_xn else []
                ea = TT(sch, S["t1"][:], o_t[i], P["bc1"][:], ALU.mult,
                        deps=[eb1, e_o] + dprev)
                eb = TT(sch, S["t1"][:], S["t1"][:], P["bc2"][:], ALU.add, deps=[eb2, ea])
                ec = TS(sch, S["t1"][:], S["t1"][:], S["s_lnw"][:, i:i + 1],
                        S["s_lnb"][:, i:i + 1], ALU.mult, ALU.add, deps=[eb])
                ed = TT(sch, x_t[i], x_t[i], S["t1"][:], ALU.add, deps=[ec, e_x])
                e_xn.append(ed)
            ev_rows, cast_evs = emit_ln_stats(sch, sb, x_t, xp_t, TB, S["onesc"][:],
                                              e_xn + [eb1, eb2])
            eb3 = MM(sch, P["bc1"][:], S["onesr"][:], sb["row_rb"], True, True,
                     deps=ev_rows)
            e_xb = [TT(sch, xb_t[i], x_t[i], P["bc1"][:], ALU.mult, deps=[eb3, e_xn[i]])
                    for i in range(NDT)]
            e_hid = []
            prev_evac = None
            for j in range(NJ):
                ev = None
                for k in range(NDT):
                    d = [e_xb[k]] + ([prev_evac] if prev_evac else [])
                    ev = MM(sch, P["p"][:], S["w1"][:, (k * NJ + j) * 128:(k * NJ + j) * 128 + 128],
                            xb_t[k], start=(k == 0), stop=False, deps=d)
                ev = MM(sch, P["p"][:], S["sw1"][:, j * 128:(j + 1) * 128], sb["row_ub"],
                        False, True, deps=[ev] + ev_rows + e_sw)
                ja = j
                ea = TS(sch, S["t1"][:], P["p"][:], 0.1, S["s_th"][:, ja:ja + 1],
                        ALU.mult, ALU.add, deps=[ev, e_th])
                er = emit_range_reduce(sch, S["t1"][:], S["ti"][:], S["t2"][:], [ea])
                es = ACT(sch, S["t2"][:], S["t1"][:], AF.Sin, deps=[er])
                eh = TS(sch, S["t3"][:], P["p"][:], S["s_b1"][:, ja:ja + 1], None,
                        ALU.add, deps=[ev, es])
                eri = sch.em("v", lambda e, ja=ja: e.scalar_tensor_tensor(
                    S["t1"][:], S["t2"][:], S["coef"][:, ja:ja + 1], S["t3"][:],
                    ALU.mult, ALU.add), deps=[es, eh, e_coef])
                erl = ACT(sch, S["hid"][:, j * TB:(j + 1) * TB], S["t1"][:], AF.Relu,
                          deps=[eri])
                prev_evac = erl
                e_hid.append(erl)
            e_st = []
            prev_evac2 = None
            for i in range(NDT):
                ev = None
                for k in range(NJ):
                    d = [e_hid[k]] + ([prev_evac2] if prev_evac2 else [])
                    if i == 0 and k == 0:
                        d += e_hid
                    ev = MM(sch, P["p"][:], S["w2"][:, (k * NDT + i) * 128:(k * NDT + i) * 128 + 128],
                            S["hid"][:, k * TB:(k + 1) * TB], start=(k == 0),
                            stop=(k == NJ - 1), deps=d)
                eo = sch.em("v", lambda e, i=i: e.scalar_tensor_tensor(
                    x_t[i], P["p"][:], S["s_b2"][:, i:i + 1], x_t[i], ALU.add, ALU.add),
                    deps=[ev] + cast_evs)
                prev_evac2 = eo
                e_st.append(DMA(sch, "y", yT.rearrange("(a p) n -> p a n", p=128)[:, i:i + 1, sl],
                                x_t[i].rearrange("p (u n) -> p u n", u=1), deps=[eo]))
            prev_blk = e_st + [prev_evac2]
        sch.build(block)
    return nc, {}



def build_proj(spec, name):
    TB = 512
    NB = TPC // TB
    nc = bass.Bass("TRN2", target_bir_lowering=False)
    xT = nc.dram_tensor("xT", [D, TPC], F32, kind="ExternalInput").ap()
    waps, baps, oaps = {}, {}, {}
    totdo = 0
    for (pn, dout, act) in spec["projs"]:
        waps[pn] = nc.dram_tensor(f"W_{pn}", [D, dout], F32, kind="ExternalInput").ap()
        if pn in spec.get("biases", ()):
            baps[pn] = nc.dram_tensor(f"b_{pn}", [1, dout], F32, kind="ExternalInput").ap()
        oaps[pn] = nc.dram_tensor(f"{pn}T", [dout, TPC], BF16, kind="ExternalOutput").ap()
        totdo += dout
    gb = spec.get("gamma_beta", False)
    if gb:
        gam = nc.dram_tensor("gam", [128, NDT], F32, kind="ExternalInput").ap()
        betap = nc.dram_tensor("betap", [128, NDT], F32, kind="ExternalInput").ap()
    hn = spec.get("headnorm")
    if hn:
        for (nh, hd, src) in hn:
            nc.dram_tensor(f"n_{src}", [nh, 1], F32, kind="ExternalOutput")
    scr = nc.dram_tensor("scr", [max(1, totdo // 128), 128], F32, kind="Internal")
    cs = consts(nc)
    colsel = {}
    for (nh, hd, src) in (hn or ()):
        if (nh, hd) not in colsel:
            mats = []
            for i in range(NDT):
                m = np.zeros((128, nh), np.float32)
                for p in range(128):
                    h = (i * 128 + p) // hd
                    m[p, h] = 1.0
                mats.append(nc.inline_tensor(_bfa(m), f"c_cs{nh}_{hd}_{i}").ap())
            colsel[(nh, hd)] = mats

    sch = Sched(nc)
    with ExitStack() as _st:
        S = {}
        for n, sh, dt in [
            ("x", [128, NDT * TB], F32), ("xp", [128, NDT * TB], BF16),
            ("xb", [128, NDT * TB], BF16), ("sq", [128, TB], BF16),
            ("rows", [1, 6 * TB], F32), ("rowsb", [1, 2 * TB], BF16),
            ("w", [128, NDT * totdo], BF16), ("sw", [1, totdo], BF16),
            ("bcol", [128, totdo // 128], F32),
            ("onesc", [128, 1], BF16), ("onesr", [1, 128], BF16),
            ("gamc", [128, NDT], F32), ("betc", [128, NDT], F32),
            ("betb", [128, NDT], BF16),
            ("csel", [128, 64], BF16), ("nacc", [64, 2], F32),
            ("ksq", [128, TB], BF16), ("nred", [64, 1], F32),
        ]:
            S[n] = _st.enter_context(nc.sbuf_tensor(n, sh, dt))
        P = {}
        for n, sh in [("p", [128, TB]), ("r1", [1, TB]), ("r2", [1, TB]),
                      ("bc", [128, TB]), ("sw", [1, 128]), ("n", [64, TB])]:
            P[n] = _st.enter_context(nc.psum_tensor("ps" + n, sh, F32))
        block = _st.enter_context(nc.Block())

        e_c = [DMA(sch, "y", S["onesc"][:], cs["ones_col"]),
               DMA(sch, "y", S["onesr"][:], cs["ones_row"])]
        woff = {}
        off = 0
        for (pn, dout, act) in spec["projs"]:
            woff[pn] = off
            e_c.append(DMA(sch, "g",
                           S["w"][:, off * NDT:off * NDT + NDT * dout]
                           .rearrange("p (k j n) -> p k j n", j=dout // 128, n=128),
                           waps[pn].rearrange("(k p) (j n) -> p k j n", p=128, n=128)))
            off += dout
        if gb:
            e_c.append(DMA(sch, "y", S["gamc"][:], gam))
            e_c.append(DMA(sch, "y", S["betc"][:], betap))

        def wt(pn, k, j):
            dout = dict((p[0], p[1]) for p in spec["projs"])[pn]
            o = woff[pn] * NDT + (k * (dout // 128) + j) * 128
            return S["w"][:, o:o + 128]

        e_bc = []
        if gb or spec.get("biases"):
            if gb:
                e_bb = COPY(sch, S["betb"][:], S["betc"][:], deps=e_c)
            prevc = None
            for (pn, dout, act) in spec["projs"]:
                for j in range(dout // 128):
                    ev = None
                    if gb:
                        for k in range(NDT):
                            d = list(e_c) + [e_bb] + ([prevc] if prevc else [])
                            ev = MM(sch, P["sw"][:], S["betb"][:, k:k + 1], wt(pn, k, j),
                                    start=(k == 0), stop=(k == NDT - 1), deps=d)
                    if ev is None:
                        ev = sch.em("v", lambda e: e.memset(P["sw"][:], 0.0),
                                    deps=([prevc] if prevc else []) + list(e_c))
                    if pn in baps:
                        eb = DMA(sch, "y", S["rows"][:, 0:128],
                                 baps[pn][:, j * 128:(j + 1) * 128], deps=[ev])
                        ev = TT(sch, S["rows"][:, 0:128], S["rows"][:, 0:128], P["sw"][:],
                                ALU.add, deps=[eb, ev])
                        prevc = ev
                    else:
                        prevc = COPY(sch, S["rows"][:, 0:128], P["sw"][:], deps=[ev])
                    co = woff[pn] + j * 128
                    e1 = DMA(sch, "y", scr.ap()[co // 128:co // 128 + 1, :],
                             S["rows"][:, 0:128], deps=[prevc])
                    e_bc.append(e1)
                    prevc = e1
            e_bc.append(DMAS(sch, "y", S["bcol"][:],
                             scr.ap().rearrange("c p -> p c"), deps=e_bc))

        e_gs = []
        if gb:
            for (pn, dout, act) in spec["projs"]:
                for k in range(NDT):
                    for j in range(dout // 128):
                        e_gs.append(TS(sch, wt(pn, k, j), wt(pn, k, j),
                                       S["gamc"][:, k:k + 1], None, ALU.mult,
                                       deps=e_c + e_bc[-1:] + e_gs[-1:]))
        e_w = e_c + e_gs[-1:] if e_gs else list(e_c)

        e_sw = []
        prevmm = None
        for (pn, dout, act) in spec["projs"]:
            for j in range(dout // 128):
                ev = None
                for k in range(NDT):
                    d = list(e_w) + ([prevmm] if prevmm else [])
                    ev = MM(sch, P["sw"][:], S["onesc"][:], wt(pn, k, j),
                            start=(k == 0), stop=(k == NDT - 1), deps=d)
                prevmm = COPY(sch, S["sw"][:, woff[pn] + j * 128:woff[pn] + j * 128 + 128],
                              P["sw"][:], deps=[ev])
                e_sw.append(prevmm)

        sb = {
            "sq": S["sq"][:], "r1": P["r1"][:], "r2": P["r2"][:],
            "row_m": S["rows"][:, TB:2 * TB], "row_v": S["rows"][:, 2 * TB:3 * TB],
            "row_t": S["rows"][:, 3 * TB:4 * TB], "row_r": S["rows"][:, 4 * TB:5 * TB],
            "row_uf": S["rows"][:, 5 * TB:6 * TB],
            "row_rb": S["rowsb"][:, 0:TB], "row_ub": S["rowsb"][:, TB:2 * TB],
        }
        e_nrm = {}
        if hn:
            e_nrm[0] = sch.em("v", lambda e: e.memset(S["nacc"][:], -1.0), deps=[])

        prev_blk = sch.barrier()
        for blk in range(NB):
            sl = slice(blk * TB, (blk + 1) * TB)
            x_t = [S["x"][:, i * TB:(i + 1) * TB] for i in range(NDT)]
            xp_t = [S["xp"][:, i * TB:(i + 1) * TB] for i in range(NDT)]
            xb_t = [S["xb"][:, i * TB:(i + 1) * TB] for i in range(NDT)]
            e_x = DMA(sch, "y", S["x"][:].rearrange("p (a n) -> p a n", n=TB),
                      xT.rearrange("(a p) n -> p a n", p=128)[:, :, sl], deps=prev_blk)
            ev_rows, cast_evs = emit_ln_stats(sch, sb, x_t, xp_t, TB, S["onesc"][:],
                                              [e_x] + prev_blk)
            e_rbc = MM(sch, P["bc"][:], S["onesr"][:], sb["row_rb"], True, True,
                       deps=ev_rows + prev_blk)
            e_xb = [TT(sch, xb_t[i], x_t[i], P["bc"][:], ALU.mult, deps=[e_rbc])
                    for i in range(NDT)]
            prev_evac = [None]
            e_out = {}
            for (pn, dout, act) in spec["projs"]:
                e_out[pn] = []
                for j in range(dout // 128):
                    ev = None
                    for k in range(NDT):
                        d = [e_xb[k]] + ([prev_evac[0]] if prev_evac[0] else [])
                        ev = MM(sch, P["p"][:], wt(pn, k, j), xb_t[k], start=(k == 0),
                                stop=False, deps=d)
                    so = woff[pn] + j * 128
                    ev = MM(sch, P["p"][:], S["sw"][:, so:so + 128], sb["row_ub"],
                            False, True, deps=[ev] + ev_rows + e_sw[-1:])
                    ot = oaps[pn].rearrange("(a p) n -> p a n", p=128)
                    dst = S["ksq"][:]
                    hasb = gb or (pn in baps)
                    if hasb:
                        if act == "copy":
                            ee = TS(sch, dst, P["p"][:], S["bcol"][:, so // 128:so // 128 + 1],
                                    None, ALU.add, deps=[ev] + e_bc[-1:])
                        elif act == "silu":
                            ee = ACT(sch, dst, P["p"][:], AF.Silu,
                                     bias=S["bcol"][:, so // 128:so // 128 + 1],
                                     deps=[ev] + e_bc[-1:])
                        elif act == "sigmoid":
                            ee = ACT(sch, dst, P["p"][:], AF.Sigmoid,
                                     bias=S["bcol"][:, so // 128:so // 128 + 1],
                                     deps=[ev] + e_bc[-1:])
                    else:
                        if act == "copy":
                            ee = COPY(sch, dst, P["p"][:], deps=[ev])
                        elif act == "silu":
                            ee = ACT(sch, dst, P["p"][:], AF.Silu, deps=[ev])
                        elif act == "sigmoid":
                            ee = ACT(sch, dst, P["p"][:], AF.Sigmoid, deps=[ev])
                        elif act == "scale":
                            ee = TS(sch, dst, P["p"][:], spec["scale"][pn], None,
                                    ALU.mult, deps=[ev])
                    es = DMA(sch, "y", ot[:, j:j + 1, sl],
                             dst.rearrange("p (u n) -> p u n", u=1), deps=[ee])
                    prev_evac[0] = es
                    e_out[pn].append(es)
            prev_blk = [prev_evac[0]] + [e_x]
        sch.build(block)
    return nc, {}



def build_wvsum():
    nc = bass.Bass("TRN2", target_bir_lowering=False)
    Wv = nc.dram_tensor("Wv", [D, 8 * D], F32, kind="ExternalInput").ap()
    bv = nc.dram_tensor("bv", [1, 8 * D], F32, kind="ExternalInput").ap()
    Wt = nc.dram_tensor("Wt", [D, D], F32, kind="ExternalOutput").ap()
    bt = nc.dram_tensor("bt", [1, D], F32, kind="ExternalOutput").ap()
    sch = Sched(nc)
    with ExitStack() as _st:
        stg = _st.enter_context(nc.sbuf_tensor("stg", [128, 8 * D], F32))
        out = _st.enter_context(nc.sbuf_tensor("out", [128, D], F32))
        bvr = _st.enter_context(nc.sbuf_tensor("bvr", [1, 8 * D], F32))
        btr = _st.enter_context(nc.sbuf_tensor("btr", [1, D], F32))
        block = _st.enter_context(nc.Block())
        eprev = None
        for k in range(NDT):
            el = DMA(sch, "y", stg[:], Wv[k * 128:(k + 1) * 128, :],
                     deps=[eprev] if eprev else [])
            er = RED(sch, out[:].rearrange("p (j u) -> p j u", u=1),
                     stg[:].rearrange("p (j i) -> p j i", i=8), "sum", deps=[el])
            eprev = DMA(sch, "y", Wt[k * 128:(k + 1) * 128, :], out[:], deps=[er])
        el = DMA(sch, "y", bvr[:], bv)
        er = RED(sch, btr[:].rearrange("p (j u) -> p j u", u=1),
                 bvr[:].rearrange("p (j i) -> p j i", i=8), "sum", deps=[el])
        DMA(sch, "y", bt, btr[:], deps=[er])
        sch.build(block)
    return nc, {}



def build_l2_scan():
    NBH = 4
    QB = 512
    NQ = L // QB
    KB = 128
    NK = L // KB
    SCALE = 1.0 / np.sqrt(128.0)
    nc = bass.Bass("TRN2", target_bir_lowering=False)
    kf = nc.dram_tensor("kf", [NBH, 128, L], BF16, kind="ExternalInput").ap()
    qf = nc.dram_tensor("qf", [NBH, 128, L], BF16, kind="ExternalInput").ap()
    vrm = nc.dram_tensor("vrm", [NBH, L, 128], BF16, kind="ExternalInput").ap()
    OUT = nc.dram_tensor("OUT", [NBH, 128, L], BF16, kind="ExternalOutput").ap()
    cs = consts(nc)
    sch = Sched(nc)
    with ExitStack() as _st:
        S = {}
        for n, sh, dt in [
            ("kfA", [128, L], BF16), ("qfA", [128, L], BF16),
            ("kfB", [128, L], BF16), ("qfB", [128, L], BF16),
            ("vrA", [128, L], BF16), ("vrB", [128, L], BF16),
            ("pt0", [128, QB], BF16), ("pt1", [128, QB], BF16),
            ("sq", [128, QB], BF16), ("mrow", [1, 8], F32),
            ("nred", [1, QB], F32), ("ml", [1, 1], F32),
            ("negm", [128, 1], F32),
            ("lrow", [1, QB], F32), ("lrb", [1, QB], BF16),
            ("rbc", [128, QB], F32), ("outS", [128, QB], BF16),
            ("onesc", [128, 1], BF16), ("onesr", [1, 128], BF16),
        ]:
            S[n] = _st.enter_context(nc.sbuf_tensor(n, sh, dt))
        P = {}
        for n, sh in [("s0", [128, QB]), ("s1", [128, QB]), ("o", [128, QB]),
                      ("l", [1, QB]), ("b", [128, QB]), ("n", [1, QB])]:
            P[n] = _st.enter_context(nc.psum_tensor("ps" + n, sh, F32))
        block = _st.enter_context(nc.Block())

        e_c = [DMA(sch, "y", S["onesc"][:], cs["ones_col"]),
               DMA(sch, "y", S["onesr"][:], cs["ones_row"])]

        buf = ["A", "B"]
        e_load = {"A": None, "B": None}
        e_free = {"A": list(e_c), "B": list(e_c)}
        prev_qb = []

        def load_bh(b, pb, deps):
            evs = [
                DMA(sch, "y", S["kf" + pb][:], kf[b], deps=deps),
                DMA(sch, "y", S["qf" + pb][:], qf[b], deps=deps),
                DMA(sch, "y", S["vr" + pb][:].rearrange("p (c n) -> p c n", n=128),
                    vrm[b].rearrange("(c p) n -> p c n", p=128), deps=deps),
            ]
            return evs

        e_load["A"] = load_bh(0, "A", e_c)
        for b in range(NBH):
            pb = buf[b % 2]
            eL = e_load[pb]
            kfb, qfb, vrb = S["kf" + pb], S["qf" + pb], S["vr" + pb]
            e_m = sch.em("v", lambda e: e.memset(S["mrow"][:], 0.0), deps=e_c + e_free[pb])
            prevmm = None
            for (src, co) in ((qfb, 0), (kfb, 1)):
                for c in range(NK):
                    d = eL + [e_m] + ([prevmm] if prevmm else [])
                    esq = ACT(sch, S["sq"][:, 0:KB], src[:, c * KB:(c + 1) * KB],
                              AF.Square, deps=d)
                    prevmm = MM(sch, P["n"][:, 0:KB], S["onesc"][:], S["sq"][:, 0:KB],
                                True, True, deps=[esq])
                    er = RED(sch, S["ml"][:], P["n"][:, 0:KB], "max", deps=[prevmm])
                    prevmm = TT(sch, S["mrow"][:, co + 2:co + 3], S["mrow"][:, co + 2:co + 3],
                                S["ml"][:], ALU.max, deps=[er, e_m])
            e1 = ACT(sch, S["mrow"][:, 4:6], S["mrow"][:, 2:4], AF.Sqrt, deps=[prevmm])
            e2 = TT(sch, S["mrow"][:, 0:1], S["mrow"][:, 4:5], S["mrow"][:, 5:6],
                    ALU.mult, deps=[e1])
            e3 = TS(sch, S["mrow"][:, 0:1], S["mrow"][:, 0:1], -SCALE, None, ALU.mult,
                    deps=[e2])
            e4 = COPY(sch, S["lrb"][:, 0:1], S["mrow"][:, 0:1], deps=[e3])
            e5 = MM(sch, P["b"][:, 0:1], S["onesr"][:], S["lrb"][:, 0:1], True, True,
                    deps=[e4])
            e_nm = COPY(sch, S["negm"][:], P["b"][:, 0:1], deps=[e5])
            if b + 1 < NBH:
                e_load[buf[(b + 1) % 2]] = load_bh(b + 1, buf[(b + 1) % 2],
                                                   e_free[buf[(b + 1) % 2]])
            freed = []
            for qb in range(NQ):
                qsl = slice(qb * QB, (qb + 1) * QB)
                e_pv_prev = None
                e_l_prev = None
                e_exp = {}
                for kb in range(NK):
                    par = kb % 2
                    psS = P["s0"] if par == 0 else P["s1"]
                    ptb = S["pt0"] if par == 0 else S["pt1"]
                    d = eL + [e_nm]
                    if kb >= 2:
                        d = d + [e_exp[kb - 2]]
                    elif prev_qb:
                        d = d + [prev_qb[2 + par]]
                    eS = MM(sch, psS[:], kfb[:, kb * KB:(kb + 1) * KB], qfb[:, qsl],
                            True, True, deps=d)
                    dd = [eS]
                    if kb >= 2:
                        dd.append(e_pv_prev)
                    elif prev_qb:
                        dd.append(prev_qb[4 + par])
                    eE = ACT(sch, ptb[:], psS[:], AF.Exp, bias=S["negm"][:, 0:1],
                             scale=SCALE, deps=dd)
                    e_exp[kb] = eE
                    dpv = [eE] + ([e_pv_prev] if e_pv_prev else [])
                    dls = [eE] + ([e_l_prev] if e_l_prev else [])
                    if kb == 0 and prev_qb:
                        dpv += [prev_qb[0]]
                        dls += [prev_qb[1]]
                    ePV = MM(sch, P["o"][:], vrb[:, kb * 128:(kb + 1) * 128], ptb[:],
                             start=(kb == 0), stop=(kb == NK - 1), deps=dpv)
                    eLs = MM(sch, P["l"][:], S["onesc"][:], ptb[:],
                             start=(kb == 0), stop=(kb == NK - 1), deps=dls)
                    e_pv_prev, e_l_prev = ePV, eLs
                er1 = COPY(sch, S["lrow"][:], P["l"][:], deps=[e_l_prev])
                er2 = RECIP(sch, S["lrow"][:], S["lrow"][:], deps=[er1])
                er3 = COPY(sch, S["lrb"][:], S["lrow"][:], deps=[er2])
                er4 = MM(sch, P["b"][:], S["onesr"][:], S["lrb"][:], True, True,
                         deps=[er3, e_nm])
                er5 = COPY(sch, S["rbc"][:], P["b"][:], deps=[er4])
                dpo = [er5, e_pv_prev] + ([prev_qb[6]] if prev_qb else [])
                eo = TT(sch, S["outS"][:], P["o"][:], S["rbc"][:], ALU.mult, deps=dpo)
                es = DMA(sch, "y", OUT[b][:, qsl], S["outS"][:], deps=[eo])
                freed += [es, eo, e_l_prev]
                prev_qb = [eo, er1, e_exp.get(NK - 2), e_exp.get(NK - 1),
                           e_pv_prev, e_pv_prev, es]
            e_free[pb] = freed + [e_exp[NK - 1], e_exp[NK - 2]]
        sch.build(block)
    return nc, {}



def build_l2_postffn():
    TB = 512
    NB = TPC // TB
    nc = bass.Bass("TRN2", target_bir_lowering=False)
    xT = nc.dram_tensor("xT", [D, TPC], F32, kind="ExternalInput").ap()
    attnT = nc.dram_tensor("attnT", [D, TPC], BF16, kind="ExternalInput").ap()
    WrT = nc.dram_tensor("WrT", [D, D], F32, kind="ExternalInput").ap()
    WkT = nc.dram_tensor("WkT", [D, D], F32, kind="ExternalInput").ap()
    WvT = nc.dram_tensor("WvT", [D, D], F32, kind="ExternalInput").ap()
    mur = nc.dram_tensor("mur", [128, NDT], F32, kind="ExternalInput").ap()
    muk = nc.dram_tensor("muk", [128, NDT], F32, kind="ExternalInput").ap()
    xlast = nc.dram_tensor("xlast", [128, NDT], F32, kind="ExternalInput").ap()
    yT = nc.dram_tensor("yT", [D, TPC], F32, kind="ExternalOutput").ap()
    cs = consts(nc)
    sch = Sched(nc)
    with ExitStack() as _st:
        S = {}
        for n, sh, dt in [
            ("wr", [128, NDT * D], BF16), ("wk", [128, NDT * D], BF16),
            ("wv", [128, NDT * D], BF16),
            ("x", [128, NDT * TB], F32), ("at", [128, NDT * TB], BF16),
            ("xp", [128, NDT * TB], BF16), ("sq", [128, TB], BF16),
            ("xn", [128, NDT * (TB + 1)], F32),
            ("xr", [128, NDT * TB], BF16), ("xk", [128, NDT * TB], BF16),
            ("rg", [128, NDT * TB], F32), ("ka", [128, NDT * TB], BF16),
            ("rows", [1, 6 * TB], F32), ("rowsb", [1, 2 * TB], BF16),
            ("murS", [128, NDT], F32), ("mukS", [128, NDT], F32),
            ("xl", [128, NDT], F32), ("xlb", [128, NDT], BF16),
            ("hrow", [1, 16], F32),
            ("t1", [128, TB], F32), ("carry", [128, NDT], F32),
            ("onesc", [128, 1], BF16), ("onesr", [1, 128], BF16),
        ]:
            S[n] = _st.enter_context(nc.sbuf_tensor(n, sh, dt))
        P = {}
        for n, sh in [("p", [128, TB]), ("r1", [1, TB]), ("r2", [1, TB]),
                      ("bc", [128, TB]), ("h", [1, 16])]:
            P[n] = _st.enter_context(nc.psum_tensor("ps" + n, sh, F32))
        block = _st.enter_context(nc.Block())

        e_c = [DMA(sch, "y", S["onesc"][:], cs["ones_col"]),
               DMA(sch, "y", S["onesr"][:], cs["ones_row"]),
               DMA(sch, "y", S["murS"][:], mur), DMA(sch, "y", S["mukS"][:], muk),
               DMA(sch, "y", S["xl"][:], xlast)]
        for nm, src in [("wr", WrT), ("wk", WkT), ("wv", WvT)]:
            e_c.append(DMA(sch, "g",
                           S[nm][:].rearrange("p (k j n) -> p k j n", j=NDT, n=128),
                           src.rearrange("(k p) (j n) -> p k j n", p=128, n=128)))

        eb = COPY(sch, S["xlb"][:], S["xl"][:], deps=e_c)
        e1 = MM(sch, P["h"][:, 0:NDT], S["onesc"][:], S["xlb"][:], True, True, deps=[eb])
        er1 = RED(sch, S["hrow"][:, 0:1], P["h"][:, 0:NDT], "sum", deps=[e1])
        esq = ACT(sch, S["sq"][:, 0:NDT], S["xlb"][:], AF.Square, deps=[eb])
        e2 = MM(sch, P["h"][:, 8:8 + NDT], S["onesc"][:], S["sq"][:, 0:NDT], True, True,
                deps=[esq, er1])
        er2 = RED(sch, S["hrow"][:, 1:2], P["h"][:, 8:8 + NDT], "sum", deps=[e2])
        e3 = TS(sch, S["hrow"][:, 0:2], S["hrow"][:, 0:2], 1.0 / D, None, ALU.mult,
                deps=[er2, er1])
        e4 = TT(sch, S["hrow"][:, 2:3], S["hrow"][:, 0:1], S["hrow"][:, 0:1], ALU.mult,
                deps=[e3])
        e5 = TT(sch, S["hrow"][:, 3:4], S["hrow"][:, 1:2], S["hrow"][:, 2:3],
                ALU.subtract, deps=[e4])
        e6 = TS(sch, S["hrow"][:, 3:4], S["hrow"][:, 3:4], LN_EPS, None, ALU.add,
                deps=[e5])
        e7 = ACT(sch, S["hrow"][:, 4:5], S["hrow"][:, 3:4], AF.Sqrt, deps=[e6])
        e8 = RECIP(sch, S["hrow"][:, 5:6], S["hrow"][:, 4:5], deps=[e7])
        e9 = TT(sch, S["hrow"][:, 6:7], S["hrow"][:, 0:1], S["hrow"][:, 5:6], ALU.mult,
                deps=[e8])
        e10 = TS(sch, S["hrow"][:, 6:7], S["hrow"][:, 6:7], -1.0, None, ALU.mult,
                 deps=[e9])
        eb1 = COPY(sch, S["rowsb"][:, 0:2], S["hrow"][:, 5:7], deps=[e10])
        ebc = MM(sch, P["bc"][:, 0:2], S["onesr"][:], S["rowsb"][:, 0:2], True, True,
                 deps=[eb1])
        e_hl = STT(sch, S["carry"][:], S["xl"][:], P["bc"][:, 0:1], S["xl"][:],
                   ALU.mult, ALU.bypass, deps=[ebc])
        e_hl = sch.em("v", lambda e: e.tensor_scalar(
            S["carry"][:], S["carry"][:], P["bc"][:, 1:2], None, ALU.add), deps=[e_hl])

        def wt(nm, k, j):
            o = (k * NDT + j) * 128
            return S[nm][:, o:o + 128]

        sb = {
            "sq": S["sq"][:], "r1": P["r1"][:], "r2": P["r2"][:],
            "row_m": S["rows"][:, TB:2 * TB], "row_v": S["rows"][:, 2 * TB:3 * TB],
            "row_t": S["rows"][:, 3 * TB:4 * TB], "row_r": S["rows"][:, 4 * TB:5 * TB],
            "row_uf": S["rows"][:, 5 * TB:6 * TB],
            "row_rb": S["rowsb"][:, 0:TB], "row_ub": S["rowsb"][:, TB:2 * TB],
        }
        prev_blk = sch.barrier()
        for blk in range(NB):
            sl = slice(blk * TB, (blk + 1) * TB)
            x_t = [S["x"][:, i * TB:(i + 1) * TB] for i in range(NDT)]
            at_t = [S["at"][:, i * TB:(i + 1) * TB] for i in range(NDT)]
            xp_t = [S["xp"][:, i * TB:(i + 1) * TB] for i in range(NDT)]
            xn_t = [S["xn"][:, i * (TB + 1):(i + 1) * (TB + 1)] for i in range(NDT)]
            e_x = DMA(sch, "y", S["x"][:].rearrange("p (a n) -> p a n", n=TB),
                      xT.rearrange("(a p) n -> p a n", p=128)[:, :, sl], deps=prev_blk)
            e_a = DMA(sch, "y", S["at"][:].rearrange("p (a n) -> p a n", n=TB),
                      attnT.rearrange("(a p) n -> p a n", p=128)[:, :, sl], deps=prev_blk)
            e_xn = []
            for i in range(NDT):
                e_xn.append(TT(sch, x_t[i], x_t[i], at_t[i], ALU.add,
                               deps=[e_x, e_a] + prev_blk))
            ev_rows, cast_evs = emit_ln_stats(sch, sb, x_t, xp_t, TB, S["onesc"][:],
                                              e_xn)
            e_car = []
            for i in range(NDT):
                e_car.append(COPY(sch, xn_t[i][:, 0:1], S["carry"][:, i:i + 1],
                                  deps=[e_hl] + prev_blk))
            e_rbc = MM(sch, P["bc"][:], S["onesr"][:], sb["row_rb"], True, True,
                       deps=ev_rows)
            e_ubc = MM(sch, P["r1"][:] if False else P["bc"][:], None, None, True, True) if False else None
            e_ub2 = MM(sch, P["r2"][:] if False else P["bc"][:], None, None, True, True) if False else None
            e_xnf = []
            eu = MM(sch, P["p"][:], S["onesr"][:], sb["row_ub"], True, True,
                    deps=ev_rows)
            for i in range(NDT):
                ea = TT(sch, S["t1"][:], x_t[i], P["bc"][:], ALU.mult,
                        deps=[e_rbc, e_xn[i]] + (e_xnf[-1:] if e_xnf else []))
                ebx = TT(sch, xn_t[i][:, 1:TB + 1], S["t1"][:], P["p"][:], ALU.add,
                         deps=[ea, eu, e_car[i]])
                e_xnf.append(ebx)
            e_hl = None
            e_carn = []
            for i in range(NDT):
                e_carn.append(COPY(sch, S["carry"][:, i:i + 1], xn_t[i][:, TB:TB + 1],
                                   deps=[e_xnf[i]]))
            e_xr, e_xk = [], []
            for i in range(NDT):
                d = [e_xnf[i], e_car[i]]
                ea = TT(sch, S["t1"][:], xn_t[i][:, 1:TB + 1], xn_t[i][:, 0:TB],
                        ALU.subtract, deps=d + (e_xk[-1:] if e_xk else []))
                e1_ = sch.em("v", lambda e, i=i: e.scalar_tensor_tensor(
                    S["xr"][:, i * TB:(i + 1) * TB], S["t1"][:], S["murS"][:, i:i + 1],
                    xn_t[i][:, 0:TB], ALU.mult, ALU.add), deps=[ea])
                e_xr.append(e1_)
                e2_ = sch.em("v", lambda e, i=i: e.scalar_tensor_tensor(
                    S["xk"][:, i * TB:(i + 1) * TB], S["t1"][:], S["mukS"][:, i:i + 1],
                    xn_t[i][:, 0:TB], ALU.mult, ALU.add), deps=[e1_])
                e_xk.append(e2_)
            prev_ev = [e_xnf[-1]]

            def mmblock(wname, srcs, j, extra=()):
                ev = None
                for k in range(NDT):
                    d = [srcs[k]] + list(extra) + ([prev_ev[0]] if prev_ev[0] else [])
                    ev = MM(sch, P["p"][:], wt(wname, k, j),
                            S[{"wr": "xr", "wk": "xk", "wv": "ka"}[wname]][:, k * TB:(k + 1) * TB],
                            start=(k == 0), stop=(k == NDT - 1), deps=d)
                return ev

            e_r = []
            for j in range(NDT):
                ev = mmblock("wr", e_xr, j)
                ee = ACT(sch, S["rg"][:, j * TB:(j + 1) * TB], P["p"][:], AF.Sigmoid,
                         deps=[ev])
                prev_ev[0] = ee
                e_r.append(ee)
            e_ka = []
            for j in range(NDT):
                ev = mmblock("wk", e_xk, j)
                e1_ = TS(sch, S["t1"][:], P["p"][:], 0.0, None, ALU.max,
                         deps=[ev] + (e_ka[-1:] if e_ka else []))
                e2_ = ACT(sch, S["ka"][:, j * TB:(j + 1) * TB], S["t1"][:], AF.Square,
                          deps=[e1_])
                prev_ev[0] = e2_
                e_ka.append(e2_)
            e_st = []
            for j in range(NDT):
                ev = mmblock("wv", e_ka, j, extra=e_ka)
                eo = sch.em("v", lambda e, j=j: e.tensor_tensor(
                    S["t1"][:], P["p"][:], S["rg"][:, j * TB:(j + 1) * TB], ALU.mult),
                    deps=[ev] + e_r)
                prev_ev[0] = eo
                eo2 = TT(sch, x_t[j], x_t[j], S["t1"][:], ALU.add, deps=[eo] + cast_evs)
                e_st.append(DMA(sch, "y", yT.rearrange("(a p) n -> p a n", p=128)[:, j:j + 1, sl],
                                x_t[j].rearrange("p (u n) -> p u n", u=1), deps=[eo2]))
                prev_ev[0] = eo2
            prev_blk = e_st + e_carn + [prev_ev[0]]
        sch.build(block)
    return nc, {}



def build_l3_scan():
    NBH = 8
    HD = 64
    QB = 512
    NQ = L // QB
    KB = 128
    NK = L // KB
    SCALE = 1.0 / np.sqrt(HD)
    nc = bass.Bass("TRN2", target_bir_lowering=False)
    kf = nc.dram_tensor("kf", [NBH, HD, L], BF16, kind="ExternalInput").ap()
    qf = nc.dram_tensor("qf", [NBH, HD, L], BF16, kind="ExternalInput").ap()
    vrm = nc.dram_tensor("vrm", [NBH, L, HD], BF16, kind="ExternalInput").ap()
    OUT = nc.dram_tensor("OUT", [NBH, HD, L], BF16, kind="ExternalOutput").ap()
    cs = consts(nc)
    idf = nc.inline_tensor(np.eye(128, dtype=np.float32), "c_identf").ap()
    ones64 = nc.inline_tensor(_bfa(np.ones((1, HD))), "c_ones64").ap()
    sch = Sched(nc)
    with ExitStack() as _st:
        S = {}
        for n, sh, dt in [
            ("kfA", [HD, L], BF16), ("qfA", [HD, L], BF16), ("vrA", [128, NK * HD], BF16),
            ("kfB", [HD, L], BF16), ("qfB", [HD, L], BF16), ("vrB", [128, NK * HD], BF16),
            ("vaug", [128, 2 * 66], BF16), ("vaugL", [128, 66], BF16),
            ("pt0", [128, QB], BF16), ("pt1", [128, QB], BF16),
            ("mrun", [128, 4], F32), ("mtmp", [128, 4], F32),
            ("mlast", [128, 4], F32), ("mg", [128, 4], F32),
            ("dexp", [128, 4], F32),
            ("mrow", [1, QB], BF16), ("erow", [1, QB], F32),
            ("slrow", [1, QB], F32), ("den", [1, QB], F32), ("denb", [1, QB], BF16),
            ("rbc", [HD, QB], F32), ("outS", [HD, QB], BF16),
            ("onesc", [128, 1], BF16), ("onesr", [1, 128], BF16),
            ("ident", [128, 128], F32), ("o64", [1, HD], BF16),
        ]:
            S[n] = _st.enter_context(nc.sbuf_tensor(n, sh, dt))
        P = {}
        for n, sh in [("s0", [128, QB]), ("s1", [128, QB]), ("o", [66, QB]),
                      ("b", [HD, QB]), ("t", [1, 128]), ("st", [128, QB])]:
            P[n] = _st.enter_context(nc.psum_tensor("ps" + n, sh, F32))
        block = _st.enter_context(nc.Block())

        e_c = [DMA(sch, "y", S["onesc"][:], cs["ones_col"]),
               DMA(sch, "y", S["onesr"][:], cs["ones_row"]),
               DMA(sch, "y", S["ident"][:], idf),
               DMA(sch, "y", S["o64"][:], ones64)]
        ez1 = sch.em("v", lambda e: e.memset(S["vaug"][:], 0.0))
        ez2 = sch.em("v", lambda e: e.memset(S["vaugL"][:], 0.0), deps=[ez1])
        e_c.append(sch.em("v", lambda e: e.memset(S["vaugL"][64:128, 64:65], 1.0),
                          deps=[ez2]))

        buf = ["A", "B"]
        e_load = {"A": None, "B": None}
        e_free = {"A": list(e_c), "B": list(e_c)}
        prev_qb = []

        def load_bh(b, pb, deps):
            return [
                DMA(sch, "y", S["kf" + pb][:], kf[b], deps=deps),
                DMA(sch, "y", S["qf" + pb][:], qf[b], deps=deps),
                DMA(sch, "y", S["vr" + pb][:].rearrange("p (c n) -> p c n", n=HD),
                    vrm[b].rearrange("(c p) n -> p c n", p=128), deps=deps),
            ]

        e_load["A"] = load_bh(0, "A", e_c)
        for b in range(NBH):
            pb = buf[b % 2]
            eL = e_load[pb]
            kfb, qfb, vrb = S["kf" + pb], S["qf" + pb], S["vr" + pb]
            if b + 1 < NBH:
                e_load[buf[(b + 1) % 2]] = load_bh(b + 1, buf[(b + 1) % 2],
                                                   e_free[buf[(b + 1) % 2]])
            freed = []
            for qb in range(NQ):
                qsl = slice(qb * QB, (qb + 1) * QB)
                e_mi = sch.em("v", lambda e: e.memset(S["mrun"][:], -1e30),
                              deps=prev_qb + e_c)
                e_ml = sch.em("v", lambda e: e.memset(S["mlast"][:], -1e30),
                              deps=[e_mi])
                eSt_prev = None
                NSK = L // 512
                for sk in range(NSK):
                    for qs in range(4):
                        d = eL + [e_ml] + ([eSt_prev] if eSt_prev else [])
                        eS = MM(sch, P["st"][:], qfb[:, qb * QB + qs * 128: qb * QB + (qs + 1) * 128],
                                kfb[:, sk * 512:(sk + 1) * 512], True, True, deps=d)
                        if sk < NSK - 1:
                            er = RED(sch, S["mtmp"][:, qs:qs + 1], P["st"][:], "max",
                                     deps=[eS])
                        else:
                            er = RED(sch, S["mtmp"][:, qs:qs + 1], P["st"][:, 0:448],
                                     "max", deps=[eS])
                            er2 = RED(sch, S["mlast"][:, qs:qs + 1], P["st"][:, 448:512],
                                      "max", deps=[eS, er])
                        eSt_prev = TT(sch, S["mrun"][:, qs:qs + 1], S["mrun"][:, qs:qs + 1],
                                      S["mtmp"][:, qs:qs + 1], ALU.max,
                                      deps=[er] if sk < 15 else [er2])
                e_g = TT(sch, S["mg"][:], S["mrun"][:], S["mlast"][:], ALU.max,
                         deps=[eSt_prev])
                e_d = TT(sch, S["dexp"][:], S["mrun"][:], S["mg"][:], ALU.subtract,
                         deps=[e_g])
                e_d = ACT(sch, S["dexp"][:], S["dexp"][:], AF.Exp, scale=SCALE,
                          deps=[e_d])
                e_d = TS(sch, S["dexp"][:], S["dexp"][:], 1.0, None, ALU.add, deps=[e_d])
                e_mr = []
                for qs in range(4):
                    et = TRANS(sch, P["t"][:], S["mg"][:, qs:qs + 1], S["ident"][:],
                               deps=[e_g] + (e_mr[-1:] if e_mr else []))
                    e1 = TS(sch, S["mrow"][:, qs * 128:(qs + 1) * 128], P["t"][:],
                            -1.0, None, ALU.mult, deps=[et])
                    e_mr.append(e1)
                    et2 = TRANS(sch, P["t"][:], S["dexp"][:, qs:qs + 1], S["ident"][:],
                                deps=[e_d, e1])
                    e2 = COPY(sch, S["erow"][:, qs * 128:(qs + 1) * 128], P["t"][:],
                              deps=[et2])
                    e_mr.append(e2)
                e_pv_prev = None
                e_exp = {}
                eVload_prev = None
                for kb in range(NK):
                    par = kb % 2
                    psS = P["s0"] if par == 0 else P["s1"]
                    ptb = S["pt0"] if par == 0 else S["pt1"]
                    d = eL + e_mr
                    if kb >= 2:
                        d = d + [e_exp[kb - 2]]
                    elif prev_qb:
                        d = d + [prev_qb[2 + par]]
                    eS = MM(sch, psS[:], kfb[:, kb * KB:(kb + 1) * KB], qfb[:, qsl],
                            True, False, deps=d)
                    eS = MM(sch, psS[:], S["onesr"][:], S["mrow"][:], False, True,
                            deps=[eS] + e_mr)
                    dd = [eS]
                    if kb >= 2:
                        dd.append(e_pv_prev)
                    elif prev_qb:
                        dd.append(prev_qb[4 + par])
                    eE = ACT(sch, ptb[:], psS[:], AF.Exp, scale=SCALE, deps=dd)
                    e_exp[kb] = eE
                    if kb == NK - 1:
                        vt = S["vaugL"]
                        ecp = COPY(sch, S["vaugL"][:, 0:HD],
                                   vrb[:, kb * HD:(kb + 1) * HD],
                                   deps=eL + [e_exp.get(kb - 2)] + e_c)
                    else:
                        vt = S["vaug"][:, (kb % 2) * 66:(kb % 2) * 66 + 66]
                        ecp = COPY(sch, S["vaug"][:, (kb % 2) * 66:(kb % 2) * 66 + HD],
                                   vrb[:, kb * HD:(kb + 1) * HD],
                                   deps=eL + [e_exp.get(kb - 2)] + e_c +
                                        ([e_pv_prev] if kb >= 2 else (prev_qb[4 + par:5 + par] if prev_qb else [])))
                    dpv = [eE, ecp] + ([e_pv_prev] if e_pv_prev else [])
                    if kb == 0 and prev_qb:
                        dpv += [prev_qb[0]]
                    ePV = MM(sch, P["o"][:], vt[:] if kb == NK - 1 else vt, ptb[:],
                             start=(kb == 0), stop=(kb == NK - 1), deps=dpv)
                    e_pv_prev = ePV
                es1 = COPY(sch, S["slrow"][:], P["o"][64:65, :], deps=[e_pv_prev])
                es2 = TT(sch, S["den"][:], S["slrow"][:], S["erow"][:], ALU.mult,
                         deps=[es1])
                es3 = RECIP(sch, S["den"][:], S["den"][:], deps=[es2])
                es4 = COPY(sch, S["denb"][:], S["den"][:], deps=[es3])
                es5 = MM(sch, P["b"][:], S["o64"][:], S["denb"][:], True, True,
                         deps=[es4] + (prev_qb[5:6] if prev_qb else []))
                es6 = COPY(sch, S["rbc"][:], P["b"][:], deps=[es5] +
                           (prev_qb[6:7] if prev_qb else []))
                dpo = [es6, e_pv_prev] + (prev_qb[7:8] if prev_qb else [])
                eo = TT(sch, S["outS"][:], P["o"][0:HD, :], S["rbc"][:], ALU.mult,
                        deps=dpo)
                es = DMA(sch, "y", OUT[b][:, qsl], S["outS"][:], deps=[eo])
                freed += [es, eo]
                prev_qb = [eo, es1, e_exp.get(NK - 2), e_exp.get(NK - 1),
                           e_pv_prev, es5, es6, es, e_pv_prev]
            e_free[pb] = freed + [e_exp[NK - 1], e_exp[NK - 2], eSt_prev]
        sch.build(block)
    return nc, {}



def build_l3_postgate():
    TB = 512
    NB = TPC // TB
    nc = bass.Bass("TRN2", target_bir_lowering=False)
    xT = nc.dram_tensor("xT", [D, TPC], F32, kind="ExternalInput").ap()
    attT = nc.dram_tensor("attT", [D, TPC], BF16, kind="ExternalInput").ap()
    WoT = nc.dram_tensor("WoT", [D, D], F32, kind="ExternalInput").ap()
    ob = nc.dram_tensor("ob", [1, D], F32, kind="ExternalInput").ap()
    Wg = nc.dram_tensor("Wg", [D, 8], F32, kind="ExternalInput").ap()
    ebias = nc.dram_tensor("ebias", [1, 8], F32, kind="ExternalInput").ap()
    Sw1T = nc.dram_tensor("Sw1T", [D, D], F32, kind="ExternalInput").ap()
    Sw3T = nc.dram_tensor("Sw3T", [D, D], F32, kind="ExternalInput").ap()
    Sw2T = nc.dram_tensor("Sw2T", [D, D], F32, kind="ExternalInput").ap()
    yT = nc.dram_tensor("yT", [D, TPC], F32, kind="ExternalOutput").ap()
    e1o = nc.dram_tensor("e1o", [TPC, 8], BF16, kind="ExternalOutput").ap()
    e2o = nc.dram_tensor("e2o", [TPC, 8], BF16, kind="ExternalOutput").ap()
    wro = nc.dram_tensor("wro", [TPC, 2], F32, kind="ExternalOutput").ap()
    cnto = nc.dram_tensor("cnto", [1, 8], F32, kind="ExternalOutput").ap()
    xn8o = nc.dram_tensor("xn8o", [128, NDT * 8], F32, kind="ExternalOutput").ap()
    scr = nc.dram_tensor("scrb", [NDT, 128], F32, kind="Internal")
    cs = consts(nc)
    onesf_c = nc.inline_tensor(np.ones((128, 1), np.float32), "c_onesf3").ap()
    onesrf_c = nc.inline_tensor(np.ones((1, 128), np.float32), "c_onesrf3").ap()
    identf_c = nc.inline_tensor(np.eye(128, dtype=np.float32), "c_identf3").ap()
    sch = Sched(nc)
    with ExitStack() as _st:
        S = {}
        for n, sh, dt in [
            ("wo", [128, NDT * D], BF16), ("w1", [128, NDT * D], BF16),
            ("w3", [128, NDT * D], BF16), ("w2", [128, NDT * D], BF16),
            ("wg", [128, NDT * 8], F32), ("onesf", [128, 1], F32),
            ("onesrf", [1, 128], F32), ("identf", [128, 128], F32),
            ("swgr", [1, 8], F32), ("swgbc", [128, 8], F32),
            ("rcol", [128, 1], F32), ("ucol", [128, 1], F32),
            ("gatef", [128, 8], F32),
            ("x", [128, NDT * TB], F32), ("at", [128, NDT * TB], BF16),
            ("xp", [128, NDT * TB], BF16), ("xb", [128, NDT * TB], BF16),
            ("hb", [128, NDT * TB], BF16), ("sq", [128, TB], BF16),
            ("rows", [1, 6 * TB], F32), ("rowsb", [1, 2 * TB], BF16),
            ("sw", [1, 2 * D + 8], BF16), ("bcol", [128, NDT], F32),
            ("gts", [128, 8], F32), ("g4", [128, 4], F32), ("m81", [128, 1], F32),
            ("m82", [128, 1], F32), ("p8", [128, 8], F32),
            ("msk", [128, 8], F32), ("msk2", [128, 8], F32),
            ("e1", [128, 8], F32), ("e2", [128, 8], F32),
            ("eb8", [128, 8], F32), ("e12b", [128, 8], BF16),
            ("keep", [128, 4], F32), ("wout", [128, 2], F32),
            ("t1", [128, TB], F32), ("obr", [1, D], F32), ("ebb", [1, 8], BF16),
            ("cnt", [1, 8], F32),
            ("onesc", [128, 1], BF16), ("onesr", [1, 128], BF16),
        ]:
            S[n] = _st.enter_context(nc.sbuf_tensor(n, sh, dt))
        P = {}
        for n, sh in [("p", [128, TB]), ("p2", [128, TB]), ("r1", [1, TB]),
                      ("r2", [1, TB]), ("bc", [128, TB]), ("g", [128, 8]),
                      ("c", [128, 8])]:
            P[n] = _st.enter_context(nc.psum_tensor("ps" + n, sh, F32))
        block = _st.enter_context(nc.Block())

        e_c = [DMA(sch, "y", S["onesc"][:], cs["ones_col"]),
               DMA(sch, "y", S["onesr"][:], cs["ones_row"]),
               DMA(sch, "y", S["obr"][:], ob),
               DMA(sch, "g", S["ebb"][:], ebias),
               DMA(sch, "y", S["wg"][:].rearrange("p (k n) -> p k n", n=8),
                   Wg.rearrange("(k p) n -> p k n", p=128)),
               DMA(sch, "y", S["onesf"][:], onesf_c),
               DMA(sch, "y", S["onesrf"][:], onesrf_c),
               DMA(sch, "y", S["identf"][:], identf_c)]
        for nm, srcw in [("wo", WoT), ("w1", Sw1T), ("w3", Sw3T), ("w2", Sw2T)]:
            e_c.append(DMA(sch, "g",
                           S[nm][:].rearrange("p (k j n) -> p k j n", j=NDT, n=128),
                           srcw.rearrange("(k p) (j n) -> p k j n", p=128, n=128)))

        def wt(nm, k, j):
            return S[nm][:, (k * NDT + j) * 128:(k * NDT + j) * 128 + 128]

        def wgt(k):
            return S["wg"][:, k * 8:(k + 1) * 8]

        eb1 = MM(sch, P["g"][:], S["onesr"][:], S["ebb"][:], True, True, deps=e_c)
        e_eb = COPY(sch, S["eb8"][:], P["g"][:], deps=[eb1])

        e_sw = []
        prevmm = e_eb
        for wi, nm in ((0, "w1"), (1, "w3")):
            for j in range(NDT):
                ev = None
                for k in range(NDT):
                    ev = MM(sch, P["r1"][:, 0:128], S["onesc"][:], wt(nm, k, j),
                            start=(k == 0), stop=(k == NDT - 1),
                            deps=list(e_c) + [prevmm])
                prevmm = COPY(sch, S["sw"][:, wi * D + j * 128:wi * D + j * 128 + 128],
                              P["r1"][:, 0:128], deps=[ev])
                e_sw.append(prevmm)
        ev = None
        for k in range(NDT):
            ev = MM(sch, P["c"][0:1, :], S["onesf"][:], wgt(k), start=(k == 0),
                    stop=(k == NDT - 1), deps=list(e_c) + [prevmm])
        e_swg = COPY(sch, S["swgr"][:], P["c"][0:1, :], deps=[ev])
        eswb = MM(sch, P["c"][:], S["onesrf"][:], S["swgr"][:], True, True,
                  deps=[e_swg])
        e_swg = COPY(sch, S["swgbc"][:], P["c"][:], deps=[eswb])
        e_sw.append(e_swg)
        e_ob1 = DMA(sch, "y", scr.ap().rearrange("c p -> (c p)").rearrange("(u x) -> u x", u=1),
                    S["obr"][:], deps=e_c)
        e_ob2 = DMAS(sch, "y", S["bcol"][:], scr.ap().rearrange("c p -> p c"),
                     deps=[e_ob1])

        sb = {
            "sq": S["sq"][:], "r1": P["r1"][:], "r2": P["r2"][:],
            "row_m": S["rows"][:, TB:2 * TB], "row_v": S["rows"][:, 2 * TB:3 * TB],
            "row_t": S["rows"][:, 3 * TB:4 * TB], "row_r": S["rows"][:, 4 * TB:5 * TB],
            "row_uf": S["rows"][:, 5 * TB:6 * TB],
            "row_rb": S["rowsb"][:, 0:TB], "row_ub": S["rowsb"][:, TB:2 * TB],
        }
        e_cnt0 = sch.em("v", lambda e: e.memset(S["cnt"][:], 0.0), deps=e_c)
        prev_blk = sch.barrier()
        for blk in range(NB):
            sl = slice(blk * TB, (blk + 1) * TB)
            x_t = [S["x"][:, i * TB:(i + 1) * TB] for i in range(NDT)]
            at_t = [S["at"][:, i * TB:(i + 1) * TB] for i in range(NDT)]
            xp_t = [S["xp"][:, i * TB:(i + 1) * TB] for i in range(NDT)]
            xb_t = [S["xb"][:, i * TB:(i + 1) * TB] for i in range(NDT)]
            hb_t = [S["hb"][:, i * TB:(i + 1) * TB] for i in range(NDT)]
            e_x = DMA(sch, "y", S["x"][:].rearrange("p (a n) -> p a n", n=TB),
                      xT.rearrange("(a p) n -> p a n", p=128)[:, :, sl], deps=prev_blk)
            e_a = DMA(sch, "y", S["at"][:].rearrange("p (a n) -> p a n", n=TB),
                      attT.rearrange("(a p) n -> p a n", p=128)[:, :, sl], deps=prev_blk)
            prev_ev = [None]
            e_xn = []
            for j in range(NDT):
                ev = None
                for k in range(NDT):
                    d = [e_a, e_x] + ([prev_ev[0]] if prev_ev[0] else []) + prev_blk
                    ev = MM(sch, P["p"][:], wt("wo", k, j), at_t[k], start=(k == 0),
                            stop=(k == NDT - 1), deps=d)
                eo = sch.em("v", lambda e, j=j: e.scalar_tensor_tensor(
                    x_t[j], P["p"][:], S["bcol"][:, j:j + 1], x_t[j], ALU.add, ALU.add),
                    deps=[ev, e_ob2])
                prev_ev[0] = eo
                e_xn.append(eo)
            ev_rows, cast_evs = emit_ln_stats(sch, sb, x_t, xp_t, TB, S["onesc"][:],
                                              e_xn)
            e_rbc = MM(sch, P["bc"][:], S["onesr"][:], sb["row_rb"], True, True,
                       deps=ev_rows + [prev_ev[0]])
            e_xb = [TT(sch, xb_t[i], x_t[i], P["bc"][:], ALU.mult, deps=[e_rbc])
                    for i in range(NDT)]
            if blk == 0:
                eu8 = MM(sch, P["p2"][:, 0:8], S["onesr"][:], sb["row_ub"][:, 0:8],
                         True, True, deps=ev_rows)
                exn8 = []
                for i in range(NDT):
                    ea = TT(sch, S["t1"][:, 0:8], x_t[i][:, 0:8], P["bc"][:, 0:8],
                            ALU.mult, deps=[e_rbc, e_xn[i]] + exn8[-1:])
                    eb_ = TT(sch, S["t1"][:, 0:8], S["t1"][:, 0:8], P["p2"][:, 0:8],
                             ALU.add, deps=[ea, eu8])
                    exn8.append(DMA(sch, "y", xn8o[:, i * 8:(i + 1) * 8],
                                    S["t1"][:, 0:8], deps=[eb_]))
            for sub in range(TB // 128):
                tsl = slice(sub * 128, (sub + 1) * 128)
                gsl = slice(blk * TB + sub * 128, blk * TB + (sub + 1) * 128)
                ev = None
                for k in range(NDT):
                    d = [e_xn[k]] + ([prev_ev[0]] if prev_ev[0] else [])
                    ev = MM(sch, P["g"][:], x_t[k][:, tsl], wgt(k), start=(k == 0),
                            stop=(k == NDT - 1), deps=d)
                etr = sch.em("t", lambda e, tsl=tsl: e.transpose(
                    P["c"][:, 0:1], sb["row_r"][:, tsl], S["identf"][0:1, 0:1]),
                    deps=ev_rows + [ev])
                etc = COPY(sch, S["rcol"][:], P["c"][:, 0:1], deps=[etr])
                etr2 = sch.em("t", lambda e, tsl=tsl: e.transpose(
                    P["c"][:, 0:1], sb["row_uf"][:, tsl], S["identf"][0:1, 0:1]),
                    deps=[etc])
                etc2 = COPY(sch, S["ucol"][:], P["c"][:, 0:1], deps=[etr2])
                eg1 = sch.em("v", lambda e: e.tensor_scalar(
                    S["gatef"][:], P["g"][:], S["rcol"][:, 0:1], None, ALU.mult),
                    deps=[ev, etc])
                eg2 = sch.em("v", lambda e: e.scalar_tensor_tensor(
                    S["gatef"][:], S["swgbc"][:], S["ucol"][:, 0:1], S["gatef"][:],
                    ALU.mult, ALU.add), deps=[eg1, etc2, e_swg])
                em1 = RED(sch, S["m81"][:], S["gatef"][:], "max", deps=[eg2],
                          negate=True)
                ee = ACT(sch, S["p8"][:], S["gatef"][:], AF.Exp, bias=S["m81"][:, 0:1],
                         deps=[em1])
                es_ = RED(sch, S["m82"][:], S["p8"][:], "sum", deps=[ee])
                er_ = RECIP(sch, S["m82"][:], S["m82"][:], deps=[es_])
                ep = TS(sch, S["p8"][:], S["p8"][:], S["m82"][:, 0:1], None, ALU.mult,
                        deps=[er_])
                eg = TT(sch, S["gts"][:], S["p8"][:], S["eb8"][:], ALU.add,
                        deps=[ep, e_eb])
                eg4 = RED(sch, S["g4"][:].rearrange("p (j u) -> p j u", u=1),
                          S["gts"][:].rearrange("p (j i) -> p j i", i=2), "sum",
                          deps=[eg])
                egm = RED(sch, S["m81"][:], S["g4"][:], "max", deps=[eg4])
                et1 = TS(sch, S["keep"][:], S["g4"][:], S["m81"][:, 0:1], None,
                         ALU.is_ge, deps=[egm])
                eg2 = STT(sch, S["msk"][:, 0:4], S["keep"][:], -1e30, S["g4"][:],
                          ALU.mult, ALU.add, deps=[et1])
                egm2 = RED(sch, S["m82"][:], S["msk"][:, 0:4], "max", deps=[eg2])
                ekp = TS(sch, S["keep"][:], S["g4"][:], S["m82"][:, 0:1], None,
                         ALU.is_ge, deps=[egm2, et1])
                ekx = COPY(sch, S["msk"][:].rearrange("p (j i) -> p j i", i=2),
                           S["keep"][:].rearrange("p (j u) -> p j u", u=1)
                           .broadcast_to([128, 4, 2]), deps=[ekp])
                emk = TS(sch, S["msk"][:], S["msk"][:], 1e30, -1e30, ALU.mult, ALU.add,
                         deps=[ekx])
                emk2 = TT(sch, S["msk"][:], S["msk"][:], S["gts"][:], ALU.add,
                          deps=[emk])
                em1_ = RED(sch, S["m81"][:], S["msk"][:], "max", deps=[emk2])
                ee1 = TS(sch, S["e1"][:], S["msk"][:], S["m81"][:, 0:1], None,
                         ALU.is_ge, deps=[em1_])
                em2_ = STT(sch, S["msk2"][:], S["e1"][:], -1e30, S["msk"][:], ALU.mult,
                           ALU.add, deps=[ee1])
                em2m = RED(sch, S["m82"][:], S["msk2"][:], "max", deps=[em2_])
                ee2 = TS(sch, S["e2"][:], S["msk2"][:], S["m82"][:, 0:1], None,
                         ALU.is_ge, deps=[em2m])
                ew1 = TT(sch, S["msk2"][:], S["p8"][:], S["e1"][:], ALU.mult,
                         deps=[ee1, ee2])
                ew1r = RED(sch, S["wout"][:, 0:1], S["msk2"][:], "sum", deps=[ew1])
                ew2 = TT(sch, S["msk2"][:], S["p8"][:], S["e2"][:], ALU.mult,
                         deps=[ew1r, ee2])
                ew2r = RED(sch, S["wout"][:, 1:2], S["msk2"][:], "sum", deps=[ew2])
                ec1 = TT(sch, S["msk2"][:], S["e1"][:], S["e2"][:], ALU.add,
                         deps=[ew2r, ee1, ee2])
                ec2 = COPY(sch, S["e12b"][:], S["msk2"][:], deps=[ec1])
                ec3 = MM(sch, P["c"][0:1, :], S["onesc"][:], S["e12b"][:], True, True,
                         deps=[ec2, e_cnt0, etc2] + ([prev_ev[0]] if prev_ev[0] else []))
                e_cnt0 = TT(sch, S["cnt"][:], S["cnt"][:], P["c"][0:1, :], ALU.add,
                            deps=[ec3])
                ee1b = COPY(sch, S["e12b"][:], S["e1"][:], deps=[e_cnt0])
                es1 = DMA(sch, "y", e1o[gsl, :], S["e12b"][:], deps=[ee1b])
                ee2b = COPY(sch, S["e12b"][:], S["e2"][:], deps=[es1])
                es2 = DMA(sch, "y", e2o[gsl, :], S["e12b"][:], deps=[ee2b])
                es3 = DMA(sch, "y", wro[gsl, :], S["wout"][:], deps=[ew2r, ee2b])
                prev_ev[0] = es3
            for j in range(NDT):
                ev = None
                for k in range(NDT):
                    d = [e_xb[k]] + ([prev_ev[0]] if prev_ev[0] else [])
                    if blk == 0 and j == 0 and k == 0:
                        d = d + exn8
                    ev = MM(sch, P["p"][:], wt("w1", k, j), xb_t[k], start=(k == 0),
                            stop=False, deps=d)
                ev = MM(sch, P["p"][:], S["sw"][:, j * 128:j * 128 + 128],
                        sb["row_ub"], False, True, deps=[ev] + ev_rows + [e_sw[-1]])
                ev2 = None
                for k in range(NDT):
                    d2 = [e_xb[k], ev]
                    if blk == 0 and j == 0 and k == 0:
                        d2 = d2 + exn8
                    ev2 = MM(sch, P["p2"][:], wt("w3", k, j), xb_t[k], start=(k == 0),
                             stop=False, deps=d2)
                ev2 = MM(sch, P["p2"][:], S["sw"][:, D + j * 128:D + j * 128 + 128],
                         sb["row_ub"], False, True, deps=[ev2] + ev_rows)
                et = ACT(sch, S["t1"][:], P["p"][:], AF.Silu, deps=[ev])
                eh = TT(sch, hb_t[j], S["t1"][:], P["p2"][:], ALU.mult, deps=[et, ev2])
                prev_ev[0] = eh
            e_st = []
            for j in range(NDT):
                ev = None
                for k in range(NDT):
                    d = [prev_ev[0]]
                    ev = MM(sch, P["p"][:], wt("w2", k, j), hb_t[k], start=(k == 0),
                            stop=(k == NDT - 1), deps=d)
                eo2 = TT(sch, x_t[j], x_t[j], P["p"][:], ALU.add, deps=[ev] + cast_evs)
                prev_ev[0] = eo2
                e_st.append(DMA(sch, "y", yT.rearrange("(a p) n -> p a n", p=128)[:, j:j + 1, sl],
                                x_t[j].rearrange("p (u n) -> p u n", u=1), deps=[eo2]))
            prev_blk = e_st + [prev_ev[0]]
        DMA(sch, "y", cnto, S["cnt"][:], deps=[e_cnt0])
        sch.build(block)
    return nc, {}



TTT_C = 64
TTT_R = 4 * TTT_C
TTT_NCH = TPC // TTT_R
TTT_LR = 0.001
TTT_CC = 2.0 / (4 * 1024)


def build_ttt_sweep():
    R = TTT_R
    NCH = TTT_NCH
    nc = bass.Bass("TRN2", target_bir_lowering=False)
    Kfm = nc.dram_tensor("Kfm", [D, TPC], BF16, kind="ExternalInput").ap()
    Krm = nc.dram_tensor("Krm", [TPC, D], BF16, kind="ExternalInput").ap()
    cVrm = nc.dram_tensor("cVrm", [TPC, D], BF16, kind="ExternalInput").ap()
    W10 = nc.dram_tensor("W10", [D, D], F32, kind="ExternalInput").ap()
    W20 = nc.dram_tensor("W20", [D, D], F32, kind="ExternalInput").ap()
    W2T0 = nc.dram_tensor("W2T0", [D, D], F32, kind="ExternalInput").ap()
    b10 = nc.dram_tensor("b10", [1, D], F32, kind="ExternalInput").ap()
    b20 = nc.dram_tensor("b20", [1, D], F32, kind="ExternalInput").ap()
    dS1p = nc.dram_tensor("dS1p", [7, D, D], F32, kind="ExternalInput").ap()
    dS2p = nc.dram_tensor("dS2p", [7, D, D], F32, kind="ExternalInput").ap()
    dS2Tp = nc.dram_tensor("dS2Tp", [7, D, D], F32, kind="ExternalInput").ap()
    db1p = nc.dram_tensor("db1p", [7, 1, D], F32, kind="ExternalInput").ap()
    db2p = nc.dram_tensor("db2p", [7, 1, D], F32, kind="ExternalInput").ap()
    DZ1p = nc.dram_tensor("DZ1p", [TPC, D], BF16, kind="ExternalInput").ap()
    Epi = nc.dram_tensor("Epi", [TPC, D], BF16, kind="ExternalInput").ap()
    S1o = nc.dram_tensor("S1o", [D, D], F32, kind="ExternalOutput").ap()
    S2o = nc.dram_tensor("S2o", [D, D], F32, kind="ExternalOutput").ap()
    S2To = nc.dram_tensor("S2To", [D, D], F32, kind="ExternalOutput").ap()
    b1o = nc.dram_tensor("b1o", [1, D], F32, kind="ExternalOutput").ap()
    b2o = nc.dram_tensor("b2o", [1, D], F32, kind="ExternalOutput").ap()
    DZ1o = nc.dram_tensor("DZ1o", [TPC, D], BF16, kind="ExternalOutput").ap()
    Eo = nc.dram_tensor("Eo", [TPC, D], BF16, kind="ExternalOutput").ap()
    w1s = nc.dram_tensor("w1s", [D, D], F32, kind="Internal")
    w2s = nc.dram_tensor("w2s", [D, D], F32, kind="Internal")
    w2ts = nc.dram_tensor("w2ts", [D, D], F32, kind="Internal")
    b1s = nc.dram_tensor("b1s", [1, D], F32, kind="Internal")
    b2s = nc.dram_tensor("b2s", [1, D], F32, kind="Internal")
    cs = consts(nc)
    step = np.arange(R) // 4
    msk = (step[:, None] < step[None, :]).astype(np.float32)
    cm0 = nc.inline_tensor(_bfa(msk[0:128]), "c_msku0").ap()
    cm1 = nc.inline_tensor(_bfa(msk[128:256]), "c_msku1").ap()
    idb = nc.inline_tensor(_bfa(np.eye(128)), "c_idb").ap()

    sch = Sched(nc)
    with ExitStack() as _st:
        S = {}
        for n, sh, dt in [
            ("w1c", [128, NDT * D], F32), ("w2c", [128, NDT * D], F32),
            ("w2tc", [128, NDT * D], F32),
            ("w1b", [128, NDT * D], BF16), ("w2b", [128, NDT * D], BF16),
            ("w2tb", [128, NDT * D], BF16),
            ("b1c", [1, D], F32), ("b2c", [1, D], F32),
            ("b1b", [1, D], BF16), ("b2b", [1, D], BF16),
            ("kf", [128, NDT * R], BF16), ("krm", [128, 2 * D], BF16),
            ("cv", [128, 2 * D], BF16),
            ("dzp", [128, 2 * D], BF16), ("ep", [128, 2 * D], BF16),
            ("h1", [128, 2 * D], BF16), ("ds", [128, 2 * D], BF16),
            ("er", [128, 2 * D], BF16),
            ("h1f", [128, NDT * R], BF16), ("ef", [128, NDT * R], BF16),
            ("g1", [128, 2 * R], BF16), ("g2", [128, 2 * R], BF16),
            ("g3", [128, 2 * R], BF16),
            ("m0", [128, R], BF16), ("m1", [128, R], BF16),
            ("gsc", [128, R], F32), ("stg", [128, D], F32), ("ident", [128, 128], BF16),
            ("onesc", [128, 1], BF16), ("onesr", [1, 128], BF16),
        ]:
            S[n] = _st.enter_context(nc.sbuf_tensor(n, sh, dt))
        P = {}
        for n, sh, dt in [("a", [128, 512], F32), ("b", [128, 512], F32),
                          ("g", [128, R], F32), ("t", [128, 128], BF16),
                          ("r", [1, 512], F32)]:
            P[n] = _st.enter_context(nc.psum_tensor("ps" + n, sh, dt))
        block = _st.enter_context(nc.Block())

        e_c = [DMA(sch, "y", S["onesc"][:], cs["ones_col"]),
               DMA(sch, "y", S["onesr"][:], cs["ones_row"]),
               DMA(sch, "y", S["m0"][:], cm0), DMA(sch, "y", S["m1"][:], cm1),
               DMA(sch, "y", S["ident"][:], idb)]

        stgv = None

        def minit(wc, w0, dsp, save):
            ev0 = DMA(sch, "y", S[wc][:].rearrange("p (k n) -> p k n", n=D),
                      w0.rearrange("(k p) n -> p k n", p=128), deps=e_c)
            last = ev0
            for i in range(7):
                for k in range(NDT):
                    el = DMA(sch, "y", S["stg"][:], dsp[i][k * 128:(k + 1) * 128, :],
                             deps=[last])
                    last = TT(sch, S[wc][:, k * D:(k + 1) * D],
                              S[wc][:, k * D:(k + 1) * D], S["stg"][:],
                              ALU.subtract, deps=[el, ev0])
            es = DMA(sch, "y", save.ap().rearrange("(k p) n -> p k n", p=128),
                     S[wc][:].rearrange("p (k n) -> p k n", n=D), deps=[last])
            return [last, es]

        def binit(bc, b0, dbp, save, after):
            ev0 = DMA(sch, "y", S[bc][:], b0, deps=e_c + after)
            last = ev0
            for i in range(7):
                el = DMA(sch, "y", S["stg"][0:1, 0:D], dbp[i], deps=[last])
                last = TT(sch, S[bc][:], S[bc][:], S["stg"][0:1, 0:D], ALU.subtract,
                          deps=[el, ev0])
            es = DMA(sch, "y", save.ap(), S[bc][:], deps=[last])
            return [last, es]

        e_m1 = minit("w1c", W10, dS1p, w1s)
        e_m2 = minit("w2c", W20, dS2p, w2s)
        e_m3 = minit("w2tc", W2T0, dS2Tp, w2ts)
        e_b1 = binit("b1c", b10, db1p, b1s, e_m1 + e_m2 + e_m3)
        e_b2 = binit("b2c", b20, db2p, b2s, e_b1)
        e_init = e_m1 + e_m2 + e_m3 + e_b1 + e_b2

        def kft(k):
            return S["kf"][:, k * R:(k + 1) * R]

        prev_ch = sch.barrier()
        for ch in range(NCH):
            rsl = slice(ch * R, (ch + 1) * R)
            e_cast = [COPY(sch, S["w1b"][:], S["w1c"][:], deps=prev_ch + e_init),
                      COPY(sch, S["w2b"][:], S["w2c"][:], deps=prev_ch + e_init),
                      COPY(sch, S["w2tb"][:], S["w2tc"][:], deps=prev_ch + e_init),
                      COPY(sch, S["b1b"][:], S["b1c"][:], deps=prev_ch + e_init),
                      COPY(sch, S["b2b"][:], S["b2c"][:], deps=prev_ch + e_init)]
            e_ld = [
                DMA(sch, "y", S["kf"][:].rearrange("p (k n) -> p k n", n=R),
                    Kfm.rearrange("(k p) n -> p k n", p=128)[:, :, rsl], deps=prev_ch),
                DMA(sch, "y", S["krm"][:].rearrange("p (t n) -> p t n", n=D),
                    Krm[rsl, :].rearrange("(t p) n -> p t n", p=128), deps=prev_ch),
                DMA(sch, "y", S["cv"][:].rearrange("p (t n) -> p t n", n=D),
                    cVrm[rsl, :].rearrange("(t p) n -> p t n", p=128), deps=prev_ch),
                DMA(sch, "y", S["dzp"][:].rearrange("p (t n) -> p t n", n=D),
                    DZ1p[rsl, :].rearrange("(t p) n -> p t n", p=128), deps=prev_ch),
                DMA(sch, "y", S["ep"][:].rearrange("p (t n) -> p t n", n=D),
                    Epi[rsl, :].rearrange("(t p) n -> p t n", p=128), deps=prev_ch),
            ]

            def gram(srcf, dst, plus1, e_src):
                evs = []
                for st in range(2):
                    ev = None
                    for k in range(NDT):
                        d = list(e_src) + (evs[-1:] if evs else list(prev_ch))
                        ev = MM(sch, P["g"][:], srcf(k)[:, st * 128:(st + 1) * 128],
                                srcf(k), start=(k == 0), stop=(k == NDT - 1), deps=d)
                    if plus1:
                        e1 = sch.em("v", lambda e, st=st, dst=dst: e.tensor_scalar(
                            S["gsc"][:], P["g"][:], 1.0, -TTT_LR, ALU.add,
                            ALU.mult), deps=[ev])
                    else:
                        e1 = TS(sch, S["gsc"][:], P["g"][:], -TTT_LR, None,
                                ALU.mult, deps=[ev])
                    e2 = TT(sch, S[dst][:, st * R:(st + 1) * R], S["gsc"][:],
                            S["m0" if st == 0 else "m1"][:], ALU.mult, deps=[e1])
                    evs.append(e2)
                return evs

            e_g1 = gram(kft, "g1", True, e_ld)

            e_h = []
            for rt in range(2):
                for jh in range(2):
                    ps = P["a"] if (rt * 2 + jh) % 2 == 0 else P["b"]
                    ev = None
                    for k in range(NDT):
                        d = [e_ld[0], e_cast[0]] + (e_h[-1:] if len(e_h) >= 2 else list(prev_ch))
                        ev = MM(sch, ps[:], kft(k)[:, rt * 128:(rt + 1) * 128],
                                S["w1b"][:, k * D + jh * 512:k * D + jh * 512 + 512],
                                start=(k == 0), stop=False, deps=d)
                    ev = MM(sch, ps[:], S["onesr"][:],
                            S["b1b"][:, jh * 512:jh * 512 + 512], False, False,
                            deps=[ev, e_cast[3]])
                    ev = MM(sch, ps[:], S["g1"][:, rt * 128:rt * 128 + 0 + 128] if False else S["g1"][:, 0 * R + rt * 128: 0 * R + rt * 128 + 128],
                            S["dzp"][:, 0 * D + jh * 512: 0 * D + jh * 512 + 512],
                            False, False, deps=[ev, e_g1[0], e_ld[3]])
                    ev = MM(sch, ps[:], S["g1"][:, 1 * R + rt * 128: 1 * R + rt * 128 + 128],
                            S["dzp"][:, 1 * D + jh * 512: 1 * D + jh * 512 + 512],
                            False, True, deps=[ev, e_g1[1]])
                    co = rt * D + jh * 512
                    eh = ACT(sch, S["h1"][:, co:co + 512], ps[:], AF.Silu, deps=[ev])
                    eg0 = ACT(sch, S["stg"][:, 0:512], ps[:], AF.Sigmoid, deps=[eh])
                    eg1 = TS(sch, S["stg"][:, 512:1024], S["stg"][:, 0:512], -1.0, 1.0,
                             ALU.mult, ALU.add, deps=[eg0])
                    eg2 = TT(sch, S["stg"][:, 512:1024], S["stg"][:, 512:1024], ps[:],
                             ALU.mult, deps=[eg1])
                    eg3 = TS(sch, S["stg"][:, 512:1024], S["stg"][:, 512:1024], 1.0,
                             None, ALU.add, deps=[eg2])
                    ed = TT(sch, S["ds"][:, co:co + 512], S["stg"][:, 0:512],
                            S["stg"][:, 512:1024], ALU.mult, deps=[eg3])
                    e_h += [eh, ed]

            e_tr = []
            for t in range(2):
                for j in range(NDT):
                    et = TRANS(sch, P["t"][:], S["h1"][:, t * D + j * 128:t * D + (j + 1) * 128],
                               S["ident"][:], deps=e_h + (e_tr[-1:] if e_tr else []))
                    e_tr.append(COPY(sch, S["h1f"][:, j * R + t * 128:j * R + (t + 1) * 128],
                                     P["t"][:], deps=[et]))

            def h1ft(j):
                return S["h1f"][:, j * R:(j + 1) * R]

            def eft(j):
                return S["ef"][:, j * R:(j + 1) * R]

            e_g2 = gram(h1ft, "g2", True, e_tr)

            e_e = []
            for rt in range(2):
                for jh in range(2):
                    ps = P["a"] if (rt * 2 + jh) % 2 == 0 else P["b"]
                    ev = None
                    for k in range(NDT):
                        d = [e_tr[-1], e_cast[1]] + (e_e[-1:] if len(e_e) >= 1 else [])
                        ev = MM(sch, ps[:], h1ft(k)[:, rt * 128:(rt + 1) * 128],
                                S["w2b"][:, k * D + jh * 512:k * D + jh * 512 + 512],
                                start=(k == 0), stop=False, deps=d)
                    ev = MM(sch, ps[:], S["onesr"][:],
                            S["b2b"][:, jh * 512:jh * 512 + 512], False, False,
                            deps=[ev, e_cast[4]])
                    ev = MM(sch, ps[:], S["g2"][:, 0 * R + rt * 128: 0 * R + rt * 128 + 128],
                            S["ep"][:, 0 * D + jh * 512: 0 * D + jh * 512 + 512],
                            False, False, deps=[ev, e_g2[0], e_ld[4]])
                    ev = MM(sch, ps[:], S["g2"][:, 1 * R + rt * 128: 1 * R + rt * 128 + 128],
                            S["ep"][:, 1 * D + jh * 512: 1 * D + jh * 512 + 512],
                            False, True, deps=[ev, e_g2[1]])
                    co = rt * D + jh * 512
                    ee = sch.em("v", lambda e, co=co, ps=ps: e.scalar_tensor_tensor(
                        S["er"][:, co:co + 512], ps[:], TTT_CC, S["cv"][:, co:co + 512],
                        ALU.mult, ALU.subtract), deps=[ev, e_ld[2]])
                    e_e.append(ee)

            e_tre = []
            for t in range(2):
                for j in range(NDT):
                    et = TRANS(sch, P["t"][:], S["er"][:, t * D + j * 128:t * D + (j + 1) * 128],
                               S["ident"][:], deps=e_e + (e_tre[-1:] if e_tre else []))
                    e_tre.append(COPY(sch, S["ef"][:, j * R + t * 128:j * R + (t + 1) * 128],
                                      P["t"][:], deps=[et]))

            e_g3 = gram(eft, "g3", False, e_tre)

            e_dz = []
            for rt in range(2):
                for jh in range(2):
                    ps = P["a"] if (rt * 2 + jh) % 2 == 0 else P["b"]
                    ev = None
                    for k in range(NDT):
                        d = [e_tre[-1], e_cast[2]] + (e_dz[-1:] if e_dz else [])
                        ev = MM(sch, ps[:], eft(k)[:, rt * 128:(rt + 1) * 128],
                                S["w2tb"][:, k * D + jh * 512:k * D + jh * 512 + 512],
                                start=(k == 0), stop=False, deps=d)
                    ev = MM(sch, ps[:], S["g3"][:, 0 * R + rt * 128: 0 * R + rt * 128 + 128],
                            S["h1"][:, 0 * D + jh * 512: 0 * D + jh * 512 + 512],
                            False, False, deps=[ev, e_g3[0]])
                    ev = MM(sch, ps[:], S["g3"][:, 1 * R + rt * 128: 1 * R + rt * 128 + 128],
                            S["h1"][:, 1 * D + jh * 512: 1 * D + jh * 512 + 512],
                            False, True, deps=[ev, e_g3[1]])
                    co = rt * D + jh * 512
                    ez = TT(sch, S["dzp"][:, co:co + 512], ps[:], S["ds"][:, co:co + 512],
                            ALU.mult, deps=[ev])
                    e_dz.append(ez)

            e_st = [
                DMA(sch, "y", DZ1o[rsl, :].rearrange("(t p) n -> p t n", p=128),
                    S["dzp"][:].rearrange("p (t n) -> p t n", n=D), deps=e_dz),
                DMA(sch, "y", Eo[rsl, :].rearrange("(t p) n -> p t n", p=128),
                    S["er"][:].rearrange("p (t n) -> p t n", n=D), deps=e_dz + e_tre),
            ]

            e_up = []

            def upd(lhs_tiles, rhs_buf, master, extra):
                evs = []
                for dt_ in range(NDT):
                    for jh in range(2):
                        ps = P["a"] if (dt_ + jh) % 2 == 0 else P["b"]
                        ev = None
                        for t in range(2):
                            d = list(extra) + (evs[-2:] if len(evs) >= 2 else list(extra))
                            ev = MM(sch, ps[:], lhs_tiles(t)[:, dt_ * 128:(dt_ + 1) * 128],
                                    rhs_buf[:, t * D + jh * 512:t * D + jh * 512 + 512],
                                    start=(t == 0), stop=(t == 1), deps=d)
                        eu = sch.em("v", lambda e, ps=ps, dt_=dt_, jh=jh, master=master:
                                    e.scalar_tensor_tensor(
                                        S[master][:, dt_ * D + jh * 512:dt_ * D + jh * 512 + 512],
                                        ps[:], -TTT_LR,
                                        S[master][:, dt_ * D + jh * 512:dt_ * D + jh * 512 + 512],
                                        ALU.mult, ALU.add), deps=[ev])
                        evs.append(eu)
                return evs

            e_up += upd(lambda t: S["krm"][:, t * D:(t + 1) * D], S["dzp"], "w1c",
                        e_dz + [e_ld[1]])
            e_up += upd(lambda t: S["h1"][:, t * D:(t + 1) * D], S["er"], "w2c",
                        e_dz + e_e + e_up[-2:])
            e_up += upd(lambda t: S["er"][:, t * D:(t + 1) * D], S["h1"], "w2tc",
                        e_dz + e_e + e_up[-2:])
            for (buf, bc) in (("dzp", "b1c"), ("er", "b2c")):
                for jh in range(2):
                    ev = None
                    for t in range(2):
                        d = e_dz + e_up[-2:]
                        ev = MM(sch, P["r"][:], S["onesc"][:],
                                S[buf][:, t * D + jh * 512:t * D + jh * 512 + 512],
                                start=(t == 0), stop=(t == 1), deps=d)
                    e_up.append(sch.em("v", lambda e, bc=bc, jh=jh: e.scalar_tensor_tensor(
                        S[bc][:, jh * 512:jh * 512 + 512], P["r"][:], -TTT_LR,
                        S[bc][:, jh * 512:jh * 512 + 512], ALU.mult, ALU.add),
                        deps=[ev]))
            prev_ch = e_up + e_st

        e_fin = prev_ch
        for (sv, mc, out) in ((w1s, "w1c", S1o), (w2s, "w2c", S2o), (w2ts, "w2tc", S2To)):
            for k in range(NDT):
                el = DMA(sch, "y", S["stg"][:], sv.ap()[k * 128:(k + 1) * 128, :],
                         deps=e_fin)
                ed = TT(sch, S["stg"][:], S["stg"][:],
                        S[mc][:, k * D:(k + 1) * D], ALU.subtract, deps=[el])
                e_fin = [DMA(sch, "y", out[k * 128:(k + 1) * 128, :], S["stg"][:],
                             deps=[ed]), ed]
        for (sv, mc, out) in ((b1s, "b1c", b1o), (b2s, "b2c", b2o)):
            el = DMA(sch, "y", S["stg"][0:1, 0:D], sv.ap(), deps=e_fin)
            ed = TT(sch, S["stg"][0:1, 0:D], S["stg"][0:1, 0:D], S[mc][:], ALU.subtract, deps=[el])
            e_fin = [DMA(sch, "y", out, S["stg"][0:1, 0:D], deps=[ed]), ed]
        sch.build(block)
    return nc, {}



def build_l1_final():
    TB = 512
    NB = TPC // TB
    nc = bass.Bass("TRN2", target_bir_lowering=False)
    xT = nc.dram_tensor("xT", [D, TPC], F32, kind="ExternalInput").ap()
    qT = nc.dram_tensor("qT", [D, TPC], BF16, kind="ExternalInput").ap()
    W10 = nc.dram_tensor("W10", [D, D], F32, kind="ExternalInput").ap()
    W20 = nc.dram_tensor("W20", [D, D], F32, kind="ExternalInput").ap()
    b10 = nc.dram_tensor("b10", [1, D], F32, kind="ExternalInput").ap()
    b20 = nc.dram_tensor("b20", [1, D], F32, kind="ExternalInput").ap()
    dS1a = nc.dram_tensor("dS1a", [8, D, D], F32, kind="ExternalInput").ap()
    dS2a = nc.dram_tensor("dS2a", [8, D, D], F32, kind="ExternalInput").ap()
    db1a = nc.dram_tensor("db1a", [8, 1, D], F32, kind="ExternalInput").ap()
    db2a = nc.dram_tensor("db2a", [8, 1, D], F32, kind="ExternalInput").ap()
    yT = nc.dram_tensor("yT", [D, TPC], F32, kind="ExternalOutput").ap()
    scr1 = nc.dram_tensor("scr1f", [NDT, 128], F32, kind="Internal")
    scr2 = nc.dram_tensor("scr2f", [NDT, 128], F32, kind="Internal")
    cs = consts(nc)
    sch = Sched(nc)
    with ExitStack() as _st:
        S = {}
        for n, sh, dt in [
            ("w1c", [128, NDT * D], F32), ("w2c", [128, NDT * D], F32),
            ("w1b", [128, NDT * D], BF16), ("w2b", [128, NDT * D], BF16),
            ("b1c", [1, D], F32), ("b2c", [1, D], F32),
            ("b1col", [128, NDT], F32), ("b2col", [128, NDT], F32),
            ("stg", [128, D], F32),
            ("x", [128, NDT * TB], F32), ("q", [128, NDT * TB], BF16),
            ("hb", [128, NDT * TB], BF16),
            ("onesr", [1, 128], BF16),
        ]:
            S[n] = _st.enter_context(nc.sbuf_tensor(n, sh, dt))
        P = {"a": _st.enter_context(nc.psum_tensor("psa", [128, TB], F32))}
        block = _st.enter_context(nc.Block())
        e_c = [DMA(sch, "y", S["onesr"][:], cs["ones_row"])]

        def minit(wc, w0, dsp):
            ev0 = DMA(sch, "y", S[wc][:].rearrange("p (k n) -> p k n", n=D),
                      w0.rearrange("(k p) n -> p k n", p=128), deps=e_c)
            last = ev0
            for i in range(8):
                for k in range(NDT):
                    el = DMA(sch, "y", S["stg"][:], dsp[i][k * 128:(k + 1) * 128, :],
                             deps=[last])
                    last = TT(sch, S[wc][:, k * D:(k + 1) * D],
                              S[wc][:, k * D:(k + 1) * D], S["stg"][:], ALU.subtract,
                              deps=[el, ev0])
            return last

        e1 = minit("w1c", W10, dS1a)
        e2 = minit("w2c", W20, dS2a)
        eb1 = DMA(sch, "y", S["b1c"][:], b10, deps=e_c)
        eb2 = DMA(sch, "y", S["b2c"][:], b20, deps=e_c)
        for i in range(8):
            el = DMA(sch, "y", S["stg"][0:1, 0:D], db1a[i], deps=[eb1, e2])
            eb1 = TT(sch, S["b1c"][:], S["b1c"][:], S["stg"][0:1, 0:D], ALU.subtract,
                     deps=[el])
            el2 = DMA(sch, "y", S["stg"][0:1, 0:D], db2a[i], deps=[eb1, eb2])
            eb2 = TT(sch, S["b2c"][:], S["b2c"][:], S["stg"][0:1, 0:D], ALU.subtract,
                     deps=[el2])
        ec1 = DMA(sch, "y", scr1.ap().rearrange("a n -> (a n)").rearrange("(u x) -> u x", u=1),
                  S["b1c"][:], deps=[eb1, eb2])
        ec2 = DMAS(sch, "y", S["b1col"][:], scr1.ap().rearrange("c p -> p c"), deps=[ec1])
        ec3 = DMA(sch, "y", scr2.ap().rearrange("a n -> (a n)").rearrange("(u x) -> u x", u=1),
                  S["b2c"][:], deps=[eb2])
        ec4 = DMAS(sch, "y", S["b2col"][:], scr2.ap().rearrange("c p -> p c"), deps=[ec3])
        e_cast = [COPY(sch, S["w1b"][:], S["w1c"][:], deps=[e1]),
                  COPY(sch, S["w2b"][:], S["w2c"][:], deps=[e2]), ec2, ec4]
        prev_blk = sch.barrier()
        for blk in range(NB):
            sl = slice(blk * TB, (blk + 1) * TB)
            e_x = DMA(sch, "y", S["x"][:].rearrange("p (a n) -> p a n", n=TB),
                      xT.rearrange("(a p) n -> p a n", p=128)[:, :, sl], deps=prev_blk)
            e_q = DMA(sch, "y", S["q"][:].rearrange("p (a n) -> p a n", n=TB),
                      qT.rearrange("(a p) n -> p a n", p=128)[:, :, sl], deps=prev_blk)
            prev = [None]
            e_h = []
            for j in range(NDT):
                ev = None
                for k in range(NDT):
                    d = [e_q] + e_cast + ([prev[0]] if prev[0] else [])
                    ev = MM(sch, P["a"][:], S["w1b"][:, (k * NDT + j) * 128:(k * NDT + j) * 128 + 128],
                            S["q"][:, k * TB:(k + 1) * TB], start=(k == 0),
                            stop=(k == NDT - 1), deps=d)
                ev = sch.em("t", lambda e, ev=ev: e.matmul(
                    P["a"][:], S["w1b"][:, 0:128], S["q"][:, 0:TB], start=False,
                    stop=True, skip_group_check=True), deps=[ev]) if False else ev
                eh = ACT(sch, S["hb"][:, j * TB:(j + 1) * TB], P["a"][:], AF.Silu,
                         bias=S["b1col"][:, j:j + 1], deps=[ev] + e_cast)
                prev[0] = eh
                e_h.append(eh)
            e_st = []
            for j in range(NDT):
                ev = None
                for k in range(NDT):
                    d = [e_h[k], prev[0]]
                    ev = MM(sch, P["a"][:], S["w2b"][:, (k * NDT + j) * 128:(k * NDT + j) * 128 + 128],
                            S["hb"][:, k * TB:(k + 1) * TB], start=(k == 0),
                            stop=(k == NDT - 1), deps=d)
                eo = sch.em("v", lambda e, j=j: e.scalar_tensor_tensor(
                    S["x"][:, j * TB:(j + 1) * TB], P["a"][:], S["b2col"][:, j:j + 1],
                    S["x"][:, j * TB:(j + 1) * TB], ALU.add, ALU.add),
                    deps=[ev, e_x] + e_cast)
                prev[0] = eo
                e_st.append(DMA(sch, "y", yT.rearrange("(a p) n -> p a n", p=128)[:, j:j + 1, sl],
                                S["x"][:, j * TB:(j + 1) * TB].rearrange("p (u n) -> p u n", u=1),
                                deps=[eo]))
            prev_blk = e_st + [prev[0]]
        sch.build(block)
    return nc, {}



def build_l1_ffn():
    TB = 256
    NB = TPC // TB
    nc = bass.Bass("TRN2", target_bir_lowering=False)
    xT = nc.dram_tensor("xT", [D, TPC], F32, kind="ExternalInput").ap()
    Wg = nc.dram_tensor("Wg", [D, 8], F32, kind="ExternalInput").ap()
    gb = nc.dram_tensor("gb", [1, 8], F32, kind="ExternalInput").ap()
    We = nc.dram_tensor("We", [8, D, D], F32, kind="ExternalInput").ap()
    be = nc.dram_tensor("be", [8, D], F32, kind="ExternalInput").ap()
    yT = nc.dram_tensor("yT", [D, TPC], F32, kind="ExternalOutput").ap()
    cs = consts(nc)
    sel = []
    for e in range(8):
        m = np.zeros((8, 128), np.float32)
        m[e, :] = 1.0
        sel.append(nc.inline_tensor(_bfa(m), f"c_sel{e}").ap())
    sch = Sched(nc)
    with ExitStack() as _st:
        S = {}
        for n, sh, dt in [
            ("we", [128, 8 * NDT * D], BF16),
            ("x", [128, NDT * TB], F32), ("xp", [128, NDT * TB], BF16),
            ("xb", [128, NDT * TB], BF16), ("sq", [128, TB], BF16),
            ("acc", [128, NDT * TB], F32),
            ("rows", [1, 6 * TB], F32), ("rowsb", [1, 2 * TB], BF16),
            ("sw", [1, 8 * D + 8], BF16),
            ("wgs", [128, NDT * 8], BF16), ("gbb", [1, 8], BF16),
            ("gb8", [128, 8], F32), ("bes", [8, D], BF16),
            ("p8", [128, 8], F32), ("m81", [128, 1], F32), ("m82", [128, 1], F32),
            ("msk", [128, 8], F32), ("wg8", [128, 8], BF16),
            ("w8", [8, TB], BF16), ("selS", [8, 8 * 128], BF16),
            ("webc", [128, TB], F32), ("t1", [128, TB], F32),
            ("idb", [128, 128], BF16),
            ("onesc", [128, 1], BF16), ("onesr", [1, 128], BF16),
        ]:
            S[n] = _st.enter_context(nc.sbuf_tensor(n, sh, dt))
        P = {}
        for n, sh in [("a", [128, TB]), ("b", [128, TB]), ("r1", [1, TB]),
                      ("r2", [1, TB]), ("g", [128, 8])]:
            P[n] = _st.enter_context(nc.psum_tensor("ps" + n, sh, F32))
        P["w8"] = _st.enter_context(nc.psum_tensor("psw8", [8, 128], BF16))
        block = _st.enter_context(nc.Block())
        e_c = [DMA(sch, "y", S["onesc"][:], cs["ones_col"]),
               DMA(sch, "y", S["onesr"][:], cs["ones_row"]),
               DMA(sch, "y", S["idb"][:], cs["ident"]),
               DMA(sch, "g", S["gbb"][:], gb),
               DMA(sch, "g", S["bes"][:], be),
               DMA(sch, "g", S["wgs"][:].rearrange("p (k n) -> p k n", n=8),
                   Wg.rearrange("(k p) n -> p k n", p=128))]
        for e in range(8):
            e_c.append(DMA(sch, "y", S["selS"][:, e * 128:(e + 1) * 128], sel[e]))
            e_c.append(DMA(sch, "g",
                           S["we"][:, e * NDT * D:(e + 1) * NDT * D]
                           .rearrange("p (k j n) -> p k j n", j=NDT, n=128),
                           We[e].rearrange("(k p) (j n) -> p k j n", p=128, n=128)))

        def wt(e, k, j):
            o = e * NDT * D + (k * NDT + j) * 128
            return S["we"][:, o:o + 128]

        eg1 = MM(sch, P["g"][:], S["onesr"][:], S["gbb"][:], True, True, deps=e_c)
        e_gb8 = COPY(sch, S["gb8"][:], P["g"][:], deps=[eg1])
        e_sw = []
        prevmm = e_gb8
        for e in range(8):
            for j in range(NDT):
                ev = None
                for k in range(NDT):
                    ev = MM(sch, P["r1"][:, 0:128], S["onesc"][:], wt(e, k, j),
                            start=(k == 0), stop=(k == NDT - 1),
                            deps=list(e_c) + [prevmm])
                prevmm = COPY(sch, S["sw"][:, e * D + j * 128:e * D + j * 128 + 128],
                              P["r1"][:, 0:128], deps=[ev])
                e_sw.append(prevmm)
        ev = None
        for k in range(NDT):
            ev = MM(sch, P["r1"][:, 0:8], S["onesc"][:], S["wgs"][:, k * 8:(k + 1) * 8],
                    start=(k == 0), stop=(k == NDT - 1), deps=list(e_c) + [prevmm])
        e_swg = COPY(sch, S["sw"][:, 8 * D:8 * D + 8], P["r1"][:, 0:8], deps=[ev])

        sb = {
            "sq": S["sq"][:], "r1": P["r1"][:], "r2": P["r2"][:],
            "row_m": S["rows"][:, TB:2 * TB], "row_v": S["rows"][:, 2 * TB:3 * TB],
            "row_t": S["rows"][:, 3 * TB:4 * TB], "row_r": S["rows"][:, 4 * TB:5 * TB],
            "row_uf": S["rows"][:, 5 * TB:6 * TB],
            "row_rb": S["rowsb"][:, 0:TB], "row_ub": S["rowsb"][:, TB:2 * TB],
        }
        prev_blk = sch.barrier()
        for blk in range(NB):
            sl = slice(blk * TB, (blk + 1) * TB)
            x_t = [S["x"][:, i * TB:(i + 1) * TB] for i in range(NDT)]
            xp_t = [S["xp"][:, i * TB:(i + 1) * TB] for i in range(NDT)]
            xb_t = [S["xb"][:, i * TB:(i + 1) * TB] for i in range(NDT)]
            e_x = DMA(sch, "y", S["x"][:].rearrange("p (a n) -> p a n", n=TB),
                      xT.rearrange("(a p) n -> p a n", p=128)[:, :, sl], deps=prev_blk)
            ev_rows, cast_evs = emit_ln_stats(sch, sb, x_t, xp_t, TB, S["onesc"][:],
                                              [e_x] + prev_blk)
            e_rbc = MM(sch, P["a"][:], S["onesr"][:], sb["row_rb"], True, True,
                       deps=ev_rows + prev_blk)
            e_xb = [TT(sch, xb_t[i], x_t[i], P["a"][:], ALU.mult, deps=[e_rbc])
                    for i in range(NDT)]
            prev_ev = [e_xb[-1]]
            e_w8 = []
            for sub in range(TB // 128):
                tsl = slice(sub * 128, (sub + 1) * 128)
                ev = None
                for k in range(NDT):
                    d = [e_xb[k], prev_ev[0]]
                    ev = MM(sch, P["g"][:], xb_t[k][:, tsl], S["wgs"][:, k * 8:(k + 1) * 8],
                            start=(k == 0), stop=False, deps=d)
                ev = MM(sch, P["g"][:], sb["row_ub"][:, tsl], S["sw"][:, 8 * D:8 * D + 8],
                        False, True, deps=[ev] + ev_rows + [e_swg])
                et = sch.em("v", lambda e: e.tensor_tensor(
                    S["msk"][:], P["g"][:], S["gb8"][:], ALU.add), deps=[ev, e_gb8])
                em1 = RED(sch, S["m81"][:], S["msk"][:], "max", deps=[et], negate=True)
                ee = ACT(sch, S["p8"][:], S["msk"][:], AF.Exp, bias=S["m81"][:, 0:1],
                         deps=[em1])
                es_ = RED(sch, S["m82"][:], S["p8"][:], "sum", deps=[ee])
                er_ = RECIP(sch, S["m82"][:], S["m82"][:], deps=[es_])
                ep = TS(sch, S["p8"][:], S["p8"][:], S["m82"][:, 0:1], None, ALU.mult,
                        deps=[er_])
                ems = TS(sch, S["msk"][:], S["p8"][:], 0.1, None, ALU.is_ge, deps=[ep])
                ew = TT(sch, S["msk"][:], S["msk"][:], S["p8"][:], ALU.mult, deps=[ems])
                esm = RED(sch, S["m81"][:], S["msk"][:], "sum", deps=[ew])
                eis = TS(sch, S["m82"][:], S["m81"][:], 0.0, None, ALU.is_equal,
                         deps=[esm])
                eadd = TT(sch, S["m81"][:], S["m81"][:], S["m82"][:], ALU.add,
                          deps=[eis])
                erc = RECIP(sch, S["m81"][:], S["m81"][:], deps=[eadd])
                ewf = TS(sch, S["wg8"][:], S["msk"][:], S["m81"][:, 0:1], None,
                         ALU.mult, deps=[erc])
                etr = sch.em("t", lambda e: e.transpose(P["w8"][:], S["wg8"][:],
                                                        S["idb"][:]),
                             deps=[ewf] + e_w8[-1:])
                e_w8.append(COPY(sch, S["w8"][:, tsl], P["w8"][:].bitcast(BF16)
                                 if False else P["w8"][:], deps=[etr]))
                prev_ev[0] = e_w8[-1]
            e_z = []
            for i in range(NDT):
                e_z.append(sch.em("v", lambda e, i=i: e.memset(
                    S["acc"][:, i * TB:(i + 1) * TB], 0.0),
                    deps=prev_blk + [e_w8[-1], e_z[-1] if e_z else e_w8[-1]]))
            prev2 = [e_z[-1]]
            for e in range(8):
                ebc = MM(sch, P["b"][:], S["selS"][:, e * 128:(e + 1) * 128],
                         S["w8"][:], True, True, deps=e_w8 + [prev2[0]])
                ewb = COPY(sch, S["webc"][:], P["b"][:], deps=[ebc, prev2[0]])
                for j in range(NDT):
                    ev = None
                    for k in range(NDT):
                        d = [e_xb[k], prev2[0]]
                        ev = MM(sch, P["a"][:], wt(e, k, j), xb_t[k], start=(k == 0),
                                stop=False, deps=d)
                    ev = MM(sch, P["a"][:], S["sw"][:, e * D + j * 128:e * D + j * 128 + 128],
                            sb["row_ub"], False, True, deps=[ev] + ev_rows)
                    e1_ = TT(sch, S["t1"][:], P["a"][:], S["webc"][:], ALU.mult,
                             deps=[ev, ewb] + [prev2[0]])
                    e2_ = sch.em("v", lambda e_, j=j: e_.tensor_tensor(
                        S["acc"][:, j * TB:(j + 1) * TB],
                        S["acc"][:, j * TB:(j + 1) * TB], S["t1"][:], ALU.add),
                        deps=[e1_, e_z[j]])
                    prev2[0] = e2_
            e_st = []
            for j in range(NDT):
                ev = MM(sch, P["a"][:], S["bes"][:, j * 128:(j + 1) * 128], S["w8"][:],
                        True, True, deps=e_w8 + [prev2[0]] + list(e_c))
                e1_ = TT(sch, S["acc"][:, j * TB:(j + 1) * TB],
                         S["acc"][:, j * TB:(j + 1) * TB], P["a"][:], ALU.add,
                         deps=[ev, prev2[0]])
                e2_ = TT(sch, S["x"][:, j * TB:(j + 1) * TB],
                         S["x"][:, j * TB:(j + 1) * TB],
                         S["acc"][:, j * TB:(j + 1) * TB], ALU.add,
                         deps=[e1_] + cast_evs)
                prev2[0] = e2_
                e_st.append(DMA(sch, "y", yT.rearrange("(a p) n -> p a n", p=128)[:, j:j + 1, sl],
                                S["x"][:, j * TB:(j + 1) * TB].rearrange("p (u n) -> p u n", u=1),
                                deps=[e2_]))
            prev_blk = e_st + [prev2[0]]
        sch.build(block)
    return nc, {}



def build_l3_routed():
    NT = 2 * TPC // 128
    nc = bass.Bass("TRN2", target_bir_lowering=False)
    cnts = nc.dram_tensor("cnts", [8, 1, 8], F32, kind="ExternalInput").ap()
    srcmT = nc.dram_tensor("srcmT", [8, 2 * TPC], BF16, kind="ExternalInput").ap()
    wcol = nc.dram_tensor("wcol", [128, NT], F32, kind="ExternalInput").ap()
    xn8 = nc.dram_tensor("xn8", [128, NDT * 8], F32, kind="ExternalInput").ap()
    w1r = nc.dram_tensor("w1r", [8, D, 512], F32, kind="ExternalInput").ap()
    w3r = nc.dram_tensor("w3r", [8, D, 512], F32, kind="ExternalInput").ap()
    Hpart = nc.dram_tensor("Hpart", [64, 512], F32, kind="ExternalOutput").ap()
    apd = nc.dram_tensor("apd", [64, 512], BF16, kind="Internal")
    bpd = nc.dram_tensor("bpd", [64, 512], BF16, kind="Internal")
    cs = consts(nc)
    iot = nc.dram_tensor("iot", [8, 128], F32, kind="ExternalInput").ap()
    L8 = nc.inline_tensor(np.triu(np.ones((8, 8), np.float32)), "c_L8").ap()
    onesf = nc.inline_tensor(np.ones((128, 1), np.float32), "c_onesf").ap()
    E8m = np.zeros((8, 64), np.float32)
    T8m = np.zeros((8, 64), np.float32)
    for g in range(8):
        for e in range(8):
            E8m[g, g * 8 + e] = 1.0
            T8m[e, g * 8 + e] = 1.0
    E8 = nc.inline_tensor(_bfa(E8m), "c_E8").ap()
    T8 = nc.inline_tensor(_bfa(T8m), "c_T8").ap()
    sch = Sched(nc)
    with ExitStack() as _st:
        S = {}
        for n, sh, dt in [
            ("xnb", [128, NDT * 8], BF16), ("wstg", [128, 4 * 512], BF16),
            ("ap8", [8, 512], F32), ("apS", [64, 512], BF16), ("bpS", [64, 512], BF16),
            ("cntS", [8, 8], F32), ("ctot", [1, 8], F32), ("onescf", [128, 1], F32),
            ("offs", [8, 1], F32), ("prevs", [8, 1], F32), ("cnt1", [8, 1], F32),
            ("iotS", [8, 128], F32), ("iota8", [8, 128], F32),
            ("gA", [8, 128], F32), ("gB", [8, 128], F32), ("gmT", [8, 128], BF16),
            ("smT", [8, 128], BF16),
            ("p8t", [64, 128], BF16), ("p8r", [128, 64], BF16),
            ("wcS", [128, 2 * TPC // 128], F32), ("t1", [128, 512], F32), ("t2", [128, 512], F32),
            ("t3", [128, 512], F32), ("hrow", [128, 512], BF16),
            ("L8S", [8, 8], F32), ("E8S", [8, 64], BF16), ("T8S", [8, 64], BF16),
            ("scr8", [1, 8], F32),
            ("onesc", [128, 1], BF16), ("onesr", [1, 128], BF16),
        ]:
            S[n] = _st.enter_context(nc.sbuf_tensor(n, sh, dt))
        P = {}
        for n, sh in [("a", [128, 512]), ("b", [128, 512]), ("h", [64, 512]),
                      ("x", [64, 128]), ("y", [128, 64]), ("o", [8, 512]),
                      ("s", [8, 1])]:
            P[n] = _st.enter_context(nc.psum_tensor("ps" + n, sh, F32))
        block = _st.enter_context(nc.Block())
        e_c = [DMA(sch, "y", S["onesc"][:], cs["ones_col"]),
               DMA(sch, "y", S["onesr"][:], cs["ones_row"]),
               DMA(sch, "y", S["iotS"][:], iot),
               DMA(sch, "y", S["L8S"][:], L8),
               DMA(sch, "y", S["E8S"][:], E8), DMA(sch, "y", S["T8S"][:], T8),
               DMA(sch, "y", S["cntS"][:].rearrange("p (u n) -> p u n", u=1),
                   cnts),
               DMA(sch, "g", S["xnb"][:], xn8),
               DMA(sch, "y", S["onescf"][:], onesf),
               DMA(sch, "y", S["wcS"][:], wcol)]
        ec1 = MM(sch, P["s"][:, 0:1] if False else P["o"][:, 0:8], S["onesc"][0:8, :]
                 if False else S["onesc"][:, :], S["cntS"][:], True, True, deps=e_c)
        sch.steps["t"].pop()
        sch.count["t"] -= 1
        ec1 = MM(sch, P["o"][0:1, 0:8], S["onescf"][0:8, 0:1], S["cntS"][:], True, True,
                 deps=e_c)
        ec2 = COPY(sch, S["ctot"][:], P["o"][0:1, 0:8], deps=[ec1])
        eb1 = DMA(sch, "y", S["scr8"][:], S["ctot"][:], deps=[ec2]) if False else None
        scrd = nc.dram_tensor("scrd", [1, 8], F32, kind="Internal")
        eb1 = DMA(sch, "y", scrd.ap(), S["ctot"][:], deps=[ec2])
        eb2 = DMA(sch, "y", S["cnt1"][:], scrd.ap().rearrange("u n -> n u"), deps=[eb1])
        ep1 = MM(sch, P["s"][:], S["L8S"][:], S["cnt1"][:], True, True, deps=[eb2])
        ep2 = COPY(sch, S["offs"][:], P["s"][:], deps=[ep1])
        ep3 = sch.em("v", lambda e: e.tensor_tensor(
            S["prevs"][:], S["offs"][:], S["cntS"][:, 0:1] if False else S["offs"][:],
            ALU.subtract), deps=[ep2]) if False else None
        ecc = COPY(sch, S["prevs"][:], S["cnt1"][:], deps=[eb2])
        ep3 = TT(sch, S["prevs"][:], S["offs"][:], S["prevs"][:], ALU.subtract,
                 deps=[ep2, ecc])
        for (wsrc, dstd) in ((w1r, apd), (w3r, bpd)):
            prevw = [e_c[-1]]
            for g in range(8):
                el = DMA(sch, "g", S["wstg"][:].rearrange("p (k n) -> p k n", n=512)
                         [:, 0:NDT // 2, :], wsrc[g].rearrange("(k p) n -> p k n", p=128)
                         [:, 0:NDT // 2, :], deps=[prevw[0]])
                el2 = DMA(sch, "g", S["wstg"][:].rearrange("p (k n) -> p k n", n=512)
                          [:, NDT // 2:NDT, :] if False else S["wstg"][:, 4 * 512:4 * 512]
                          , None) if False else None
                ev = None
                for k in range(NDT // 2):
                    d = [el, prevw[0]]
                    ev = MM(sch, P["o"][:], S["xnb"][:, k * 8:(k + 1) * 8],
                            S["wstg"][:, k * 512:(k + 1) * 512], start=(k == 0),
                            stop=False, deps=d)
                el2 = DMA(sch, "g", S["wstg"][:].rearrange("p (k n) -> p k n", n=512),
                          wsrc[g].rearrange("(k p) n -> p k n", p=128)[:, NDT // 2:NDT, :],
                          deps=[ev])
                for k in range(NDT // 2):
                    d = [el2]
                    ev = MM(sch, P["o"][:], S["xnb"][:, (4 + k) * 8:(5 + k) * 8],
                            S["wstg"][:, k * 512:(k + 1) * 512], start=False,
                            stop=(k == NDT // 2 - 1), deps=d)
                ee = COPY(sch, S["ap8"][:], P["o"][:], deps=[ev])
                prevw[0] = DMA(sch, "g", dstd.ap()[g * 8:(g + 1) * 8, :], S["ap8"][:],
                               deps=[ee])
            sch.barrier()
        eap = DMA(sch, "y", S["apS"][:], apd.ap(), deps=sch.barrier())
        ebp = DMA(sch, "y", S["bpS"][:], bpd.ap(), deps=sch.barrier())

        eH_open = None
        prev_t = sch.barrier()
        for t in range(NT):
            tb = t * 128
            ei = TS(sch, S["iota8"][:], S["iotS"][:], float(tb), None,
                    ALU.add, deps=prev_t + e_c)
            ega = sch.em("v", lambda e: e.tensor_scalar(
                S["gA"][:], S["iota8"][:], S["prevs"][:, 0:1], None, ALU.is_ge),
                deps=[ei, ep3])
            egb = sch.em("v", lambda e: e.tensor_scalar(
                S["gB"][:], S["iota8"][:], S["offs"][:, 0:1], None, ALU.is_lt),
                deps=[ega, ep2])
            egm = TT(sch, S["gmT"][:], S["gA"][:], S["gB"][:], ALU.mult, deps=[egb])
            esm = COPY(sch, S["smT"][:], srcmT[:, tb:tb + 128], deps=prev_t) if False else \
                DMA(sch, "y", S["smT"][:], srcmT[:, tb:tb + 128], deps=prev_t)
            em1 = MM(sch, P["x"][:], S["E8S"][:], S["gmT"][:], True, True,
                     deps=[egm] + prev_t)
            em2 = MM(sch, P["y"][:] if False else P["x"][:], None, None, True, True) if False else None
            ee1 = COPY(sch, S["p8t"][:], P["x"][:], deps=[em1])
            em2 = MM(sch, P["x"][:], S["T8S"][:], S["smT"][:], True, True,
                     deps=[ee1, esm])
            ee2 = TT(sch, S["p8t"][:], S["p8t"][:], P["x"][:], ALU.mult, deps=[em2])
            em3 = MM(sch, P["y"][:], S["gmT"][:], S["E8S"][:], True, True,
                     deps=[egm] + prev_t + [ee2])
            ee3 = COPY(sch, S["p8r"][:], P["y"][:], deps=[em3])
            em4 = MM(sch, P["y"][:], S["smT"][:], S["T8S"][:], True, True,
                     deps=[ee3, esm])
            ee4 = TT(sch, S["p8r"][:], S["p8r"][:], P["y"][:], ALU.mult, deps=[em4])
            ea = MM(sch, P["a"][:], S["p8t"][:], S["apS"][:], True, True,
                    deps=[ee2, eap] + prev_t)
            eb = MM(sch, P["b"][:], S["p8t"][:], S["bpS"][:], True, True,
                    deps=[ee2, ebp, ea] + prev_t)
            e1 = sch.em("v", lambda e, t=t: e.tensor_scalar(
                S["t1"][:], P["a"][:], S["wcS"][:, t:t + 1], None, ALU.mult),
                deps=[ea] + e_c)
            e2 = ACT(sch, S["t2"][:], S["t1"][:], AF.Silu, deps=[e1])
            e3 = sch.em("v", lambda e, t=t: e.tensor_scalar(
                S["t3"][:], P["b"][:], S["wcS"][:, t:t + 1], None, ALU.mult),
                deps=[eb, e2])
            e4 = TT(sch, S["hrow"][:], S["t2"][:], S["t3"][:], ALU.mult, deps=[e3])
            eH = MM(sch, P["h"][:], S["p8r"][:], S["hrow"][:], start=(t == 0),
                    stop=(t == NT - 1), deps=[e4, ee4] +
                    ([eH_open] if eH_open else []))
            eH_open = eH
            prev_t = [e4, ee4, eH, ea, eb, egm]
        efin = COPY(sch, S["t1"][0:64, 0:512], P["h"][:], deps=[eH_open])
        DMA(sch, "y", Hpart, S["t1"][0:64, 0:512], deps=[efin])
        sch.build(block)
    return nc, {}


def build_l3_final():
    nc = bass.Bass("TRN2", target_bir_lowering=False)
    xT = nc.dram_tensor("xT", [D, TPC], F32, kind="ExternalInput").ap()
    Hparts = nc.dram_tensor("Hparts", [8, 64, 512], F32, kind="ExternalInput").ap()
    w2r = nc.dram_tensor("w2r", [8, 512, D], F32, kind="ExternalInput").ap()
    mski = nc.dram_tensor("mski", [1, 1], F32, kind="ExternalInput").ap()
    yT = nc.dram_tensor("yT", [D, TPC], F32, kind="ExternalOutput").ap()
    cs = consts(nc)
    sch = Sched(nc)
    with ExitStack() as _st:
        S = {}
        for n, sh, dt in [
            ("x", [128, NDT * TPC // 1], F32) if False else ("x", [128, NDT * TPC], F32),
            ("h64", [64, 512], F32), ("hstg", [64, 512], F32),
            ("h64t", [128, 4 * 64], BF16),
            ("w2", [128, 4 * D], BF16), ("o8", [8, D], F32), ("o8b", [128, 8], F32),
            ("ident", [128, 128], BF16), ("mcol", [128, 1], F32),
            ("mrow", [1, 1], BF16),
            ("onesc", [128, 1], BF16), ("onesr", [1, 128], BF16),
        ]:
            S[n] = _st.enter_context(nc.sbuf_tensor(n, sh, dt))
        P = {}
        for n, sh, dt in [("a", [8, 512], F32), ("t", [128, 64], F32),
                          ("tb", [128, 64], BF16), ("m", [128, 1], F32)]:
            P[n] = _st.enter_context(nc.psum_tensor("ps" + n, sh, dt))
        block = _st.enter_context(nc.Block())
        e_c = [DMA(sch, "y", S["onesc"][:], cs["ones_col"]),
               DMA(sch, "y", S["onesr"][:], cs["ones_row"]),
               DMA(sch, "y", S["ident"][:], cs["ident"]),
               DMA(sch, "g", S["mrow"][:], mski)]
        e_x = DMA(sch, "y", S["x"][:].rearrange("p (a n) -> p a n", n=TPC),
                  xT.rearrange("(a p) n -> p a n", p=128), deps=[])
        em1 = MM(sch, P["m"][:], S["onesr"][:], S["mrow"][:], True, True, deps=e_c)
        em2 = COPY(sch, S["mcol"][:], P["m"][:], deps=[em1])
        eh = DMA(sch, "y", S["h64"][:], Hparts[0], deps=e_c)
        for i in range(1, 8):
            el = DMA(sch, "y", S["hstg"][:], Hparts[i], deps=[eh])
            eh = TT(sch, S["h64"][:], S["h64"][:], S["hstg"][:], ALU.add, deps=[el])
        ehb = COPY(sch, S["hstg"][:].bitcast(BF16)[:, 0:512] if False else S["h64t"][:, 0:0 + 0] if False else S["hstg"][:], S["h64"][:], deps=[eh]) if False else None
        hb16 = _st.enter_context(nc.sbuf_tensor("hb16", [64, 512], BF16))
        ehb = COPY(sch, hb16[:], S["h64"][:], deps=[eh])
        e_tr = []
        for c4 in range(4):
            et = sch.em("t", lambda e, c4=c4: e.transpose(
                P["tb"][:, 0:64], hb16[:, c4 * 128:(c4 + 1) * 128], S["ident"][0:64, 0:64]),
                deps=[ehb] + e_tr[-1:])
            e_tr.append(COPY(sch, S["h64t"][:, c4 * 64:(c4 + 1) * 64], P["tb"][:, 0:64],
                             deps=[et]))
        e_o = []
        for half in range(2):
            ev = None
            for g in range(8):
                el = DMA(sch, "g", S["w2"][:].rearrange("p (k n) -> p k n", n=D)
                         [:, :, half * 512:(half + 1) * 512],
                         w2r[g].rearrange("(k p) n -> p k n", p=128)
                         [:, :, half * 512:(half + 1) * 512], deps=e_tr + e_o[-1:] +
                         ([ev] if ev else []))
                for kt in range(4):
                    ev = MM(sch, P["a"][:], S["h64t"][:, kt * 64 + 0:kt * 64 + 64]
                            [:, g * 8 - g * 8:8] if False else
                            S["h64t"][:, kt * 64:(kt + 1) * 64][:, 0:8] if False else
                            S["h64t"][:, kt * 64 + 0:kt * 64 + 64],
                            None, True, True) if False else ev
                    ev = MM(sch, P["a"][:],
                            S["h64t"][:, kt * 64 + g * 8:kt * 64 + g * 8 + 8],
                            S["w2"][:].rearrange("p (k n) -> p k n", n=D)
                            [:, kt, half * 512:half * 512 + 512],
                            start=(g == 0 and kt == 0), stop=(g == 7 and kt == 3),
                            deps=[el] + ([ev] if ev else []))
            e_o.append(COPY(sch, S["o8"][:, half * 512:(half + 1) * 512], P["a"][:],
                            deps=[ev]))
        e_fin = []
        o8b16 = _st.enter_context(nc.sbuf_tensor("o8b16", [8, D], BF16))
        eo_b = COPY(sch, o8b16[:], S["o8"][:], deps=e_o)
        for i in range(NDT):
            et = sch.em("t", lambda e, i=i: e.transpose(
                P["tb"][:, 0:8], o8b16[:, i * 128:(i + 1) * 128], S["ident"][0:8, 0:8]),
                deps=[eo_b] + e_fin[-1:])
            e1 = sch.em("v", lambda e, i=i: e.tensor_scalar(
                S["o8b"][:, 0:8], P["tb"][:, 0:8], S["mcol"][:, 0:1], None, ALU.mult),
                deps=[et])
            e2 = TT(sch, S["x"][:, i * TPC:i * TPC + 8], S["x"][:, i * TPC:i * TPC + 8],
                    S["o8b"][:, 0:8], ALU.add, deps=[e1, e_x, em2])
            e_fin.append(e2)
        e_st = DMA(sch, "y", yT.rearrange("(a p) n -> p a n", p=128),
                   S["x"][:].rearrange("p (a n) -> p a n", n=TPC), deps=e_fin + [e_x])
        sch.build(block)
    return nc, {}



def _get(name, builder):
    if name not in _progs:
        _progs[name] = builder()
    return _progs[name][0]


_progs = {}


def _col(v):
    return np.ascontiguousarray(np.asarray(v, np.float32).reshape(-1, 128).T)


def kernel(**inp):
    cores = list(range(8))

    def RUN(nc, ims):
        return run_bass_kernel_spmd(nc, ims, core_ids=cores).results

    def shT(a):
        return [np.ascontiguousarray(np.asarray(a)[c * TPC:(c + 1) * TPC].T)
                for c in cores]

    def shTb(a):
        return [np.ascontiguousarray(np.asarray(a)[c * TPC:(c + 1) * TPC].T)
                for c in cores]

    def unT(rs, key):
        return np.concatenate([np.asarray(r[key]).T for r in rs], 0)

    x = _f32(inp["L0_attn_x"]).reshape(TOK, D)

    nc = _get("l0p", build_l0_proj)
    rs = RUN(nc, [{"xT": s, "Wq": _f32(inp["L0_attn_Wq"]), "Wk": _f32(inp["L0_attn_Wk"]),
                   "Wv": _f32(inp["L0_attn_Wv"]), "Wb": _f32(inp["L0_attn_Wb"])}
                  for s in shT(x)])
    q0 = unT(rs, "qT"); k0 = unT(rs, "kT"); v0 = unT(rs, "vT")
    beta0 = np.concatenate([np.asarray(r["betaT"]) for r in rs], 1)

    nc = _get("l0s", build_l0_scan)
    ims = []
    for c in cores:
        kf, qf, krm, vrm, bc, br = [], [], [], [], [], []
        for i in range(4):
            bh = c * 4 + i
            b, h = bh // 8, bh % 8
            sl = slice(b * L, (b + 1) * L)
            hs = slice(h * 128, (h + 1) * 128)
            kf.append(k0[sl, hs].T); qf.append(q0[sl, hs].T)
            krm.append(k0[sl, hs]); vrm.append(v0[sl, hs])
            brow = _f32(beta0[h, b * L:(b + 1) * L])
            bc.append(brow.reshape(64, 128).T)
            br.append(brow.reshape(1, L))
        ims.append({"kf": np.ascontiguousarray(np.stack(kf)),
                    "qf": np.ascontiguousarray(np.stack(qf)),
                    "krm": np.ascontiguousarray(np.stack(krm)),
                    "vrm": np.ascontiguousarray(np.stack(vrm)),
                    "bcol": np.ascontiguousarray(np.stack(bc)),
                    "brow": np.ascontiguousarray(np.stack(br))})
    rs = RUN(nc, ims)
    o0 = np.zeros((TOK, D), bfdt)
    for c in cores:
        for i in range(4):
            bh = c * 4 + i
            b, h = bh // 8, bh % 8
            o0[b * L:(b + 1) * L, h * 128:(h + 1) * 128] = np.asarray(rs[c]["O"])[i]

    nc = _get("l0pf", build_l0_postffn)
    rs = RUN(nc, [{"xT": sx, "oT": so,
                   "lnw": _col(inp["L0_attn_ln_w"]), "lnb": _col(inp["L0_attn_ln_b"]),
                   "W1": _f32(inp["L0_ffn_W1"]), "b1c": _col(inp["L0_ffn_b1"]),
                   "thc": _col(inp["L0_ffn_quantum_phases"]),
                   "wrc": _col(inp["L0_ffn_quantum_weights_real"]),
                   "wic": _col(inp["L0_ffn_quantum_weights_imag"]),
                   "W2": _f32(inp["L0_ffn_W2"]), "b2c": _col(inp["L0_ffn_b2"])}
                  for sx, so in zip(shT(x), shTb(o0))])
    x = unT(rs, "yT").astype(np.float32)

    W_KV = _f32(inp["L1_attn_W_KV"])
    spec1 = {"projs": [("k1", D, "copy"), ("cv", D, "scale"), ("q1", D, "copy")],
             "biases": set(), "scale": {"cv": TTT_CC}}
    nc = _get("l1p", lambda: build_proj(spec1, "l1p"))
    rs = RUN(nc, [{"xT": s, "W_k1": np.ascontiguousarray(W_KV[:, :D]),
                   "W_cv": np.ascontiguousarray(W_KV[:, D:]),
                   "W_q1": _f32(inp["L1_attn_W_Q"])} for s in shT(x)])
    k1 = unT(rs, "k1T"); cv1 = unT(rs, "cvT"); q1 = unT(rs, "q1T")
    krows = np.ascontiguousarray(k1.reshape(4, L, D).transpose(1, 0, 2).reshape(TOK, D))
    cvrows = np.ascontiguousarray(cv1.reshape(4, L, D).transpose(1, 0, 2).reshape(TOK, D))
    W10 = _f32(inp["L1_attn_W1"]); W20 = _f32(inp["L1_attn_W2"])
    b10 = _f32(inp["L1_attn_b1"]).reshape(1, D)
    b20 = _f32(inp["L1_attn_b2"]).reshape(1, D)
    dz = np.zeros((TOK, D), bfdt); ep = np.zeros((TOK, D), bfdt)
    dS1 = np.zeros((8, D, D), np.float32); dS2 = np.zeros((8, D, D), np.float32)
    dS2T = np.zeros((8, D, D), np.float32)
    db1 = np.zeros((8, 1, D), np.float32); db2 = np.zeros((8, 1, D), np.float32)
    nc = _get("ttt", build_ttt_sweep)
    for sweep in range(4):
        ims = []
        for c in cores:
            sl = slice(c * TPC, (c + 1) * TPC)
            p1 = np.zeros((7, D, D), np.float32); p2 = np.zeros((7, D, D), np.float32)
            p2t = np.zeros((7, D, D), np.float32)
            pb1 = np.zeros((7, 1, D), np.float32); pb2 = np.zeros((7, 1, D), np.float32)
            p1[:c] = dS1[:c]; p2[:c] = dS2[:c]; p2t[:c] = dS2T[:c]
            pb1[:c] = db1[:c]; pb2[:c] = db2[:c]
            ims.append({"Kfm": np.ascontiguousarray(krows[sl].T),
                        "Krm": np.ascontiguousarray(krows[sl]),
                        "cVrm": np.ascontiguousarray(cvrows[sl]),
                        "W10": W10, "W20": W20,
                        "W2T0": np.ascontiguousarray(W20.T),
                        "b10": b10, "b20": b20,
                        "dS1p": p1, "dS2p": p2, "dS2Tp": p2t,
                        "db1p": pb1, "db2p": pb2,
                        "DZ1p": np.ascontiguousarray(dz[sl]),
                        "Epi": np.ascontiguousarray(ep[sl])})
        rs = RUN(nc, ims)
        for c in cores:
            sl = slice(c * TPC, (c + 1) * TPC)
            dS1[c] = np.asarray(rs[c]["S1o"])
            dS2T[c] = np.asarray(rs[c]["S2To"])
            dS2[c] = np.ascontiguousarray(np.asarray(rs[c]["S2To"]).T)
            db1[c] = np.asarray(rs[c]["b1o"]); db2[c] = np.asarray(rs[c]["b2o"])
            dz[sl] = np.asarray(rs[c]["DZ1o"]); ep[sl] = np.asarray(rs[c]["Eo"])
    nc = _get("l1f", build_l1_final)
    q1sh = [np.ascontiguousarray(q1[c * TPC:(c + 1) * TPC].T) for c in cores]
    rs = RUN(nc, [{"xT": sx, "qT": sq, "W10": W10, "W20": W20, "b10": b10,
                   "b20": b20, "dS1a": dS1, "dS2a": dS2, "db1a": db1, "db2a": db2}
                  for sx, sq in zip(shT(x), q1sh)])
    x = unT(rs, "yT").astype(np.float32)

    nc = _get("l1ffn", build_l1_ffn)
    rs = RUN(nc, [{"xT": s, "Wg": _f32(inp["L1_ffn_gate_w"]),
                   "gb": _f32(inp["L1_ffn_gate_b"]).reshape(1, 8),
                   "We": _f32(inp["L1_ffn_expert_w"]),
                   "be": _f32(inp["L1_ffn_expert_b"])} for s in shT(x)])
    x = unT(rs, "yT").astype(np.float32)

    nc = _get("wvs", build_wvsum)
    rs = RUN(nc, [{"Wv": _f32(inp["L2_attn_Wv"]),
                   "bv": _f32(inp["L2_attn_bv"]).reshape(1, -1)} for _ in cores])
    Wt = np.asarray(rs[0]["Wt"]); bt = np.asarray(rs[0]["bt"])
    spec2 = {"projs": [("q2", D, "copy"), ("k2", D, "copy"), ("vt", D, "copy")],
             "biases": {"q2", "k2", "vt"}, "gamma_beta": True}
    nc = _get("l2p", lambda: build_proj(spec2, "l2p"))
    rs = RUN(nc, [{"xT": s, "W_q2": _f32(inp["L2_attn_Wq"]),
                   "W_k2": _f32(inp["L2_attn_Wk"]), "W_vt": Wt,
                   "b_q2": _f32(inp["L2_attn_bq"]).reshape(1, D),
                   "b_k2": _f32(inp["L2_attn_bk"]).reshape(1, D), "b_vt": bt,
                   "gam": _col(inp["L2_attn_gamma"]), "betap": _col(inp["L2_attn_beta"])}
                  for s in shT(x)])
    q2 = unT(rs, "q2T"); k2 = unT(rs, "k2T"); vt2 = unT(rs, "vtT")
    nc = _get("l2s", build_l2_scan)
    ims = []
    for c in cores:
        kf, qf, vr = [], [], []
        for i in range(4):
            bh = c * 4 + i
            b, h = bh // 8, bh % 8
            sl = slice(b * L, (b + 1) * L)
            hs = slice(h * 128, (h + 1) * 128)
            kf.append(k2[sl, hs].T); qf.append(q2[sl, hs].T); vr.append(vt2[sl, hs])
        ims.append({"kf": np.ascontiguousarray(np.stack(kf)),
                    "qf": np.ascontiguousarray(np.stack(qf)),
                    "vrm": np.ascontiguousarray(np.stack(vr))})
    rs = RUN(nc, ims)
    att2 = np.zeros((TOK, D), bfdt)
    for c in cores:
        for i in range(4):
            bh = c * 4 + i
            b, h = bh // 8, bh % 8
            pv = np.asarray(rs[c]["OUT"])[i].T
            att2[b * L + h * 1024:b * L + (h + 1) * 1024, :] = pv.reshape(1024, D)
    nc = _get("l2pf", build_l2_postffn)
    ims = []
    for c, (sx, sa) in enumerate(zip(shT(x), shTb(att2))):
        if c % 2 == 1:
            xl = _col(x[c * TPC - 1])
        else:
            xl = np.zeros((128, NDT), np.float32)
        ims.append({"xT": sx, "attnT": sa,
                    "WrT": np.ascontiguousarray(_f32(inp["L2_ffn_W_r"]).T),
                    "WkT": np.ascontiguousarray(_f32(inp["L2_ffn_W_k"]).T),
                    "WvT": np.ascontiguousarray(_f32(inp["L2_ffn_W_v"]).T),
                    "mur": _col(inp["L2_ffn_mu_r"]), "muk": _col(inp["L2_ffn_mu_k"]),
                    "xlast": xl})
    rs = RUN(nc, ims)
    x = unT(rs, "yT").astype(np.float32)

    spec3 = {"projs": [("q3", D, "copy"), ("k3", D, "copy"), ("v3", D, "copy")],
             "biases": set()}
    nc = _get("l3p", lambda: build_proj(spec3, "l3p"))
    rs = RUN(nc, [{"xT": s, "W_q3": np.ascontiguousarray(_f32(inp["L3_attn_q_w"]).T),
                   "W_k3": np.ascontiguousarray(_f32(inp["L3_attn_k_w"]).T),
                   "W_v3": np.ascontiguousarray(_f32(inp["L3_attn_v_w"]).T)}
                  for s in shT(x)])
    q3 = unT(rs, "q3T"); k3 = unT(rs, "k3T"); v3 = unT(rs, "v3T")
    nc = _get("l3s", build_l3_scan)
    ims = []
    for c in cores:
        kf, qf, vr = [], [], []
        for i in range(8):
            bh = c * 8 + i
            b, h = bh // 16, bh % 16
            sl = slice(b * L, (b + 1) * L)
            hs = slice(h * 64, (h + 1) * 64)
            kf.append(k3[sl, hs].T); qf.append(q3[sl, hs].T); vr.append(v3[sl, hs])
        ims.append({"kf": np.ascontiguousarray(np.stack(kf)),
                    "qf": np.ascontiguousarray(np.stack(qf)),
                    "vrm": np.ascontiguousarray(np.stack(vr))})
    rs = RUN(nc, ims)
    att3 = np.zeros((TOK, D), bfdt)
    for c in cores:
        for i in range(8):
            bh = c * 8 + i
            b, h = bh // 16, bh % 16
            att3[b * L:(b + 1) * L, h * 64:(h + 1) * 64] = np.asarray(rs[c]["OUT"])[i].T
    nc = _get("l3pg", build_l3_postgate)
    rs = RUN(nc, [{"xT": sx, "attT": sa,
                   "WoT": np.ascontiguousarray(_f32(inp["L3_attn_out_w"]).T),
                   "ob": _f32(inp["L3_attn_out_b"]).reshape(1, D),
                   "Wg": np.ascontiguousarray(_f32(inp["L3_ffn_w_gate"]).T),
                   "ebias": _f32(inp["L3_ffn_expert_bias"]).reshape(1, 8),
                   "Sw1T": np.ascontiguousarray(_f32(inp["L3_ffn_sw1"]).T),
                   "Sw3T": np.ascontiguousarray(_f32(inp["L3_ffn_sw3"]).T),
                   "Sw2T": np.ascontiguousarray(_f32(inp["L3_ffn_sw2"]).T)}
                  for sx, sa in zip(shT(x), shTb(att3))])
    x = unT(rs, "yT").astype(np.float32)
    cnts = np.stack([np.asarray(r["cnto"]).reshape(1, 8) for r in rs])
    xn8 = np.asarray(rs[0]["xn8o"])
    e1s = [np.asarray(r["e1o"]) for r in rs]
    e2s = [np.asarray(r["e2o"]) for r in rs]
    wrs = [np.asarray(r["wro"]) for r in rs]
    nc = _get("l3r", build_l3_routed)
    ims = []
    for c in cores:
        srcm = np.zeros((2 * TPC, 8), bfdt)
        srcm[0::2] = e1s[c]; srcm[1::2] = e2s[c]
        wrow = np.zeros((2 * TPC,), np.float32)
        wrow[0::2] = wrs[c][:, 0]; wrow[1::2] = wrs[c][:, 1]
        ims.append({"cnts": cnts, "srcmT": np.ascontiguousarray(srcm.T),
                    "wcol": np.ascontiguousarray(wrow.reshape(-1, 128).T),
                    "xn8": xn8, "w1r": _f32(inp["L3_ffn_w1"]),
                    "w3r": _f32(inp["L3_ffn_w3"]),
                    "iot": np.tile(np.arange(128, dtype=np.float32), (8, 1)) + c * 2 * TPC})
    rs = RUN(nc, ims)
    Hparts = np.stack([np.asarray(r["Hpart"]) for r in rs])
    nc = _get("l3f", build_l3_final)
    rs = RUN(nc, [{"xT": s, "Hparts": Hparts, "w2r": _f32(inp["L3_ffn_w2"]),
                   "mski": np.asarray([[1.0 if c == 0 else 0.0]], np.float32)}
                  for c, s in enumerate(shT(x))])
    x = unT(rs, "yT").astype(np.float32)
    return x.reshape(4, L, D)


# version 05
# speedup vs baseline: 1.1118x; 1.1118x over previous
import numpy as np
from contextlib import ExitStack
import ml_dtypes
import concourse.bass as bass
import concourse.mybir as mybir
from concourse.bass_utils import run_bass_kernel_spmd

bfdt = ml_dtypes.bfloat16
F32 = mybir.dt.float32
BF16 = mybir.dt.bfloat16
AF = mybir.ActivationFunctionType
ALU = mybir.AluOpType
AX = mybir.AxisListType

NCORE = 8
B, L, D = 4, 8192, 1024
TOK = B * L
TPC = TOK // NCORE
NDT = D // 128
LN_EPS = 1e-5


def _f32(x):
    return np.ascontiguousarray(np.asarray(x, np.float32))


def _bfa(x):
    return np.ascontiguousarray(np.asarray(x).astype(bfdt))



ENGS = ("t", "v", "s", "y", "g")
NDMASEM = 8


class Sched:

    def __init__(self, nc):
        self.nc = nc
        self.steps = {e: [] for e in ENGS}
        self.count = {e: 0 for e in ENGS}
        for q in ("y", "g"):
            for i in range(NDMASEM):
                self.count[f"{q}{i}"] = 0
        self.rr = {"y": 0, "g": 0}
        self.eng_clock = {e: {} for e in ENGS}
        self.ev_clock = {}
        self.own_wait = {e: 0 for e in ENGS}

    def em(self, eng, fn, deps=(), dma=False):
        if dma and eng in ("y", "g"):
            key = f"{eng}{self.rr[eng]}"
            self.rr[eng] = (self.rr[eng] + 1) % NDMASEM
            inc = 16
        else:
            key = eng
            inc = 1
        req = {}
        if inc == 16 and self.count[key] > 0:
            req[key] = self.count[key]
            for k, v in self.ev_clock.get((key, self.count[key]), {}).items():
                if req.get(k, 0) < v:
                    req[k] = v
        for d in deps:
            if d is None:
                continue
            (k, v) = d
            if req.get(k, 0) < v:
                req[k] = v
            for k2, v2 in self.ev_clock.get((k, v), {}).items():
                if req.get(k2, 0) < v2:
                    req[k2] = v2
        ec = self.eng_clock[eng]
        waits = {}
        for k, v in req.items():
            if k == eng:
                if k == "t":
                    continue
                if v > self.own_wait[eng]:
                    waits[k] = v
                    self.own_wait[eng] = v
                continue
            if ec.get(k, 0) >= v:
                continue
            waits[k] = v
        self.count[key] += inc
        self.steps[eng].append((fn, waits, inc, key))
        for k, v in req.items():
            if ec.get(k, 0) < v:
                ec[k] = v
        if inc != 16:
            ec[key] = self.count[key]
        if inc == 16:
            evc = dict(req)
            evc[key] = self.count[key]
            self.ev_clock[(key, self.count[key])] = evc
        else:
            self.ev_clock[(key, self.count[key])] = dict(ec)
        return (key, self.count[key])

    def barrier(self):
        return [(k, v) for k, v in self.count.items() if v > 0]

    def build(self, block):
        nc = self.nc
        sems = {k: nc.alloc_semaphore(f"sem_{k}") for k in self.count}
        self.sems = sems
        final = self.barrier()

        def run_stream(engobj, eng):
            for (fn, waits, inc, key) in self.steps[eng]:
                for (wk, wv) in waits.items():
                    engobj.wait_ge(sems[wk], wv)
                inst = fn(engobj)
                inst.then_inc(sems[key], inc)
            for (wk, wv) in final:
                if self.eng_clock[eng].get(wk, 0) >= wv and wk != eng:
                    pass
                engobj.wait_ge(sems[wk], wv)

        @block.tensor
        def _(e):
            run_stream(e, "t")

        @block.vector
        def _(e):
            run_stream(e, "v")

        @block.scalar
        def _(e):
            run_stream(e, "s")

        @block.sync
        def _(e):
            run_stream(e, "y")

        @block.gpsimd
        def _(e):
            run_stream(e, "g")



def DMA(sch, eng, dst, src, deps=()):
    return sch.em(eng, lambda e: e.dma_start(out=dst, in_=src), deps, dma=True)


def DMAS(sch, eng, dst, src, deps=()):
    return sch.em(eng, lambda e: e.dma_start(out=dst, in_=src,
                                             allow_slow_non_contiguous=True),
                  deps, dma=True)


def MM(sch, out, lhsT, rhs, start, stop, deps=()):
    return sch.em("t", lambda e: e.matmul(out, lhsT, rhs, start=start, stop=stop), deps)


def TRANS(sch, out, in_, ident, deps=()):
    return sch.em("t", lambda e: e.transpose(out, in_, ident), deps)


def ACT(sch, out, in_, func, bias=0.0, scale=1.0, deps=(), accum_out=None):
    return sch.em("s", lambda e: e.activation(out, in_, func, bias=bias, scale=scale,
                                              accum_out=accum_out), deps)


def COPY(sch, out, in_, deps=(), eng="v"):
    return sch.em(eng, lambda e: e.tensor_copy(out, in_), deps)


def TT(sch, out, a, b, op, deps=(), eng="v"):
    return sch.em(eng, lambda e: e.tensor_tensor(out, a, b, op), deps)


def TS(sch, out, in_, s1, s2, op0, op1=ALU.bypass, deps=(), eng="v"):
    if s2 is None:
        return sch.em(eng, lambda e: e.tensor_scalar(out, in_, s1, None, op0), deps)
    return sch.em(eng, lambda e: e.tensor_scalar(out, in_, s1, s2, op0, op1), deps)


def STT(sch, out, in0, scalar, in1, op0, op1, deps=(), eng="v"):
    return sch.em(eng, lambda e: e.scalar_tensor_tensor(out, in0, scalar, in1, op0, op1), deps)


def RED(sch, out, in_, op, deps=(), negate=False, eng="v"):
    if op == "max":
        return sch.em(eng, lambda e: e.reduce_max(out, in_, axis=AX.X, negate=negate), deps)
    return sch.em(eng, lambda e: e.reduce_sum(out, in_, axis=AX.X, negate=negate), deps)


def RECIP(sch, out, in_, deps=(), eng="v"):
    return sch.em(eng, lambda e: e.reciprocal(out, in_), deps)


def COLL(sch, kind, op, ins, outs, rg, deps=()):
    return sch.em("g", lambda e: e.collective_compute(kind, op, replica_groups=rg,
                                                      ins=ins, outs=outs), deps)



def consts(nc):
    c = {}
    c["ones_col"] = nc.inline_tensor(_bfa(np.ones((128, 1))), "c_ones_col").ap()
    c["ones_row"] = nc.inline_tensor(_bfa(np.ones((1, 128))), "c_ones_row").ap()
    c["ident"] = nc.inline_tensor(_bfa(np.eye(128)), "c_ident").ap()
    return c


def emit_ln_stats(sch, sb, blk_xf_tiles, xp_tiles, TB, ones_col_sb, after):
    evs = []
    cast_evs = []
    for i in range(NDT):
        cast_evs.append(COPY(sch, xp_tiles[i], blk_xf_tiles[i], deps=after))
    ev_mm1 = None
    for i in range(NDT):
        ev_mm1 = MM(sch, sb["r1"], ones_col_sb, xp_tiles[i], start=(i == 0),
                    stop=(i == NDT - 1), deps=[cast_evs[i]])
    prev_mm = None
    for i in range(NDT):
        d = [cast_evs[i]] + ([prev_mm] if prev_mm else [])
        ev_sq = ACT(sch, sb["sq"], xp_tiles[i], AF.Square, deps=d)
        prev_mm = MM(sch, sb["r2"], ones_col_sb, sb["sq"], start=(i == 0),
                     stop=(i == NDT - 1), deps=[ev_sq])
    e1 = TS(sch, sb["row_m"], sb["r1"], 1.0 / D, None, ALU.mult, deps=[ev_mm1])
    e2 = TS(sch, sb["row_v"], sb["r2"], 1.0 / D, None, ALU.mult, deps=[prev_mm])
    e3 = TT(sch, sb["row_t"], sb["row_m"], sb["row_m"], ALU.mult, deps=[e1])
    e4 = TT(sch, sb["row_v"], sb["row_v"], sb["row_t"], ALU.subtract, deps=[e2, e3])
    e4b = TS(sch, sb["row_v"], sb["row_v"], LN_EPS, None, ALU.add, deps=[e4])
    e5 = ACT(sch, sb["row_t"], sb["row_v"], AF.Sqrt, deps=[e4b])
    e6 = RECIP(sch, sb["row_r"], sb["row_t"], deps=[e5])
    e7 = TT(sch, sb["row_t"], sb["row_m"], sb["row_r"], ALU.mult, deps=[e6])
    e8 = TS(sch, sb["row_uf"], sb["row_t"], -1.0, None, ALU.mult, deps=[e7])
    e9 = COPY(sch, sb["row_rb"], sb["row_r"], deps=[e6])
    e10 = COPY(sch, sb["row_ub"], sb["row_uf"], deps=[e8])
    return [e9, e10], cast_evs



def build_l0_proj():
    TB = 512
    NB = TPC // TB
    nc = bass.Bass("TRN2", target_bir_lowering=False)
    xT = nc.dram_tensor("xT", [D, TPC], F32, kind="ExternalInput").ap()
    Wq = nc.dram_tensor("Wq", [D, D], F32, kind="ExternalInput").ap()
    Wk = nc.dram_tensor("Wk", [D, D], F32, kind="ExternalInput").ap()
    Wv = nc.dram_tensor("Wv", [D, D], F32, kind="ExternalInput").ap()
    Wb = nc.dram_tensor("Wb", [D, 8], F32, kind="ExternalInput").ap()
    qT = nc.dram_tensor("qT", [D, TPC], BF16, kind="ExternalOutput").ap()
    kT = nc.dram_tensor("kT", [D, TPC], BF16, kind="ExternalOutput").ap()
    vT = nc.dram_tensor("vT", [D, TPC], BF16, kind="ExternalOutput").ap()
    betaT = nc.dram_tensor("betaT", [8, TPC], F32, kind="ExternalOutput").ap()
    scr = nc.dram_tensor("scr", [8, TB], BF16, kind="Internal")

    cs = consts(nc)
    colsel = []
    for h in range(8):
        m = np.zeros((128, 8), np.float32)
        m[:, h] = 1.0
        colsel.append(nc.inline_tensor(_bfa(m), f"c_colsel{h}").ap())

    sch = Sched(nc)
    with ExitStack() as _st:
        xblk = _st.enter_context(nc.sbuf_tensor("xblk", [128, NDT * TB], F32))
        w_sb = _st.enter_context(nc.sbuf_tensor("w_sb", [128, 3 * NDT * D // 128 * 128 + NDT * 8], BF16))
        xp = _st.enter_context(nc.sbuf_tensor("xp", [128, NDT * TB], BF16))
        xb = _st.enter_context(nc.sbuf_tensor("xb", [128, NDT * TB], BF16))
        sq = _st.enter_context(nc.sbuf_tensor("sq", [128, TB], BF16))
        rows = _st.enter_context(nc.sbuf_tensor("rows", [1, 6 * TB], F32))
        rowsb = _st.enter_context(nc.sbuf_tensor("rowsb", [1, 2 * TB], BF16))
        sw = _st.enter_context(nc.sbuf_tensor("sw", [1, 3 * D + 8], BF16))
        qout = _st.enter_context(nc.sbuf_tensor("qout", [128, NDT * TB], BF16))
        kout = _st.enter_context(nc.sbuf_tensor("kout", [128, NDT * TB], BF16))
        vout = _st.enter_context(nc.sbuf_tensor("vout", [128, NDT * TB], BF16))
        bout = _st.enter_context(nc.sbuf_tensor("bout", [8, TB], F32))
        ksq = _st.enter_context(nc.sbuf_tensor("ksq", [128, TB], BF16))
        nrm = _st.enter_context(nc.sbuf_tensor("nrm", [8, TB], F32))
        nrmb = _st.enter_context(nc.sbuf_tensor("nrmb", [8, TB], BF16))
        nrow = _st.enter_context(nc.sbuf_tensor("nrow", [1, 8 * TB], BF16))
        onesc = _st.enter_context(nc.sbuf_tensor("onesc", [128, 1], BF16))
        onesr = _st.enter_context(nc.sbuf_tensor("onesr", [1, 128], BF16))
        colsel_sb = _st.enter_context(nc.sbuf_tensor("colsel_sb", [128, 64], BF16))
        ps_p = _st.enter_context(nc.psum_tensor("ps_p", [128, TB], F32))
        ps_r1 = _st.enter_context(nc.psum_tensor("ps_r1", [1, TB], F32))
        ps_r2 = _st.enter_context(nc.psum_tensor("ps_r2", [1, TB], F32))
        ps_n = _st.enter_context(nc.psum_tensor("ps_n", [8, TB], F32))
        ps_sw = _st.enter_context(nc.psum_tensor("ps_sw", [1, 128], F32))
        ps_bc = _st.enter_context(nc.psum_tensor("ps_bc", [128, TB], F32))
        block = _st.enter_context(nc.Block())
        WQ0, WK0, WV0, WB0 = 0, NDT * D, 2 * NDT * D, 3 * NDT * D

        def wt(base, k, j):
            o = base + (k * NDT + j) * 128
            return w_sb[:, o:o + 128]

        def wbt(k):
            return w_sb[:, WB0 + k * 8:WB0 + k * 8 + 8]

        e_c1 = DMA(sch, "y", onesc[:], cs["ones_col"])
        e_c2 = DMA(sch, "y", onesr[:], cs["ones_row"])
        e_cs = [DMA(sch, "y", colsel_sb[:, h * 8:(h + 1) * 8], colsel[h]) for h in range(8)]
        e_wq = DMA(sch, "g", w_sb[:, WQ0:WQ0 + NDT * D].rearrange("p (k j n) -> p k j n", j=NDT, n=128),
                   Wq.rearrange("(k p) (j n) -> p k j n", p=128, n=128))
        e_wk = DMA(sch, "g", w_sb[:, WK0:WK0 + NDT * D].rearrange("p (k j n) -> p k j n", j=NDT, n=128),
                   Wk.rearrange("(k p) (j n) -> p k j n", p=128, n=128))
        e_wv = DMA(sch, "g", w_sb[:, WV0:WV0 + NDT * D].rearrange("p (k j n) -> p k j n", j=NDT, n=128),
                   Wv.rearrange("(k p) (j n) -> p k j n", p=128, n=128))
        e_wb = DMA(sch, "g", w_sb[:, WB0:WB0 + NDT * 8].rearrange("p (k n) -> p k n", n=8),
                   Wb.rearrange("(k p) n -> p k n", p=128))
        e_w = [e_wq, e_wk, e_wv, e_wb, e_c1, e_c2] + e_cs

        e_sw = []
        for (base, nj, wtf) in ((WQ0, NDT, None), (WK0, NDT, None), (WV0, NDT, None)):
            for j in range(NDT):
                ev = None
                for k in range(NDT):
                    d = list(e_w) + ([e_sw[-1]] if e_sw else [])
                    ev = MM(sch, ps_sw[:], onesc[:], wt(base, k, j), start=(k == 0),
                            stop=(k == NDT - 1), deps=d)
                e_sw.append(COPY(sch, sw[:, base // NDT + j * 128: base // NDT + j * 128 + 128], ps_sw[:], deps=[ev]))
        ev = None
        for k in range(NDT):
            ev = MM(sch, ps_sw[:, 0:8], onesc[:], wbt(k), start=(k == 0),
                    stop=(k == NDT - 1), deps=list(e_w) + [e_sw[-1]])
        e_sw.append(COPY(sch, sw[:, 3 * D:3 * D + 8], ps_sw[:, 0:8], deps=[ev]))

        sb = {
            "sq": sq[:], "r1": ps_r1[:], "r2": ps_r2[:],
            "row_m": rows[:, 0:TB], "row_v": rows[:, TB:2 * TB],
            "row_t": rows[:, 2 * TB:3 * TB], "row_r": rows[:, 3 * TB:4 * TB],
            "row_uf": rows[:, 4 * TB:5 * TB],
            "row_rb": rowsb[:, 0:TB], "row_ub": rowsb[:, TB:2 * TB],
        }

        prev_block_done = sch.barrier()
        for blk in range(NB):
            xf_t = [xblk[:, i * TB:(i + 1) * TB] for i in range(NDT)]
            xp_t = [xp[:, i * TB:(i + 1) * TB] for i in range(NDT)]
            xb_t = [xb[:, i * TB:(i + 1) * TB] for i in range(NDT)]
            e_x = DMA(sch, "y", xblk[:].rearrange("p (a n) -> p a n", n=TB),
                      xT.rearrange("(a p) n -> p a n", p=128)[:, :, blk * TB:(blk + 1) * TB],
                      deps=prev_block_done)
            ev_rows, _ = emit_ln_stats(sch, sb, xf_t, xp_t, TB, onesc[:], [e_x])
            e_rbc = MM(sch, ps_bc[:], onesr[:], sb["row_rb"], start=True, stop=True,
                       deps=ev_rows)
            e_xb = [TT(sch, xb_t[i], xf_t[i], ps_bc[:], ALU.mult, deps=[e_rbc])
                    for i in range(NDT)]
            e_xb_all = e_xb

            prev_evac = [None]

            def proj(base, j, outtile, act, extra_deps=(), swoff=None):
                ev = None
                for k in range(NDT):
                    d = list(extra_deps) + [e_xb[k]]
                    if k == 0 and prev_evac[0] is not None:
                        d.append(prev_evac[0])
                    ev = MM(sch, ps_p[:], wt(base, k, j), xb_t[k], start=(k == 0),
                            stop=False, deps=d)
                so = swoff if swoff is not None else base // NDT + j * 128
                ev = MM(sch, ps_p[:], sw[:, so:so + 128], sb["row_ub"], start=False,
                        stop=True, deps=[ev] + list(ev_rows))
                if act == "copy":
                    e = COPY(sch, outtile, ps_p[:], deps=[ev])
                elif act == "silu":
                    e = ACT(sch, outtile, ps_p[:], AF.Silu, deps=[ev])
                else:
                    raise ValueError(act)
                prev_evac[0] = e
                return e

            e_q = [proj(WQ0, j, qout[:, j * TB:(j + 1) * TB], "copy") for j in range(NDT)]
            e_k = [proj(WK0, j, kout[:, j * TB:(j + 1) * TB], "copy") for j in range(NDT)]
            e_v = [proj(WV0, j, vout[:, j * TB:(j + 1) * TB], "silu") for j in range(NDT)]
            ev = None
            for k in range(NDT):
                ev = MM(sch, ps_n[:], wbt(k), xb_t[k], start=(k == 0), stop=False,
                        deps=[e_xb[k]])
            ev = MM(sch, ps_n[:], sw[:, 3 * D:3 * D + 8], sb["row_ub"], start=False,
                    stop=True, deps=[ev] + list(ev_rows))
            e_b = ACT(sch, bout[:], ps_n[:], AF.Sigmoid, deps=[ev])

            evn = e_b
            prev = None
            for h in range(8):
                d = [e_k[h], e_b] + ([prev] if prev else [])
                e_ks = ACT(sch, ksq[:], kout[:, h * TB:(h + 1) * TB], AF.Square, deps=d)
                prev = MM(sch, ps_n[:], colsel_sb[:, h * 8:(h + 1) * 8], ksq[:],
                          start=(h == 0), stop=(h == 7), deps=[e_ks])
            e_n1 = ACT(sch, nrm[:], ps_n[:], AF.Sqrt, deps=[prev])
            e_n2 = TS(sch, nrm[:], nrm[:], 1e-12, None, ALU.max, deps=[e_n1])
            e_n3 = RECIP(sch, nrm[:], nrm[:], deps=[e_n2])
            e_n3 = COPY(sch, nrmb[:], nrm[:], deps=[e_n3])
            e_n4 = DMA(sch, "y", scr.ap(), nrmb[:], deps=[e_n3])
            e_n5 = DMA(sch, "y", nrow[:].rearrange("p (a n) -> p a n", n=TB),
                       scr.ap().rearrange("(u a) n -> u a n", u=1), deps=[e_n4])
            e_kn = []
            for h in range(8):
                e_bc = MM(sch, ps_bc[:], onesr[:],
                          nrow[:, h * TB:(h + 1) * TB], start=True, stop=True,
                          deps=[e_n5] + ([e_kn[-1]] if e_kn else []))
                e_kn.append(TT(sch, kout[:, h * TB:(h + 1) * TB],
                               kout[:, h * TB:(h + 1) * TB], ps_bc[:], ALU.mult,
                               deps=[e_bc]))

            sl = slice(blk * TB, (blk + 1) * TB)
            e_s = [
                DMA(sch, "y", qT.rearrange("(a p) n -> p a n", p=128)[:, :, sl],
                    qout[:].rearrange("p (a n) -> p a n", n=TB), deps=e_q),
                DMA(sch, "y", vT.rearrange("(a p) n -> p a n", p=128)[:, :, sl],
                    vout[:].rearrange("p (a n) -> p a n", n=TB), deps=e_v),
                DMA(sch, "y", betaT[:, sl], bout[:], deps=[e_b]),
                DMA(sch, "y", kT.rearrange("(a p) n -> p a n", p=128)[:, :, sl],
                    kout[:].rearrange("p (a n) -> p a n", n=TB), deps=e_kn),
            ]
            prev_block_done = e_s + [e_kn[-1]] + e_q + e_v + [e_b]
        sch.build(block)
    return nc, {}



def kernel(**inputs):
    raise NotImplementedError("under construction")



def MMg(sch, out, lhsT, rhs, start, stop, deps=()):
    return sch.em("t", lambda e: e.matmul(out, lhsT, rhs, start=start, stop=stop,
                                          skip_group_check=True), deps)


def build_l0_scan():
    C = 128
    NCH = L // C
    NBH = 4
    NJ = 7
    nc = bass.Bass("TRN2", target_bir_lowering=False)
    kf = nc.dram_tensor("kf", [NBH, 128, L], BF16, kind="ExternalInput").ap()
    qf = nc.dram_tensor("qf", [NBH, 128, L], BF16, kind="ExternalInput").ap()
    krm = nc.dram_tensor("krm", [NBH, L, 128], BF16, kind="ExternalInput").ap()
    vrm = nc.dram_tensor("vrm", [NBH, L, 128], BF16, kind="ExternalInput").ap()
    bcol = nc.dram_tensor("bcol", [NBH, 128, NCH], F32, kind="ExternalInput").ap()
    brow = nc.dram_tensor("brow", [NBH, 1, L], F32, kind="ExternalInput").ap()
    Oout = nc.dram_tensor("O", [NBH, L, 128], BF16, kind="ExternalOutput").ap()

    ml = -np.tril(np.ones((C, C), np.float32), -1)
    mu = -np.triu(np.ones((C, C), np.float32), 1)
    mui = np.triu(np.ones((C, C), np.float32))
    c_ml = nc.inline_tensor(_bfa(ml), "c_ml").ap()
    c_mu = nc.inline_tensor(_bfa(mu), "c_mu").ap()
    c_mui = nc.inline_tensor(_bfa(mui), "c_mui").ap()
    c_onesr = nc.inline_tensor(np.ones((1, 128), np.float32), "c_onesr_f").ap()

    sch = Sched(nc)
    with ExitStack() as _st:
        sb = {}
        for n, sh, dt in [
            ("ml", [C, C], BF16), ("mu", [C, C], BF16), ("mui", [C, C], BF16),
            ("onesr", [1, 128], F32),
            ("S", [128, 128], F32), ("S0b", [128, 128], BF16),
            ("X", [C, 128], BF16), ("N", [C, C], BF16), ("NT", [C, C], BF16),
            ("Lq", [C, C], BF16), ("bu", [C, C], BF16), ("tmpX", [C, 128], F32),
            ("bcolS", [128, NCH], F32), ("browS", [1, L], F32),
            ("kf0", [128, C], BF16), ("kf1", [128, C], BF16),
            ("qf0", [128, C], BF16), ("qf1", [128, C], BF16),
            ("krm0", [C, 128], BF16), ("krm1", [C, 128], BF16),
            ("vrm0", [C, 128], BF16), ("vrm1", [C, 128], BF16),
            ("Oo0", [C, 128], BF16), ("Oo1", [C, 128], BF16),
        ]:
            sb[n] = _st.enter_context(nc.sbuf_tensor(n, sh, dt))
        ps = {}
        for n in ["G", "T1", "KQ", "O", "A", "N2", "NT2", "S"]:
            ps[n] = _st.enter_context(nc.psum_tensor(f"ps{n}", [128, 128], F32))
        block = _st.enter_context(nc.Block())

        e_c = [DMA(sch, "y", sb["ml"][:], c_ml), DMA(sch, "y", sb["mu"][:], c_mu),
               DMA(sch, "y", sb["mui"][:], c_mui), DMA(sch, "y", sb["onesr"][:], c_onesr)]

        for b in range(NBH):
            bh_start = sch.barrier()
            eb1 = DMA(sch, "y", sb["bcolS"][:], bcol[b], deps=bh_start)
            eb2 = DMA(sch, "y", sb["browS"][:], brow[b], deps=bh_start)
            ez1 = sch.em("v", lambda e: e.memset(sb["S"][:], 0.0), deps=bh_start)
            ez2 = sch.em("v", lambda e: e.memset(sb["S0b"][:], 0.0), deps=bh_start)
            e0 = [eb1, eb2, ez1, ez2] + e_c

            def bufs(c):
                p = c % 2
                return (sb[f"kf{p}"], sb[f"qf{p}"], sb[f"krm{p}"], sb[f"vrm{p}"],
                        sb[f"Oo{p}"], p)

            def load_chunk(c, deps):
                kfb, qfb, krmb, vrmb, _, _ = bufs(c)
                return [
                    DMA(sch, "y", kfb[:], kf[b][:, c * C:(c + 1) * C], deps=deps),
                    DMA(sch, "y", qfb[:], qf[b][:, c * C:(c + 1) * C], deps=deps),
                    DMA(sch, "y", krmb[:], krm[b][c * C:(c + 1) * C, :], deps=deps),
                    DMA(sch, "y", vrmb[:], vrm[b][c * C:(c + 1) * C, :], deps=deps),
                ]

            st_loaded = [None, None]
            st_freed = [e0, e0]
            st_store = [None, None]
            e_prevS = e0
            e_prevO = None
            st_loaded[0] = load_chunk(0, e0)

            for c in range(NCH):
                kfb, qfb, krmb, vrmb, Oob, p = bufs(c)
                eL = st_loaded[p]
                bc = sb["bcolS"][:, c:c + 1]
                eG = MM(sch, ps["G"][:], kfb[:], kfb[:], True, True, deps=eL)
                eT1 = MM(sch, ps["T1"][:], kfb[:], sb["S0b"][:], True, True,
                         deps=eL + e_prevS)
                eKQ = MM(sch, ps["KQ"][:], kfb[:], qfb[:], True, True, deps=eL)
                eO1 = MMg(sch, ps["O"][:], qfb[:], sb["S0b"][:], True, False,
                          deps=eL + e_prevS + ([e_prevO] if e_prevO else []))
                eN0 = sch.em("v", lambda e, bc=bc: e.tensor_scalar(
                    sb["tmpX"][:], ps["G"][:], bc, None, ALU.mult), deps=[eG])
                eN = TT(sch, sb["N"][:], sb["tmpX"][:], sb["ml"][:], ALU.mult, deps=[eN0])
                ebr = MM(sch, ps["A"][:], sb["onesr"][:],
                         sb["browS"][:, c * C:(c + 1) * C], True, True,
                         deps=e0 + [st_freed[p][0]] if False else e0)
                ebu = TT(sch, sb["bu"][:], ps["A"][:], sb["mu"][:], ALU.mult, deps=[ebr])
                eNT = TT(sch, sb["NT"][:], ps["G"][:], sb["bu"][:], ALU.mult,
                         deps=[eG, ebu])
                eX0 = STT(sch, sb["tmpX"][:], ps["T1"][:], -1.0, vrmb[:], ALU.mult,
                          ALU.add, deps=[eT1, eN])
                eX = sch.em("v", lambda e, bc=bc: e.tensor_scalar(
                    sb["X"][:], sb["tmpX"][:], bc, None, ALU.mult), deps=[eX0])
                eLq = TT(sch, sb["Lq"][:], ps["KQ"][:], sb["mui"][:], ALU.mult,
                         deps=[eKQ])
                eXc, eNc, eNTc = eX, eN, eNT
                for j in range(NJ):
                    eA = MM(sch, ps["A"][:], sb["NT"][:], sb["X"][:], True, True,
                            deps=[eXc, eNTc, ebu])
                    eXc = TT(sch, sb["X"][:], sb["X"][:], ps["A"][:], ALU.add, deps=[eA])
                    if j < NJ - 1:
                        eN2 = MM(sch, ps["N2"][:], sb["NT"][:], sb["N"][:], True, True,
                                 deps=[eNc, eNTc])
                        eNT2 = MM(sch, ps["NT2"][:], sb["N"][:], sb["NT"][:], True, True,
                                  deps=[eNc, eNTc])
                        eNc = COPY(sch, sb["N"][:], ps["N2"][:], deps=[eN2, eNT2, eA])
                        eNTc = COPY(sch, sb["NT"][:], ps["NT2"][:], deps=[eNT2, eA, eXc])
                eO2 = MMg(sch, ps["O"][:], sb["Lq"][:], sb["X"][:], False, True,
                          deps=[eO1, eLq, eXc])
                dO = [eO2] + ([st_store[p]] if st_store[p] else [])
                eOe = COPY(sch, Oob[:], ps["O"][:], deps=dO)
                e_prevO = eOe
                st_store[p] = DMA(sch, "y", Oout[b][c * C:(c + 1) * C, :], Oob[:],
                                  deps=[eOe])
                eS = MM(sch, ps["S"][:], krmb[:], sb["X"][:], True, True,
                        deps=[eXc] + eL + e_prevS)
                eS1 = TT(sch, sb["S"][:], sb["S"][:], ps["S"][:], ALU.add, deps=[eS])
                eS2 = COPY(sch, sb["S0b"][:], sb["S"][:], deps=[eS1])
                e_prevS = [eS1, eS2]
                st_freed[p] = [eG, eKQ, eX0, eS, eOe]
                if c + 1 < NCH:
                    st_loaded[(c + 1) % 2] = load_chunk(c + 1, st_freed[(c + 1) % 2])
        sch.build(block)
    return nc, {}



TWO_PI = float(2 * np.pi)


def emit_range_reduce(sch, t_f32, t_i32, scr_f32, deps):
    e1 = TS(sch, scr_f32, t_f32, 1.0 / TWO_PI, None, ALU.mult, deps=deps)
    e2 = COPY(sch, t_i32, scr_f32, deps=[e1])
    e3 = COPY(sch, scr_f32, t_i32, deps=[e2])
    e4 = STT(sch, t_f32, scr_f32, -TWO_PI, t_f32, ALU.mult, ALU.add, deps=[e3])
    e5 = TS(sch, scr_f32, t_f32, float(np.pi), None, ALU.is_gt, deps=[e4])
    e6 = STT(sch, t_f32, scr_f32, -TWO_PI, t_f32, ALU.mult, ALU.add, deps=[e5])
    e7 = TS(sch, scr_f32, t_f32, float(-np.pi), None, ALU.is_lt, deps=[e6])
    e8 = STT(sch, t_f32, scr_f32, TWO_PI, t_f32, ALU.mult, ALU.add, deps=[e7])
    return e8


def build_l0_postffn():
    TB = 256
    NB = TPC // TB
    DF = 4096
    NJ = DF // 128
    nc = bass.Bass("TRN2", target_bir_lowering=False)
    xT = nc.dram_tensor("xT", [D, TPC], F32, kind="ExternalInput").ap()
    oT = nc.dram_tensor("oT", [D, TPC], BF16, kind="ExternalInput").ap()
    lnw = nc.dram_tensor("lnw", [128, NDT], F32, kind="ExternalInput").ap()
    lnb = nc.dram_tensor("lnb", [128, NDT], F32, kind="ExternalInput").ap()
    W1 = nc.dram_tensor("W1", [D, DF], F32, kind="ExternalInput").ap()
    b1c = nc.dram_tensor("b1c", [128, NJ], F32, kind="ExternalInput").ap()
    thc = nc.dram_tensor("thc", [128, NJ], F32, kind="ExternalInput").ap()
    wrc = nc.dram_tensor("wrc", [128, NJ], F32, kind="ExternalInput").ap()
    wic = nc.dram_tensor("wic", [128, NJ], F32, kind="ExternalInput").ap()
    W2 = nc.dram_tensor("W2", [DF, D], F32, kind="ExternalInput").ap()
    b2c = nc.dram_tensor("b2c", [128, NDT], F32, kind="ExternalInput").ap()
    yT = nc.dram_tensor("yT", [D, TPC], F32, kind="ExternalOutput").ap()

    cs = consts(nc)
    sch = Sched(nc)
    with ExitStack() as _st:
        S = {}
        for n, sh, dt in [
            ("w1", [128, NDT * DF], BF16), ("w2", [128, NJ * D], BF16),
            ("x", [128, NDT * TB], F32), ("o", [128, NDT * TB], BF16),
            ("xp", [128, NDT * TB], BF16), ("xb", [128, NDT * TB], BF16),
            ("hid", [128, NJ * TB], BF16), ("sq", [128, TB], BF16),
            ("rows", [1, 6 * TB], F32), ("rowsb", [1, 2 * TB], BF16),
            ("sw1", [1, DF], BF16),
            ("s_lnw", [128, NDT], F32), ("s_lnb", [128, NDT], F32),
            ("s_b1", [128, NJ], F32), ("s_th", [128, NJ], F32),
            ("coef", [128, NJ], F32), ("s_b2", [128, NDT], F32),
            ("t1", [128, TB], F32), ("t2", [128, TB], F32), ("t3", [128, TB], F32),
            ("ti", [128, TB], mybir.dt.int32),
            ("u1", [128, TB], F32), ("u2", [128, TB], F32), ("u3", [128, TB], F32),
            ("ui", [128, TB], mybir.dt.int32),
            ("c1", [128, NJ], F32), ("c2", [128, NJ], F32),
            ("ci", [128, NJ], mybir.dt.int32),
            ("onesc", [128, 1], BF16), ("onesr", [1, 128], BF16),
        ]:
            S[n] = _st.enter_context(nc.sbuf_tensor(n, sh, dt))
        P = {}
        for n, sh in [("p", [128, TB]), ("p2", [128, TB]), ("r1", [1, TB]),
                      ("r2", [1, TB]), ("bc1", [128, TB]), ("bc2", [128, TB]),
                      ("sw", [1, 128])]:
            P[n] = _st.enter_context(nc.psum_tensor("ps" + n, sh, F32))
        block = _st.enter_context(nc.Block())

        e_c = [DMA(sch, "y", S["onesc"][:], cs["ones_col"]),
               DMA(sch, "y", S["onesr"][:], cs["ones_row"])]
        for nm, src in [("s_lnw", lnw), ("s_lnb", lnb), ("s_b1", b1c), ("s_th", thc),
                        ("s_b2", b2c)]:
            e_c.append(DMA(sch, "y", S[nm][:], src))
        e_wr = DMA(sch, "y", S["c1"][:], wrc)
        e_wi = DMA(sch, "y", S["c2"][:], wic)
        e_w1 = DMA(sch, "g", S["w1"][:].rearrange("p (k j n) -> p k j n", j=NJ, n=128),
                   W1.rearrange("(k p) (j n) -> p k j n", p=128, n=128))
        e_w2 = DMA(sch, "g", S["w2"][:].rearrange("p (k j n) -> p k j n", j=NDT, n=128),
                   W2.rearrange("(k p) (j n) -> p k j n", p=128, n=128))
        e_c += [e_w1, e_w2]

        etmp = COPY(sch, S["coef"][:], S["s_th"][:], deps=e_c + [e_wr, e_wi])
        err = emit_range_reduce(sch, S["coef"][:], S["ci"][:], S["t1"][:, 0:NJ], [etmp])
        esin = ACT(sch, S["t2"][:, 0:NJ], S["coef"][:], AF.Sin, deps=[err])
        ec1 = STT(sch, S["coef"][:], S["coef"][:], -1.0, S["coef"][:], ALU.mult,
                  ALU.bypass, deps=[esin])
        ec2 = TS(sch, S["coef"][:], S["coef"][:], float(np.pi / 2), None, ALU.add,
                 deps=[ec1])
        ec3 = emit_range_reduce(sch, S["coef"][:], S["ci"][:], S["t1"][:, 0:NJ], [ec2])
        ecos = ACT(sch, S["t3"][:, 0:NJ], S["coef"][:], AF.Sin, deps=[ec3])
        eq1 = TT(sch, S["t3"][:, 0:NJ], S["t3"][:, 0:NJ], S["c1"][:], ALU.mult,
                 deps=[ecos])
        eq2 = TT(sch, S["t2"][:, 0:NJ], S["t2"][:, 0:NJ], S["c2"][:], ALU.mult,
                 deps=[esin, ecos])
        eq3 = TT(sch, S["t3"][:, 0:NJ], S["t3"][:, 0:NJ], S["t2"][:, 0:NJ], ALU.add,
                 deps=[eq1, eq2])
        eq4 = TT(sch, S["coef"][:], S["t3"][:, 0:NJ], S["c2"][:], ALU.mult, deps=[eq3])
        e_coef = TS(sch, S["coef"][:], S["coef"][:], 0.1, None, ALU.mult, deps=[eq4])
        et1 = STT(sch, S["s_th"][:], S["s_b1"][:], 0.1, S["s_th"][:], ALU.mult, ALU.add,
                  deps=e_c + [ecos])
        e_th = emit_range_reduce(sch, S["s_th"][:], S["ci"][:], S["t1"][:, 0:NJ], [et1])

        e_sw = []
        prev = None
        for j in range(NJ):
            ev = None
            for k in range(NDT):
                d = list(e_c) + ([prev] if prev else [])
                ev = MM(sch, P["sw"][:], S["onesc"][:],
                        S["w1"][:, (k * NJ + j) * 128:(k * NJ + j) * 128 + 128],
                        start=(k == 0), stop=(k == NDT - 1), deps=d)
            prev = COPY(sch, S["sw1"][:, j * 128:(j + 1) * 128], P["sw"][:], deps=[ev])
            e_sw.append(prev)

        sb = {
            "sq": S["sq"][:], "r1": P["r1"][:], "r2": P["r2"][:],
            "row_m": S["rows"][:, 0:TB], "row_v": S["rows"][:, TB:2 * TB],
            "row_t": S["rows"][:, 2 * TB:3 * TB], "row_r": S["rows"][:, 3 * TB:4 * TB],
            "row_uf": S["rows"][:, 4 * TB:5 * TB],
            "row_rb": S["rowsb"][:, 0:TB], "row_ub": S["rowsb"][:, TB:2 * TB],
        }

        prev_blk = sch.barrier()
        for blk in range(NB):
            sl = slice(blk * TB, (blk + 1) * TB)
            x_t = [S["x"][:, i * TB:(i + 1) * TB] for i in range(NDT)]
            o_t = [S["o"][:, i * TB:(i + 1) * TB] for i in range(NDT)]
            xp_t = [S["xp"][:, i * TB:(i + 1) * TB] for i in range(NDT)]
            xb_t = [S["xb"][:, i * TB:(i + 1) * TB] for i in range(NDT)]
            e_x = DMA(sch, "y", S["x"][:].rearrange("p (a n) -> p a n", n=TB),
                      xT.rearrange("(a p) n -> p a n", p=128)[:, :, sl], deps=prev_blk)
            e_o = DMA(sch, "y", S["o"][:].rearrange("p (a n) -> p a n", n=TB),
                      oT.rearrange("(a p) n -> p a n", p=128)[:, :, sl], deps=prev_blk)

            ev_mm1 = None
            for i in range(NDT):
                ev_mm1 = MM(sch, sb["r1"], S["onesc"][:], o_t[i], start=(i == 0),
                            stop=(i == NDT - 1), deps=[e_o] + prev_blk)
            prev_mm = None
            for i in range(NDT):
                d = [e_o] + ([prev_mm] if prev_mm else []) + prev_blk
                ev_sq = ACT(sch, sb["sq"], o_t[i], AF.Square, deps=d)
                prev_mm = MM(sch, sb["r2"], S["onesc"][:], sb["sq"], start=(i == 0),
                             stop=(i == NDT - 1), deps=[ev_sq])
            e1 = TS(sch, sb["row_m"], sb["r1"], 1.0 / D, None, ALU.mult, deps=[ev_mm1])
            e2 = TS(sch, sb["row_v"], sb["r2"], 1.0 / D, None, ALU.mult, deps=[prev_mm])
            e3 = TT(sch, sb["row_t"], sb["row_m"], sb["row_m"], ALU.mult, deps=[e1])
            e4 = TT(sch, sb["row_v"], sb["row_v"], sb["row_t"], ALU.subtract, deps=[e2, e3])
            e4b = TS(sch, sb["row_v"], sb["row_v"], LN_EPS, None, ALU.add, deps=[e4])
            e5 = ACT(sch, sb["row_t"], sb["row_v"], AF.Sqrt, deps=[e4b])
            e6 = RECIP(sch, sb["row_r"], sb["row_t"], deps=[e5])
            e7 = TT(sch, sb["row_t"], sb["row_m"], sb["row_r"], ALU.mult, deps=[e6])
            e8 = TS(sch, sb["row_uf"], sb["row_t"], -1.0, None, ALU.mult, deps=[e7])
            e9 = COPY(sch, sb["row_rb"], sb["row_r"], deps=[e6])
            e10 = COPY(sch, sb["row_ub"], sb["row_uf"], deps=[e8])
            eb1 = MM(sch, P["bc1"][:], S["onesr"][:], sb["row_rb"], True, True,
                     deps=[e9] + prev_blk)
            eb2 = MM(sch, P["bc2"][:], S["onesr"][:], sb["row_ub"], True, True,
                     deps=[e10] + prev_blk)
            e_xn = []
            for i in range(NDT):
                dprev = [e_xn[-1]] if e_xn else []
                ea = TT(sch, S["t1"][:], o_t[i], P["bc1"][:], ALU.mult,
                        deps=[eb1, e_o] + dprev)
                eb = TT(sch, S["t1"][:], S["t1"][:], P["bc2"][:], ALU.add, deps=[eb2, ea])
                ec = TS(sch, S["t1"][:], S["t1"][:], S["s_lnw"][:, i:i + 1],
                        S["s_lnb"][:, i:i + 1], ALU.mult, ALU.add, deps=[eb])
                ed = TT(sch, x_t[i], x_t[i], S["t1"][:], ALU.add, deps=[ec, e_x])
                e_xn.append(ed)
            ev_rows, cast_evs = emit_ln_stats(sch, sb, x_t, xp_t, TB, S["onesc"][:],
                                              e_xn + [eb1, eb2])
            eb3 = MM(sch, P["bc1"][:], S["onesr"][:], sb["row_rb"], True, True,
                     deps=ev_rows)
            e_xb = [TT(sch, xb_t[i], x_t[i], P["bc1"][:], ALU.mult, deps=[eb3, e_xn[i]])
                    for i in range(NDT)]
            e_hid = []
            for j in range(NJ):
                par = j % 2
                ps = P["p"] if par == 0 else P["p2"]
                v1 = S["t1"][:] if par == 0 else S["u1"][:]
                v2 = S["t2"][:] if par == 0 else S["u2"][:]
                v3 = S["t3"][:] if par == 0 else S["u3"][:]
                vi = S["ti"][:] if par == 0 else S["ui"][:]
                ev = None
                for k in range(NDT):
                    d = [e_xb[k]] + ([e_hid[j - 2]] if j >= 2 else [])
                    ev = MM(sch, ps[:], S["w1"][:, (k * NJ + j) * 128:(k * NJ + j) * 128 + 128],
                            xb_t[k], start=(k == 0), stop=False, deps=d)
                ev = MM(sch, ps[:], S["sw1"][:, j * 128:(j + 1) * 128], sb["row_ub"],
                        False, True, deps=[ev] + ev_rows + e_sw)
                ja = j
                ea = TS(sch, v1, ps[:], 0.1, S["s_th"][:, ja:ja + 1],
                        ALU.mult, ALU.add, deps=[ev, e_th] +
                        ([e_hid[j - 2]] if j >= 2 else []))
                er = emit_range_reduce(sch, v1, vi, v2, [ea])
                es = ACT(sch, v2, v1, AF.Sin, deps=[er])
                eh = TS(sch, v3, ps[:], S["s_b1"][:, ja:ja + 1], None,
                        ALU.add, deps=[ev, es])
                eri = sch.em("v", lambda e, ja=ja, v1=v1, v2=v2, v3=v3:
                             e.scalar_tensor_tensor(
                                 v1, v2, S["coef"][:, ja:ja + 1], v3,
                                 ALU.mult, ALU.add), deps=[es, eh, e_coef])
                erl = ACT(sch, S["hid"][:, j * TB:(j + 1) * TB], v1, AF.Relu,
                          deps=[eri])
                e_hid.append(erl)
            e_st = []
            eo_hist = []
            for i in range(NDT):
                par = i % 2
                ps = P["p"] if par == 0 else P["p2"]
                ev = None
                for k in range(NJ):
                    d = [e_hid[k]]
                    if k == 0 and len(eo_hist) >= 2:
                        d = d + [eo_hist[-2]]
                    if i == 0 and k == 0:
                        d += e_hid
                    if i == 1 and k == 0:
                        d += e_hid[-1:]
                    ev = MM(sch, ps[:], S["w2"][:, (k * NDT + i) * 128:(k * NDT + i) * 128 + 128],
                            S["hid"][:, k * TB:(k + 1) * TB], start=(k == 0),
                            stop=(k == NJ - 1), deps=d)
                eo = sch.em("v", lambda e, i=i, ps=ps: e.scalar_tensor_tensor(
                    x_t[i], ps[:], S["s_b2"][:, i:i + 1], x_t[i], ALU.add, ALU.add),
                    deps=[ev] + cast_evs)
                eo_hist.append(eo)
                e_st.append(DMA(sch, "y", yT.rearrange("(a p) n -> p a n", p=128)[:, i:i + 1, sl],
                                x_t[i].rearrange("p (u n) -> p u n", u=1), deps=[eo]))
            prev_evac2 = eo_hist[-1]
            prev_blk = e_st + [prev_evac2]
        sch.build(block)
    return nc, {}



def build_proj(spec, name):
    TB = 512
    NB = TPC // TB
    nc = bass.Bass("TRN2", target_bir_lowering=False)
    xT = nc.dram_tensor("xT", [D, TPC], F32, kind="ExternalInput").ap()
    waps, baps, oaps = {}, {}, {}
    totdo = 0
    for (pn, dout, act) in spec["projs"]:
        waps[pn] = nc.dram_tensor(f"W_{pn}", [D, dout], F32, kind="ExternalInput").ap()
        if pn in spec.get("biases", ()):
            baps[pn] = nc.dram_tensor(f"b_{pn}", [1, dout], F32, kind="ExternalInput").ap()
        oaps[pn] = nc.dram_tensor(f"{pn}T", [dout, TPC], BF16, kind="ExternalOutput").ap()
        totdo += dout
    gb = spec.get("gamma_beta", False)
    if gb:
        gam = nc.dram_tensor("gam", [128, NDT], F32, kind="ExternalInput").ap()
        betap = nc.dram_tensor("betap", [128, NDT], F32, kind="ExternalInput").ap()
    hn = spec.get("headnorm")
    if hn:
        for (nh, hd, src) in hn:
            nc.dram_tensor(f"n_{src}", [nh, 1], F32, kind="ExternalOutput")
    scr = nc.dram_tensor("scr", [max(1, totdo // 128), 128], F32, kind="Internal")
    cs = consts(nc)
    colsel = {}
    for (nh, hd, src) in (hn or ()):
        if (nh, hd) not in colsel:
            mats = []
            for i in range(NDT):
                m = np.zeros((128, nh), np.float32)
                for p in range(128):
                    h = (i * 128 + p) // hd
                    m[p, h] = 1.0
                mats.append(nc.inline_tensor(_bfa(m), f"c_cs{nh}_{hd}_{i}").ap())
            colsel[(nh, hd)] = mats

    sch = Sched(nc)
    with ExitStack() as _st:
        S = {}
        for n, sh, dt in [
            ("x", [128, NDT * TB], F32), ("xp", [128, NDT * TB], BF16),
            ("xb", [128, NDT * TB], BF16), ("sq", [128, TB], BF16),
            ("rows", [1, 6 * TB], F32), ("rowsb", [1, 2 * TB], BF16),
            ("w", [128, NDT * totdo], BF16), ("sw", [1, totdo], BF16),
            ("bcol", [128, totdo // 128], F32),
            ("onesc", [128, 1], BF16), ("onesr", [1, 128], BF16),
            ("gamc", [128, NDT], F32), ("betc", [128, NDT], F32),
            ("betb", [128, NDT], BF16),
            ("csel", [128, 64], BF16), ("nacc", [64, 2], F32),
            ("ksq", [128, TB], BF16), ("nred", [64, 1], F32),
        ]:
            S[n] = _st.enter_context(nc.sbuf_tensor(n, sh, dt))
        P = {}
        for n, sh in [("p", [128, TB]), ("r1", [1, TB]), ("r2", [1, TB]),
                      ("bc", [128, TB]), ("sw", [1, 128]), ("n", [64, TB])]:
            P[n] = _st.enter_context(nc.psum_tensor("ps" + n, sh, F32))
        block = _st.enter_context(nc.Block())

        e_c = [DMA(sch, "y", S["onesc"][:], cs["ones_col"]),
               DMA(sch, "y", S["onesr"][:], cs["ones_row"])]
        woff = {}
        off = 0
        for (pn, dout, act) in spec["projs"]:
            woff[pn] = off
            e_c.append(DMA(sch, "g",
                           S["w"][:, off * NDT:off * NDT + NDT * dout]
                           .rearrange("p (k j n) -> p k j n", j=dout // 128, n=128),
                           waps[pn].rearrange("(k p) (j n) -> p k j n", p=128, n=128)))
            off += dout
        if gb:
            e_c.append(DMA(sch, "y", S["gamc"][:], gam))
            e_c.append(DMA(sch, "y", S["betc"][:], betap))

        def wt(pn, k, j):
            dout = dict((p[0], p[1]) for p in spec["projs"])[pn]
            o = woff[pn] * NDT + (k * (dout // 128) + j) * 128
            return S["w"][:, o:o + 128]

        e_bc = []
        if gb or spec.get("biases"):
            if gb:
                e_bb = COPY(sch, S["betb"][:], S["betc"][:], deps=e_c)
            prevc = None
            for (pn, dout, act) in spec["projs"]:
                for j in range(dout // 128):
                    ev = None
                    if gb:
                        for k in range(NDT):
                            d = list(e_c) + [e_bb] + ([prevc] if prevc else [])
                            ev = MM(sch, P["sw"][:], S["betb"][:, k:k + 1], wt(pn, k, j),
                                    start=(k == 0), stop=(k == NDT - 1), deps=d)
                    if ev is None:
                        ev = sch.em("v", lambda e: e.memset(P["sw"][:], 0.0),
                                    deps=([prevc] if prevc else []) + list(e_c))
                    if pn in baps:
                        eb = DMA(sch, "y", S["rows"][:, 0:128],
                                 baps[pn][:, j * 128:(j + 1) * 128], deps=[ev])
                        ev = TT(sch, S["rows"][:, 0:128], S["rows"][:, 0:128], P["sw"][:],
                                ALU.add, deps=[eb, ev])
                        prevc = ev
                    else:
                        prevc = COPY(sch, S["rows"][:, 0:128], P["sw"][:], deps=[ev])
                    co = woff[pn] + j * 128
                    e1 = DMA(sch, "y", scr.ap()[co // 128:co // 128 + 1, :],
                             S["rows"][:, 0:128], deps=[prevc])
                    e_bc.append(e1)
                    prevc = e1
            e_bc.append(DMAS(sch, "y", S["bcol"][:],
                             scr.ap().rearrange("c p -> p c"), deps=e_bc))

        e_gs = []
        if gb:
            for (pn, dout, act) in spec["projs"]:
                for k in range(NDT):
                    for j in range(dout // 128):
                        e_gs.append(TS(sch, wt(pn, k, j), wt(pn, k, j),
                                       S["gamc"][:, k:k + 1], None, ALU.mult,
                                       deps=e_c + e_bc[-1:] + e_gs[-1:]))
        e_w = e_c + e_gs[-1:] if e_gs else list(e_c)

        e_sw = []
        prevmm = None
        for (pn, dout, act) in spec["projs"]:
            for j in range(dout // 128):
                ev = None
                for k in range(NDT):
                    d = list(e_w) + ([prevmm] if prevmm else [])
                    ev = MM(sch, P["sw"][:], S["onesc"][:], wt(pn, k, j),
                            start=(k == 0), stop=(k == NDT - 1), deps=d)
                prevmm = COPY(sch, S["sw"][:, woff[pn] + j * 128:woff[pn] + j * 128 + 128],
                              P["sw"][:], deps=[ev])
                e_sw.append(prevmm)

        sb = {
            "sq": S["sq"][:], "r1": P["r1"][:], "r2": P["r2"][:],
            "row_m": S["rows"][:, TB:2 * TB], "row_v": S["rows"][:, 2 * TB:3 * TB],
            "row_t": S["rows"][:, 3 * TB:4 * TB], "row_r": S["rows"][:, 4 * TB:5 * TB],
            "row_uf": S["rows"][:, 5 * TB:6 * TB],
            "row_rb": S["rowsb"][:, 0:TB], "row_ub": S["rowsb"][:, TB:2 * TB],
        }
        e_nrm = {}
        if hn:
            e_nrm[0] = sch.em("v", lambda e: e.memset(S["nacc"][:], -1.0), deps=[])

        prev_blk = sch.barrier()
        for blk in range(NB):
            sl = slice(blk * TB, (blk + 1) * TB)
            x_t = [S["x"][:, i * TB:(i + 1) * TB] for i in range(NDT)]
            xp_t = [S["xp"][:, i * TB:(i + 1) * TB] for i in range(NDT)]
            xb_t = [S["xb"][:, i * TB:(i + 1) * TB] for i in range(NDT)]
            e_x = DMA(sch, "y", S["x"][:].rearrange("p (a n) -> p a n", n=TB),
                      xT.rearrange("(a p) n -> p a n", p=128)[:, :, sl], deps=prev_blk)
            ev_rows, cast_evs = emit_ln_stats(sch, sb, x_t, xp_t, TB, S["onesc"][:],
                                              [e_x] + prev_blk)
            e_rbc = MM(sch, P["bc"][:], S["onesr"][:], sb["row_rb"], True, True,
                       deps=ev_rows + prev_blk)
            e_xb = [TT(sch, xb_t[i], x_t[i], P["bc"][:], ALU.mult, deps=[e_rbc])
                    for i in range(NDT)]
            prev_evac = [None]
            e_out = {}
            for (pn, dout, act) in spec["projs"]:
                e_out[pn] = []
                for j in range(dout // 128):
                    ev = None
                    for k in range(NDT):
                        d = [e_xb[k]] + ([prev_evac[0]] if prev_evac[0] else [])
                        ev = MM(sch, P["p"][:], wt(pn, k, j), xb_t[k], start=(k == 0),
                                stop=False, deps=d)
                    so = woff[pn] + j * 128
                    ev = MM(sch, P["p"][:], S["sw"][:, so:so + 128], sb["row_ub"],
                            False, True, deps=[ev] + ev_rows + e_sw[-1:])
                    ot = oaps[pn].rearrange("(a p) n -> p a n", p=128)
                    dst = S["ksq"][:]
                    hasb = gb or (pn in baps)
                    if hasb:
                        if act == "copy":
                            ee = TS(sch, dst, P["p"][:], S["bcol"][:, so // 128:so // 128 + 1],
                                    None, ALU.add, deps=[ev] + e_bc[-1:])
                        elif act == "silu":
                            ee = ACT(sch, dst, P["p"][:], AF.Silu,
                                     bias=S["bcol"][:, so // 128:so // 128 + 1],
                                     deps=[ev] + e_bc[-1:])
                        elif act == "sigmoid":
                            ee = ACT(sch, dst, P["p"][:], AF.Sigmoid,
                                     bias=S["bcol"][:, so // 128:so // 128 + 1],
                                     deps=[ev] + e_bc[-1:])
                    else:
                        if act == "copy":
                            ee = COPY(sch, dst, P["p"][:], deps=[ev])
                        elif act == "silu":
                            ee = ACT(sch, dst, P["p"][:], AF.Silu, deps=[ev])
                        elif act == "sigmoid":
                            ee = ACT(sch, dst, P["p"][:], AF.Sigmoid, deps=[ev])
                        elif act == "scale":
                            ee = TS(sch, dst, P["p"][:], spec["scale"][pn], None,
                                    ALU.mult, deps=[ev])
                    es = DMA(sch, "y", ot[:, j:j + 1, sl],
                             dst.rearrange("p (u n) -> p u n", u=1), deps=[ee])
                    prev_evac[0] = es
                    e_out[pn].append(es)
            prev_blk = [prev_evac[0]] + [e_x]
        sch.build(block)
    return nc, {}



def build_wvsum():
    nc = bass.Bass("TRN2", target_bir_lowering=False)
    Wv = nc.dram_tensor("Wv", [D, 8 * D], F32, kind="ExternalInput").ap()
    bv = nc.dram_tensor("bv", [1, 8 * D], F32, kind="ExternalInput").ap()
    Wt = nc.dram_tensor("Wt", [D, D], F32, kind="ExternalOutput").ap()
    bt = nc.dram_tensor("bt", [1, D], F32, kind="ExternalOutput").ap()
    sch = Sched(nc)
    with ExitStack() as _st:
        stg = _st.enter_context(nc.sbuf_tensor("stg", [128, 8 * D], F32))
        out = _st.enter_context(nc.sbuf_tensor("out", [128, D], F32))
        bvr = _st.enter_context(nc.sbuf_tensor("bvr", [1, 8 * D], F32))
        btr = _st.enter_context(nc.sbuf_tensor("btr", [1, D], F32))
        block = _st.enter_context(nc.Block())
        eprev = None
        for k in range(NDT):
            el = DMA(sch, "y", stg[:], Wv[k * 128:(k + 1) * 128, :],
                     deps=[eprev] if eprev else [])
            er = RED(sch, out[:].rearrange("p (j u) -> p j u", u=1),
                     stg[:].rearrange("p (j i) -> p j i", i=8), "sum", deps=[el])
            eprev = DMA(sch, "y", Wt[k * 128:(k + 1) * 128, :], out[:], deps=[er])
        el = DMA(sch, "y", bvr[:], bv)
        er = RED(sch, btr[:].rearrange("p (j u) -> p j u", u=1),
                 bvr[:].rearrange("p (j i) -> p j i", i=8), "sum", deps=[el])
        DMA(sch, "y", bt, btr[:], deps=[er])
        sch.build(block)
    return nc, {}



def build_l2_scan():
    NBH = 4
    QB = 512
    NQ = L // QB
    KB = 128
    NK = L // KB
    SCALE = 1.0 / np.sqrt(128.0)
    nc = bass.Bass("TRN2", target_bir_lowering=False)
    kf = nc.dram_tensor("kf", [NBH, 128, L], BF16, kind="ExternalInput").ap()
    qf = nc.dram_tensor("qf", [NBH, 128, L], BF16, kind="ExternalInput").ap()
    vrm = nc.dram_tensor("vrm", [NBH, L, 128], BF16, kind="ExternalInput").ap()
    OUT = nc.dram_tensor("OUT", [NBH, 128, L], BF16, kind="ExternalOutput").ap()
    cs = consts(nc)
    sch = Sched(nc)
    with ExitStack() as _st:
        S = {}
        for n, sh, dt in [
            ("kfA", [128, L], BF16), ("qfA", [128, L], BF16),
            ("kfB", [128, L], BF16), ("qfB", [128, L], BF16),
            ("vrA", [128, L], BF16), ("vrB", [128, L], BF16),
            ("pt0", [128, QB], BF16), ("pt1", [128, QB], BF16),
            ("pt2", [128, QB], BF16), ("pt3", [128, QB], BF16),
            ("sq", [128, QB], BF16), ("mrow", [1, 8], F32),
            ("nred", [1, QB], F32), ("ml", [1, 1], F32),
            ("negm", [128, 1], F32),
            ("lrow", [1, QB], F32), ("lrb", [1, QB], BF16),
            ("rbc", [128, QB], F32), ("outS", [128, QB], BF16),
            ("onesc", [128, 1], BF16), ("onesr", [1, 128], BF16),
        ]:
            S[n] = _st.enter_context(nc.sbuf_tensor(n, sh, dt))
        P = {}
        for n, sh in [("s0", [128, QB]), ("s1", [128, QB]), ("s2", [128, QB]),
                      ("s3", [128, QB]), ("o", [128, QB]),
                      ("l", [1, QB]), ("b", [128, QB]), ("n", [1, QB])]:
            P[n] = _st.enter_context(nc.psum_tensor("ps" + n, sh, F32))
        block = _st.enter_context(nc.Block())

        e_c = [DMA(sch, "y", S["onesc"][:], cs["ones_col"]),
               DMA(sch, "y", S["onesr"][:], cs["ones_row"])]

        buf = ["A", "B"]
        e_load = {"A": None, "B": None}
        e_free = {"A": list(e_c), "B": list(e_c)}
        prev_qb = []

        def load_bh(b, pb, deps):
            evs = [
                DMA(sch, "y", S["kf" + pb][:], kf[b], deps=deps),
                DMA(sch, "y", S["qf" + pb][:], qf[b], deps=deps),
                DMA(sch, "y", S["vr" + pb][:].rearrange("p (c n) -> p c n", n=128),
                    vrm[b].rearrange("(c p) n -> p c n", p=128), deps=deps),
            ]
            return evs

        e_load["A"] = load_bh(0, "A", e_c)
        for b in range(NBH):
            pb = buf[b % 2]
            eL = e_load[pb]
            kfb, qfb, vrb = S["kf" + pb], S["qf" + pb], S["vr" + pb]
            e_m = sch.em("v", lambda e: e.memset(S["mrow"][:], 0.0), deps=e_c + e_free[pb])
            prevmm = None
            for (src, co) in ((qfb, 0), (kfb, 1)):
                for c in range(NK):
                    d = eL + [e_m] + ([prevmm] if prevmm else [])
                    esq = ACT(sch, S["sq"][:, 0:KB], src[:, c * KB:(c + 1) * KB],
                              AF.Square, deps=d)
                    prevmm = MM(sch, P["n"][:, 0:KB], S["onesc"][:], S["sq"][:, 0:KB],
                                True, True, deps=[esq])
                    er = RED(sch, S["ml"][:], P["n"][:, 0:KB], "max", deps=[prevmm])
                    prevmm = TT(sch, S["mrow"][:, co + 2:co + 3], S["mrow"][:, co + 2:co + 3],
                                S["ml"][:], ALU.max, deps=[er, e_m])
            e1 = ACT(sch, S["mrow"][:, 4:6], S["mrow"][:, 2:4], AF.Sqrt, deps=[prevmm])
            e2 = TT(sch, S["mrow"][:, 0:1], S["mrow"][:, 4:5], S["mrow"][:, 5:6],
                    ALU.mult, deps=[e1])
            e3 = TS(sch, S["mrow"][:, 0:1], S["mrow"][:, 0:1], -SCALE, None, ALU.mult,
                    deps=[e2])
            e4 = COPY(sch, S["lrb"][:, 0:1], S["mrow"][:, 0:1], deps=[e3])
            e5 = MM(sch, P["b"][:, 0:1], S["onesr"][:], S["lrb"][:, 0:1], True, True,
                    deps=[e4])
            e_nm = COPY(sch, S["negm"][:], P["b"][:, 0:1], deps=[e5])
            if b + 1 < NBH:
                e_load[buf[(b + 1) % 2]] = load_bh(b + 1, buf[(b + 1) % 2],
                                                   e_free[buf[(b + 1) % 2]])
            freed = []
            for qb in range(NQ):
                qsl = slice(qb * QB, (qb + 1) * QB)
                e_pv_prev = None
                e_l_prev = None
                e_exp = {}
                for kb in range(NK):
                    par = kb % 4
                    psS = [P["s0"], P["s1"], P["s2"], P["s3"]][par]
                    ptb = [S["pt0"], S["pt1"], S["pt2"], S["pt3"]][par]
                    d = eL + [e_nm]
                    if kb >= 4:
                        d = d + [e_exp[kb - 4]]
                    elif prev_qb:
                        d = d + [prev_qb[2]]
                    eS = MM(sch, psS[:], kfb[:, kb * KB:(kb + 1) * KB], qfb[:, qsl],
                            True, True, deps=d)
                    dd = [eS]
                    if kb >= 4:
                        dd.append(e_pv_prev)
                    elif prev_qb:
                        dd.append(prev_qb[4])
                    eE = ACT(sch, ptb[:], psS[:], AF.Exp, bias=S["negm"][:, 0:1],
                             scale=SCALE, deps=dd)
                    e_exp[kb] = eE
                    dpv = [eE] + ([e_pv_prev] if e_pv_prev else [])
                    dls = [eE] + ([e_l_prev] if e_l_prev else [])
                    if kb == 0 and prev_qb:
                        dpv += [prev_qb[0]]
                        dls += [prev_qb[1]]
                    ePV = MM(sch, P["o"][:], vrb[:, kb * 128:(kb + 1) * 128], ptb[:],
                             start=(kb == 0), stop=(kb == NK - 1), deps=dpv)
                    eLs = MM(sch, P["l"][:], S["onesc"][:], ptb[:],
                             start=(kb == 0), stop=(kb == NK - 1), deps=dls)
                    e_pv_prev, e_l_prev = ePV, eLs
                er1 = COPY(sch, S["lrow"][:], P["l"][:], deps=[e_l_prev])
                er2 = RECIP(sch, S["lrow"][:], S["lrow"][:], deps=[er1])
                er3 = COPY(sch, S["lrb"][:], S["lrow"][:], deps=[er2])
                er4 = MM(sch, P["b"][:], S["onesr"][:], S["lrb"][:], True, True,
                         deps=[er3, e_nm])
                er5 = COPY(sch, S["rbc"][:], P["b"][:], deps=[er4])
                dpo = [er5, e_pv_prev] + ([prev_qb[6]] if prev_qb else [])
                eo = TT(sch, S["outS"][:], P["o"][:], S["rbc"][:], ALU.mult, deps=dpo)
                es = DMA(sch, "y", OUT[b][:, qsl], S["outS"][:], deps=[eo])
                freed += [es, eo, e_l_prev]
                prev_qb = [eo, er1, e_exp.get(NK - 1), e_exp.get(NK - 1),
                           e_pv_prev, e_pv_prev, es]
            e_free[pb] = freed + [e_exp[NK - 1], e_exp[NK - 2], e_exp[NK - 3],
                                  e_exp[NK - 4]]
        sch.build(block)
    return nc, {}



def build_l2_postffn():
    TB = 512
    NB = TPC // TB
    nc = bass.Bass("TRN2", target_bir_lowering=False)
    xT = nc.dram_tensor("xT", [D, TPC], F32, kind="ExternalInput").ap()
    attnT = nc.dram_tensor("attnT", [D, TPC], BF16, kind="ExternalInput").ap()
    WrT = nc.dram_tensor("WrT", [D, D], F32, kind="ExternalInput").ap()
    WkT = nc.dram_tensor("WkT", [D, D], F32, kind="ExternalInput").ap()
    WvT = nc.dram_tensor("WvT", [D, D], F32, kind="ExternalInput").ap()
    mur = nc.dram_tensor("mur", [128, NDT], F32, kind="ExternalInput").ap()
    muk = nc.dram_tensor("muk", [128, NDT], F32, kind="ExternalInput").ap()
    xlast = nc.dram_tensor("xlast", [128, NDT], F32, kind="ExternalInput").ap()
    yT = nc.dram_tensor("yT", [D, TPC], F32, kind="ExternalOutput").ap()
    cs = consts(nc)
    sch = Sched(nc)
    with ExitStack() as _st:
        S = {}
        for n, sh, dt in [
            ("wr", [128, NDT * D], BF16), ("wk", [128, NDT * D], BF16),
            ("wv", [128, NDT * D], BF16),
            ("x", [128, NDT * TB], F32), ("at", [128, NDT * TB], BF16),
            ("xp", [128, NDT * TB], BF16), ("sq", [128, TB], BF16),
            ("xn", [128, NDT * (TB + 1)], F32),
            ("xr", [128, NDT * TB], BF16), ("xk", [128, NDT * TB], BF16),
            ("rg", [128, NDT * TB], F32), ("ka", [128, NDT * TB], BF16),
            ("rows", [1, 6 * TB], F32), ("rowsb", [1, 2 * TB], BF16),
            ("murS", [128, NDT], F32), ("mukS", [128, NDT], F32),
            ("xl", [128, NDT], F32), ("xlb", [128, NDT], BF16),
            ("hrow", [1, 16], F32),
            ("t1", [128, TB], F32), ("carry", [128, NDT], F32),
            ("onesc", [128, 1], BF16), ("onesr", [1, 128], BF16),
        ]:
            S[n] = _st.enter_context(nc.sbuf_tensor(n, sh, dt))
        P = {}
        for n, sh in [("p", [128, TB]), ("r1", [1, TB]), ("r2", [1, TB]),
                      ("bc", [128, TB]), ("h", [1, 16])]:
            P[n] = _st.enter_context(nc.psum_tensor("ps" + n, sh, F32))
        block = _st.enter_context(nc.Block())

        e_c = [DMA(sch, "y", S["onesc"][:], cs["ones_col"]),
               DMA(sch, "y", S["onesr"][:], cs["ones_row"]),
               DMA(sch, "y", S["murS"][:], mur), DMA(sch, "y", S["mukS"][:], muk),
               DMA(sch, "y", S["xl"][:], xlast)]
        for nm, src in [("wr", WrT), ("wk", WkT), ("wv", WvT)]:
            e_c.append(DMA(sch, "g",
                           S[nm][:].rearrange("p (k j n) -> p k j n", j=NDT, n=128),
                           src.rearrange("(k p) (j n) -> p k j n", p=128, n=128)))

        eb = COPY(sch, S["xlb"][:], S["xl"][:], deps=e_c)
        e1 = MM(sch, P["h"][:, 0:NDT], S["onesc"][:], S["xlb"][:], True, True, deps=[eb])
        er1 = RED(sch, S["hrow"][:, 0:1], P["h"][:, 0:NDT], "sum", deps=[e1])
        esq = ACT(sch, S["sq"][:, 0:NDT], S["xlb"][:], AF.Square, deps=[eb])
        e2 = MM(sch, P["h"][:, 8:8 + NDT], S["onesc"][:], S["sq"][:, 0:NDT], True, True,
                deps=[esq, er1])
        er2 = RED(sch, S["hrow"][:, 1:2], P["h"][:, 8:8 + NDT], "sum", deps=[e2])
        e3 = TS(sch, S["hrow"][:, 0:2], S["hrow"][:, 0:2], 1.0 / D, None, ALU.mult,
                deps=[er2, er1])
        e4 = TT(sch, S["hrow"][:, 2:3], S["hrow"][:, 0:1], S["hrow"][:, 0:1], ALU.mult,
                deps=[e3])
        e5 = TT(sch, S["hrow"][:, 3:4], S["hrow"][:, 1:2], S["hrow"][:, 2:3],
                ALU.subtract, deps=[e4])
        e6 = TS(sch, S["hrow"][:, 3:4], S["hrow"][:, 3:4], LN_EPS, None, ALU.add,
                deps=[e5])
        e7 = ACT(sch, S["hrow"][:, 4:5], S["hrow"][:, 3:4], AF.Sqrt, deps=[e6])
        e8 = RECIP(sch, S["hrow"][:, 5:6], S["hrow"][:, 4:5], deps=[e7])
        e9 = TT(sch, S["hrow"][:, 6:7], S["hrow"][:, 0:1], S["hrow"][:, 5:6], ALU.mult,
                deps=[e8])
        e10 = TS(sch, S["hrow"][:, 6:7], S["hrow"][:, 6:7], -1.0, None, ALU.mult,
                 deps=[e9])
        eb1 = COPY(sch, S["rowsb"][:, 0:2], S["hrow"][:, 5:7], deps=[e10])
        ebc = MM(sch, P["bc"][:, 0:2], S["onesr"][:], S["rowsb"][:, 0:2], True, True,
                 deps=[eb1])
        e_hl = STT(sch, S["carry"][:], S["xl"][:], P["bc"][:, 0:1], S["xl"][:],
                   ALU.mult, ALU.bypass, deps=[ebc])
        e_hl = sch.em("v", lambda e: e.tensor_scalar(
            S["carry"][:], S["carry"][:], P["bc"][:, 1:2], None, ALU.add), deps=[e_hl])

        def wt(nm, k, j):
            o = (k * NDT + j) * 128
            return S[nm][:, o:o + 128]

        sb = {
            "sq": S["sq"][:], "r1": P["r1"][:], "r2": P["r2"][:],
            "row_m": S["rows"][:, TB:2 * TB], "row_v": S["rows"][:, 2 * TB:3 * TB],
            "row_t": S["rows"][:, 3 * TB:4 * TB], "row_r": S["rows"][:, 4 * TB:5 * TB],
            "row_uf": S["rows"][:, 5 * TB:6 * TB],
            "row_rb": S["rowsb"][:, 0:TB], "row_ub": S["rowsb"][:, TB:2 * TB],
        }
        prev_blk = sch.barrier()
        for blk in range(NB):
            sl = slice(blk * TB, (blk + 1) * TB)
            x_t = [S["x"][:, i * TB:(i + 1) * TB] for i in range(NDT)]
            at_t = [S["at"][:, i * TB:(i + 1) * TB] for i in range(NDT)]
            xp_t = [S["xp"][:, i * TB:(i + 1) * TB] for i in range(NDT)]
            xn_t = [S["xn"][:, i * (TB + 1):(i + 1) * (TB + 1)] for i in range(NDT)]
            e_x = DMA(sch, "y", S["x"][:].rearrange("p (a n) -> p a n", n=TB),
                      xT.rearrange("(a p) n -> p a n", p=128)[:, :, sl], deps=prev_blk)
            e_a = DMA(sch, "y", S["at"][:].rearrange("p (a n) -> p a n", n=TB),
                      attnT.rearrange("(a p) n -> p a n", p=128)[:, :, sl], deps=prev_blk)
            e_xn = []
            for i in range(NDT):
                e_xn.append(TT(sch, x_t[i], x_t[i], at_t[i], ALU.add,
                               deps=[e_x, e_a] + prev_blk))
            ev_rows, cast_evs = emit_ln_stats(sch, sb, x_t, xp_t, TB, S["onesc"][:],
                                              e_xn)
            e_car = []
            for i in range(NDT):
                e_car.append(COPY(sch, xn_t[i][:, 0:1], S["carry"][:, i:i + 1],
                                  deps=[e_hl] + prev_blk))
            e_rbc = MM(sch, P["bc"][:], S["onesr"][:], sb["row_rb"], True, True,
                       deps=ev_rows)
            e_ubc = MM(sch, P["r1"][:] if False else P["bc"][:], None, None, True, True) if False else None
            e_ub2 = MM(sch, P["r2"][:] if False else P["bc"][:], None, None, True, True) if False else None
            e_xnf = []
            eu = MM(sch, P["p"][:], S["onesr"][:], sb["row_ub"], True, True,
                    deps=ev_rows)
            for i in range(NDT):
                ea = TT(sch, S["t1"][:], x_t[i], P["bc"][:], ALU.mult,
                        deps=[e_rbc, e_xn[i]] + (e_xnf[-1:] if e_xnf else []))
                ebx = TT(sch, xn_t[i][:, 1:TB + 1], S["t1"][:], P["p"][:], ALU.add,
                         deps=[ea, eu, e_car[i]])
                e_xnf.append(ebx)
            e_hl = None
            e_carn = []
            for i in range(NDT):
                e_carn.append(COPY(sch, S["carry"][:, i:i + 1], xn_t[i][:, TB:TB + 1],
                                   deps=[e_xnf[i]]))
            e_xr, e_xk = [], []
            for i in range(NDT):
                d = [e_xnf[i], e_car[i]]
                ea = TT(sch, S["t1"][:], xn_t[i][:, 1:TB + 1], xn_t[i][:, 0:TB],
                        ALU.subtract, deps=d + (e_xk[-1:] if e_xk else []))
                e1_ = sch.em("v", lambda e, i=i: e.scalar_tensor_tensor(
                    S["xr"][:, i * TB:(i + 1) * TB], S["t1"][:], S["murS"][:, i:i + 1],
                    xn_t[i][:, 0:TB], ALU.mult, ALU.add), deps=[ea])
                e_xr.append(e1_)
                e2_ = sch.em("v", lambda e, i=i: e.scalar_tensor_tensor(
                    S["xk"][:, i * TB:(i + 1) * TB], S["t1"][:], S["mukS"][:, i:i + 1],
                    xn_t[i][:, 0:TB], ALU.mult, ALU.add), deps=[e1_])
                e_xk.append(e2_)
            prev_ev = [e_xnf[-1]]

            def mmblock(wname, srcs, j, extra=()):
                ev = None
                for k in range(NDT):
                    d = [srcs[k]] + list(extra) + ([prev_ev[0]] if prev_ev[0] else [])
                    ev = MM(sch, P["p"][:], wt(wname, k, j),
                            S[{"wr": "xr", "wk": "xk", "wv": "ka"}[wname]][:, k * TB:(k + 1) * TB],
                            start=(k == 0), stop=(k == NDT - 1), deps=d)
                return ev

            e_r = []
            for j in range(NDT):
                ev = mmblock("wr", e_xr, j)
                ee = ACT(sch, S["rg"][:, j * TB:(j + 1) * TB], P["p"][:], AF.Sigmoid,
                         deps=[ev])
                prev_ev[0] = ee
                e_r.append(ee)
            e_ka = []
            for j in range(NDT):
                ev = mmblock("wk", e_xk, j)
                e1_ = TS(sch, S["t1"][:], P["p"][:], 0.0, None, ALU.max,
                         deps=[ev] + (e_ka[-1:] if e_ka else []))
                e2_ = ACT(sch, S["ka"][:, j * TB:(j + 1) * TB], S["t1"][:], AF.Square,
                          deps=[e1_])
                prev_ev[0] = e2_
                e_ka.append(e2_)
            e_st = []
            for j in range(NDT):
                ev = mmblock("wv", e_ka, j, extra=e_ka)
                eo = sch.em("v", lambda e, j=j: e.tensor_tensor(
                    S["t1"][:], P["p"][:], S["rg"][:, j * TB:(j + 1) * TB], ALU.mult),
                    deps=[ev] + e_r)
                prev_ev[0] = eo
                eo2 = TT(sch, x_t[j], x_t[j], S["t1"][:], ALU.add, deps=[eo] + cast_evs)
                e_st.append(DMA(sch, "y", yT.rearrange("(a p) n -> p a n", p=128)[:, j:j + 1, sl],
                                x_t[j].rearrange("p (u n) -> p u n", u=1), deps=[eo2]))
                prev_ev[0] = eo2
            prev_blk = e_st + e_carn + [prev_ev[0]]
        sch.build(block)
    return nc, {}



def build_l3_scan():
    NBH = 8
    HD = 64
    QB = 512
    NQ = L // QB
    KB = 128
    NK = L // KB
    SCALE = 1.0 / np.sqrt(HD)
    nc = bass.Bass("TRN2", target_bir_lowering=False)
    kf = nc.dram_tensor("kf", [NBH, HD, L], BF16, kind="ExternalInput").ap()
    qf = nc.dram_tensor("qf", [NBH, HD, L], BF16, kind="ExternalInput").ap()
    vrm = nc.dram_tensor("vrm", [NBH, L, HD], BF16, kind="ExternalInput").ap()
    OUT = nc.dram_tensor("OUT", [NBH, HD, L], BF16, kind="ExternalOutput").ap()
    cs = consts(nc)
    idf = nc.inline_tensor(np.eye(128, dtype=np.float32), "c_identf").ap()
    ones64 = nc.inline_tensor(_bfa(np.ones((1, HD))), "c_ones64").ap()
    sch = Sched(nc)
    with ExitStack() as _st:
        S = {}
        for n, sh, dt in [
            ("kfA", [HD, L], BF16), ("qfA", [HD, L], BF16), ("vrA", [128, NK * HD], BF16),
            ("kfB", [HD, L], BF16), ("qfB", [HD, L], BF16), ("vrB", [128, NK * HD], BF16),
            ("vaug", [128, 4 * 66], BF16), ("vaugL", [128, 66], BF16),
            ("pt0", [128, QB], BF16), ("pt1", [128, QB], BF16),
            ("pt2", [128, QB], BF16), ("pt3", [128, QB], BF16),
            ("mrun", [128, 4], F32), ("mtmp", [128, 4], F32),
            ("mlast", [128, 4], F32), ("mg", [128, 4], F32),
            ("dexp", [128, 4], F32),
            ("mrow", [1, QB], BF16), ("erow", [1, QB], F32),
            ("slrow", [1, QB], F32), ("den", [1, QB], F32), ("denb", [1, QB], BF16),
            ("rbc", [HD, QB], F32), ("outS", [HD, QB], BF16),
            ("onesc", [128, 1], BF16), ("onesr", [1, 128], BF16),
            ("ident", [128, 128], F32), ("o64", [1, HD], BF16),
        ]:
            S[n] = _st.enter_context(nc.sbuf_tensor(n, sh, dt))
        P = {}
        for n, sh in [("s0", [128, QB]), ("s1", [128, QB]), ("o", [66, QB]),
                      ("b", [HD, QB]), ("t", [1, 128]), ("st", [128, QB]),
                      ("o2", [128, QB])]:
            P[n] = _st.enter_context(nc.psum_tensor("ps" + n, sh, F32))
        block = _st.enter_context(nc.Block())

        e_c = [DMA(sch, "y", S["onesc"][:], cs["ones_col"]),
               DMA(sch, "y", S["onesr"][:], cs["ones_row"]),
               DMA(sch, "y", S["ident"][:], idf),
               DMA(sch, "y", S["o64"][:], ones64)]
        ez1 = sch.em("v", lambda e: e.memset(S["vaug"][:], 0.0))
        ez2 = sch.em("v", lambda e: e.memset(S["vaugL"][:], 0.0), deps=[ez1])
        e_c.append(sch.em("v", lambda e: e.memset(S["vaugL"][64:128, 64:65], 1.0),
                          deps=[ez2]))

        buf = ["A", "B"]
        e_load = {"A": None, "B": None}
        e_free = {"A": list(e_c), "B": list(e_c)}
        prev_qb = []

        def load_bh(b, pb, deps):
            return [
                DMA(sch, "y", S["kf" + pb][:], kf[b], deps=deps),
                DMA(sch, "y", S["qf" + pb][:], qf[b], deps=deps),
                DMA(sch, "y", S["vr" + pb][:].rearrange("p (c n) -> p c n", n=HD),
                    vrm[b].rearrange("(c p) n -> p c n", p=128), deps=deps),
            ]

        e_load["A"] = load_bh(0, "A", e_c)
        for b in range(NBH):
            pb = buf[b % 2]
            eL = e_load[pb]
            kfb, qfb, vrb = S["kf" + pb], S["qf" + pb], S["vr" + pb]
            if b + 1 < NBH:
                e_load[buf[(b + 1) % 2]] = load_bh(b + 1, buf[(b + 1) % 2],
                                                   e_free[buf[(b + 1) % 2]])
            freed = []
            for qb in range(NQ):
                qsl = slice(qb * QB, (qb + 1) * QB)
                e_mi = sch.em("v", lambda e: e.memset(S["mrun"][:], -1e30),
                              deps=prev_qb + e_c)
                e_ml = sch.em("v", lambda e: e.memset(S["mlast"][:], -1e30),
                              deps=[e_mi])
                eSt_prev = None
                NSK = L // 512
                st_hist = {}
                stps = [P["st"], P["s0"], P["s1"], P["o2"]]
                for sk in range(NSK):
                    for qs in range(4):
                        u = (sk * 4 + qs) % 4
                        d = eL + [e_ml]
                        if (sk * 4 + qs) >= 4:
                            d = d + [st_hist[(sk * 4 + qs) - 4]]
                        if sk == 0 and qs < 2 and prev_qb:
                            d = d + [prev_qb[2 + qs]]
                        eS = MM(sch, stps[u][:], qfb[:, qb * QB + qs * 128: qb * QB + (qs + 1) * 128],
                                kfb[:, sk * 512:(sk + 1) * 512], True, True, deps=d)
                        if sk < NSK - 1:
                            er = RED(sch, S["mtmp"][:, qs:qs + 1], stps[u][:], "max",
                                     deps=[eS, eSt_prev])
                        else:
                            er = RED(sch, S["mtmp"][:, qs:qs + 1], stps[u][:, 0:448],
                                     "max", deps=[eS, eSt_prev])
                            er2 = RED(sch, S["mlast"][:, qs:qs + 1], stps[u][:, 448:512],
                                      "max", deps=[eS, er])
                        eSt_prev = TT(sch, S["mrun"][:, qs:qs + 1], S["mrun"][:, qs:qs + 1],
                                      S["mtmp"][:, qs:qs + 1], ALU.max,
                                      deps=[er] if sk < NSK - 1 else [er2])
                        st_hist[sk * 4 + qs] = eSt_prev
                e_g = TT(sch, S["mg"][:], S["mrun"][:], S["mlast"][:], ALU.max,
                         deps=[eSt_prev])
                e_d = TT(sch, S["dexp"][:], S["mrun"][:], S["mg"][:], ALU.subtract,
                         deps=[e_g])
                e_d = ACT(sch, S["dexp"][:], S["dexp"][:], AF.Exp, scale=SCALE,
                          deps=[e_d])
                e_d = TS(sch, S["dexp"][:], S["dexp"][:], 1.0, None, ALU.add, deps=[e_d])
                e_mr = []
                for qs in range(4):
                    et = TRANS(sch, P["t"][:], S["mg"][:, qs:qs + 1], S["ident"][:],
                               deps=[e_g] + (e_mr[-1:] if e_mr else []))
                    e1 = TS(sch, S["mrow"][:, qs * 128:(qs + 1) * 128], P["t"][:],
                            -1.0, None, ALU.mult, deps=[et])
                    e_mr.append(e1)
                    et2 = TRANS(sch, P["t"][:], S["dexp"][:, qs:qs + 1], S["ident"][:],
                                deps=[e_d, e1])
                    e2 = COPY(sch, S["erow"][:, qs * 128:(qs + 1) * 128], P["t"][:],
                              deps=[et2])
                    e_mr.append(e2)
                e_pv_prev = None
                e_exp = {}
                for kb in range(NK):
                    par = kb % 4
                    psS = [P["s0"], P["s1"], P["st"], P["o2"]][par]
                    ptb = [S["pt0"], S["pt1"], S["pt2"], S["pt3"]][par]
                    d = eL + e_mr
                    if kb >= 4:
                        d = d + [e_exp[kb - 4]]
                    elif prev_qb:
                        d = d + [prev_qb[2]]
                    eS = MM(sch, psS[:], kfb[:, kb * KB:(kb + 1) * KB], qfb[:, qsl],
                            True, False, deps=d)
                    eS = MM(sch, psS[:], S["onesr"][:], S["mrow"][:], False, True,
                            deps=[eS] + e_mr)
                    dd = [eS]
                    if kb >= 4:
                        dd.append(e_pv_prev)
                    elif prev_qb:
                        dd.append(prev_qb[4])
                    eE = ACT(sch, ptb[:], psS[:], AF.Exp, scale=SCALE, deps=dd)
                    e_exp[kb] = eE
                    if kb == NK - 1:
                        vt = S["vaugL"]
                        ecp = COPY(sch, S["vaugL"][:, 0:HD],
                                   vrb[:, kb * HD:(kb + 1) * HD],
                                   deps=eL + [e_exp.get(kb - 2)] + e_c)
                    else:
                        vt = S["vaug"][:, (kb % 4) * 66:(kb % 4) * 66 + 66]
                        ecp = COPY(sch, S["vaug"][:, (kb % 4) * 66:(kb % 4) * 66 + HD],
                                   vrb[:, kb * HD:(kb + 1) * HD],
                                   deps=eL + [e_exp.get(kb - 4)] + e_c +
                                        ([e_pv_prev] if kb >= 4 else (prev_qb[4:5] if prev_qb else [])))
                    dpv = [eE, ecp] + ([e_pv_prev] if e_pv_prev else [])
                    if kb == 0 and prev_qb:
                        dpv += [prev_qb[0]]
                    ePV = MM(sch, P["o"][:], vt[:] if kb == NK - 1 else vt, ptb[:],
                             start=(kb == 0), stop=(kb == NK - 1), deps=dpv)
                    e_pv_prev = ePV
                es1 = COPY(sch, S["slrow"][:], P["o"][64:65, :], deps=[e_pv_prev])
                es2 = TT(sch, S["den"][:], S["slrow"][:], S["erow"][:], ALU.mult,
                         deps=[es1])
                es3 = RECIP(sch, S["den"][:], S["den"][:], deps=[es2])
                es4 = COPY(sch, S["denb"][:], S["den"][:], deps=[es3])
                es5 = MM(sch, P["b"][:], S["o64"][:], S["denb"][:], True, True,
                         deps=[es4] + (prev_qb[5:6] if prev_qb else []))
                es6 = COPY(sch, S["rbc"][:], P["b"][:], deps=[es5] +
                           (prev_qb[6:7] if prev_qb else []))
                dpo = [es6, e_pv_prev] + (prev_qb[7:8] if prev_qb else [])
                eo = TT(sch, S["outS"][:], P["o"][0:HD, :], S["rbc"][:], ALU.mult,
                        deps=dpo)
                es = DMA(sch, "y", OUT[b][:, qsl], S["outS"][:], deps=[eo])
                freed += [es, eo]
                prev_qb = [eo, es1, e_exp.get(NK - 1), e_exp.get(NK - 1),
                           e_pv_prev, es5, es6, es, e_pv_prev]
            e_free[pb] = freed + [e_exp[NK - 1], e_exp[NK - 2], e_exp[NK - 3],
                                  e_exp[NK - 4], eSt_prev]
        sch.build(block)
    return nc, {}



def build_l3_postgate():
    TB = 512
    NB = TPC // TB
    nc = bass.Bass("TRN2", target_bir_lowering=False)
    xT = nc.dram_tensor("xT", [D, TPC], F32, kind="ExternalInput").ap()
    attT = nc.dram_tensor("attT", [D, TPC], BF16, kind="ExternalInput").ap()
    WoT = nc.dram_tensor("WoT", [D, D], F32, kind="ExternalInput").ap()
    ob = nc.dram_tensor("ob", [1, D], F32, kind="ExternalInput").ap()
    Wg = nc.dram_tensor("Wg", [D, 8], F32, kind="ExternalInput").ap()
    ebias = nc.dram_tensor("ebias", [1, 8], F32, kind="ExternalInput").ap()
    Sw1T = nc.dram_tensor("Sw1T", [D, D], F32, kind="ExternalInput").ap()
    Sw3T = nc.dram_tensor("Sw3T", [D, D], F32, kind="ExternalInput").ap()
    Sw2T = nc.dram_tensor("Sw2T", [D, D], F32, kind="ExternalInput").ap()
    yT = nc.dram_tensor("yT", [D, TPC], F32, kind="ExternalOutput").ap()
    e1o = nc.dram_tensor("e1o", [TPC, 8], BF16, kind="ExternalOutput").ap()
    e2o = nc.dram_tensor("e2o", [TPC, 8], BF16, kind="ExternalOutput").ap()
    wro = nc.dram_tensor("wro", [TPC, 2], F32, kind="ExternalOutput").ap()
    cnto = nc.dram_tensor("cnto", [1, 8], F32, kind="ExternalOutput").ap()
    xn8o = nc.dram_tensor("xn8o", [128, NDT * 8], F32, kind="ExternalOutput").ap()
    scr = nc.dram_tensor("scrb", [NDT, 128], F32, kind="Internal")
    cs = consts(nc)
    onesf_c = nc.inline_tensor(np.ones((128, 1), np.float32), "c_onesf3").ap()
    onesrf_c = nc.inline_tensor(np.ones((1, 128), np.float32), "c_onesrf3").ap()
    identf_c = nc.inline_tensor(np.eye(128, dtype=np.float32), "c_identf3").ap()
    sch = Sched(nc)
    with ExitStack() as _st:
        S = {}
        for n, sh, dt in [
            ("wo", [128, NDT * D], BF16), ("w1", [128, NDT * D], BF16),
            ("w3", [128, NDT * D], BF16), ("w2", [128, NDT * D], BF16),
            ("wg", [128, NDT * 8], F32), ("onesf", [128, 1], F32),
            ("onesrf", [1, 128], F32), ("identf", [128, 128], F32),
            ("swgr", [1, 8], F32), ("swgbc", [128, 8], F32),
            ("rcol", [128, 1], F32), ("ucol", [128, 1], F32),
            ("gatef", [128, 8], F32),
            ("x", [128, NDT * TB], F32), ("at", [128, NDT * TB], BF16),
            ("xp", [128, NDT * TB], BF16), ("xb", [128, NDT * TB], BF16),
            ("hb", [128, NDT * TB], BF16), ("sq", [128, TB], BF16),
            ("rows", [1, 6 * TB], F32), ("rowsb", [1, 2 * TB], BF16),
            ("sw", [1, 2 * D + 8], BF16), ("bcol", [128, NDT], F32),
            ("gts", [128, 8], F32), ("g4", [128, 4], F32), ("m81", [128, 1], F32),
            ("m82", [128, 1], F32), ("p8", [128, 8], F32),
            ("msk", [128, 8], F32), ("msk2", [128, 8], F32),
            ("e1", [128, 8], F32), ("e2", [128, 8], F32),
            ("eb8", [128, 8], F32), ("e12b", [128, 8], BF16),
            ("keep", [128, 4], F32), ("wout", [128, 2], F32),
            ("t1", [128, TB], F32), ("obr", [1, D], F32), ("ebb", [1, 8], BF16),
            ("cnt", [1, 8], F32),
            ("onesc", [128, 1], BF16), ("onesr", [1, 128], BF16),
        ]:
            S[n] = _st.enter_context(nc.sbuf_tensor(n, sh, dt))
        P = {}
        for n, sh in [("p", [128, TB]), ("p2", [128, TB]), ("r1", [1, TB]),
                      ("r2", [1, TB]), ("bc", [128, TB]), ("g", [128, 8]),
                      ("c", [128, 8])]:
            P[n] = _st.enter_context(nc.psum_tensor("ps" + n, sh, F32))
        block = _st.enter_context(nc.Block())

        e_c = [DMA(sch, "y", S["onesc"][:], cs["ones_col"]),
               DMA(sch, "y", S["onesr"][:], cs["ones_row"]),
               DMA(sch, "y", S["obr"][:], ob),
               DMA(sch, "g", S["ebb"][:], ebias),
               DMA(sch, "y", S["wg"][:].rearrange("p (k n) -> p k n", n=8),
                   Wg.rearrange("(k p) n -> p k n", p=128)),
               DMA(sch, "y", S["onesf"][:], onesf_c),
               DMA(sch, "y", S["onesrf"][:], onesrf_c),
               DMA(sch, "y", S["identf"][:], identf_c)]
        for nm, srcw in [("wo", WoT), ("w1", Sw1T), ("w3", Sw3T), ("w2", Sw2T)]:
            e_c.append(DMA(sch, "g",
                           S[nm][:].rearrange("p (k j n) -> p k j n", j=NDT, n=128),
                           srcw.rearrange("(k p) (j n) -> p k j n", p=128, n=128)))

        def wt(nm, k, j):
            return S[nm][:, (k * NDT + j) * 128:(k * NDT + j) * 128 + 128]

        def wgt(k):
            return S["wg"][:, k * 8:(k + 1) * 8]

        eb1 = MM(sch, P["g"][:], S["onesr"][:], S["ebb"][:], True, True, deps=e_c)
        e_eb = COPY(sch, S["eb8"][:], P["g"][:], deps=[eb1])

        e_sw = []
        prevmm = e_eb
        for wi, nm in ((0, "w1"), (1, "w3")):
            for j in range(NDT):
                ev = None
                for k in range(NDT):
                    ev = MM(sch, P["r1"][:, 0:128], S["onesc"][:], wt(nm, k, j),
                            start=(k == 0), stop=(k == NDT - 1),
                            deps=list(e_c) + [prevmm])
                prevmm = COPY(sch, S["sw"][:, wi * D + j * 128:wi * D + j * 128 + 128],
                              P["r1"][:, 0:128], deps=[ev])
                e_sw.append(prevmm)
        ev = None
        for k in range(NDT):
            ev = MM(sch, P["c"][0:1, :], S["onesf"][:], wgt(k), start=(k == 0),
                    stop=(k == NDT - 1), deps=list(e_c) + [prevmm])
        e_swg = COPY(sch, S["swgr"][:], P["c"][0:1, :], deps=[ev])
        eswb = MM(sch, P["c"][:], S["onesrf"][:], S["swgr"][:], True, True,
                  deps=[e_swg])
        e_swg = COPY(sch, S["swgbc"][:], P["c"][:], deps=[eswb])
        e_sw.append(e_swg)
        e_ob1 = DMA(sch, "y", scr.ap().rearrange("c p -> (c p)").rearrange("(u x) -> u x", u=1),
                    S["obr"][:], deps=e_c)
        e_ob2 = DMAS(sch, "y", S["bcol"][:], scr.ap().rearrange("c p -> p c"),
                     deps=[e_ob1])

        sb = {
            "sq": S["sq"][:], "r1": P["r1"][:], "r2": P["r2"][:],
            "row_m": S["rows"][:, TB:2 * TB], "row_v": S["rows"][:, 2 * TB:3 * TB],
            "row_t": S["rows"][:, 3 * TB:4 * TB], "row_r": S["rows"][:, 4 * TB:5 * TB],
            "row_uf": S["rows"][:, 5 * TB:6 * TB],
            "row_rb": S["rowsb"][:, 0:TB], "row_ub": S["rowsb"][:, TB:2 * TB],
        }
        e_cnt0 = sch.em("v", lambda e: e.memset(S["cnt"][:], 0.0), deps=e_c)
        prev_blk = sch.barrier()
        for blk in range(NB):
            sl = slice(blk * TB, (blk + 1) * TB)
            x_t = [S["x"][:, i * TB:(i + 1) * TB] for i in range(NDT)]
            at_t = [S["at"][:, i * TB:(i + 1) * TB] for i in range(NDT)]
            xp_t = [S["xp"][:, i * TB:(i + 1) * TB] for i in range(NDT)]
            xb_t = [S["xb"][:, i * TB:(i + 1) * TB] for i in range(NDT)]
            hb_t = [S["hb"][:, i * TB:(i + 1) * TB] for i in range(NDT)]
            e_x = DMA(sch, "y", S["x"][:].rearrange("p (a n) -> p a n", n=TB),
                      xT.rearrange("(a p) n -> p a n", p=128)[:, :, sl], deps=prev_blk)
            e_a = DMA(sch, "y", S["at"][:].rearrange("p (a n) -> p a n", n=TB),
                      attT.rearrange("(a p) n -> p a n", p=128)[:, :, sl], deps=prev_blk)
            prev_ev = [None]
            e_xn = []
            for j in range(NDT):
                ev = None
                for k in range(NDT):
                    d = [e_a, e_x] + ([prev_ev[0]] if prev_ev[0] else []) + prev_blk
                    ev = MM(sch, P["p"][:], wt("wo", k, j), at_t[k], start=(k == 0),
                            stop=(k == NDT - 1), deps=d)
                eo = sch.em("v", lambda e, j=j: e.scalar_tensor_tensor(
                    x_t[j], P["p"][:], S["bcol"][:, j:j + 1], x_t[j], ALU.add, ALU.add),
                    deps=[ev, e_ob2])
                prev_ev[0] = eo
                e_xn.append(eo)
            ev_rows, cast_evs = emit_ln_stats(sch, sb, x_t, xp_t, TB, S["onesc"][:],
                                              e_xn)
            e_rbc = MM(sch, P["bc"][:], S["onesr"][:], sb["row_rb"], True, True,
                       deps=ev_rows + [prev_ev[0]])
            e_xb = [TT(sch, xb_t[i], x_t[i], P["bc"][:], ALU.mult, deps=[e_rbc])
                    for i in range(NDT)]
            if blk == 0:
                eu8 = MM(sch, P["p2"][:, 0:8], S["onesr"][:], sb["row_ub"][:, 0:8],
                         True, True, deps=ev_rows)
                exn8 = []
                for i in range(NDT):
                    ea = TT(sch, S["t1"][:, 0:8], x_t[i][:, 0:8], P["bc"][:, 0:8],
                            ALU.mult, deps=[e_rbc, e_xn[i]] + exn8[-1:])
                    eb_ = TT(sch, S["t1"][:, 0:8], S["t1"][:, 0:8], P["p2"][:, 0:8],
                             ALU.add, deps=[ea, eu8])
                    exn8.append(DMA(sch, "y", xn8o[:, i * 8:(i + 1) * 8],
                                    S["t1"][:, 0:8], deps=[eb_]))
            for sub in range(TB // 128):
                tsl = slice(sub * 128, (sub + 1) * 128)
                gsl = slice(blk * TB + sub * 128, blk * TB + (sub + 1) * 128)
                ev = None
                for k in range(NDT):
                    d = [e_xn[k]] + ([prev_ev[0]] if prev_ev[0] else [])
                    ev = MM(sch, P["g"][:], x_t[k][:, tsl], wgt(k), start=(k == 0),
                            stop=(k == NDT - 1), deps=d)
                etr = sch.em("t", lambda e, tsl=tsl: e.transpose(
                    P["c"][:, 0:1], sb["row_r"][:, tsl], S["identf"][0:1, 0:1]),
                    deps=ev_rows + [ev])
                etc = COPY(sch, S["rcol"][:], P["c"][:, 0:1], deps=[etr])
                etr2 = sch.em("t", lambda e, tsl=tsl: e.transpose(
                    P["c"][:, 0:1], sb["row_uf"][:, tsl], S["identf"][0:1, 0:1]),
                    deps=[etc])
                etc2 = COPY(sch, S["ucol"][:], P["c"][:, 0:1], deps=[etr2])
                eg1 = sch.em("v", lambda e: e.tensor_scalar(
                    S["gatef"][:], P["g"][:], S["rcol"][:, 0:1], None, ALU.mult),
                    deps=[ev, etc])
                eg2 = sch.em("v", lambda e: e.scalar_tensor_tensor(
                    S["gatef"][:], S["swgbc"][:], S["ucol"][:, 0:1], S["gatef"][:],
                    ALU.mult, ALU.add), deps=[eg1, etc2, e_swg])
                em1 = RED(sch, S["m81"][:], S["gatef"][:], "max", deps=[eg2],
                          negate=True)
                ee = ACT(sch, S["p8"][:], S["gatef"][:], AF.Exp, bias=S["m81"][:, 0:1],
                         deps=[em1])
                es_ = RED(sch, S["m82"][:], S["p8"][:], "sum", deps=[ee])
                er_ = RECIP(sch, S["m82"][:], S["m82"][:], deps=[es_])
                ep = TS(sch, S["p8"][:], S["p8"][:], S["m82"][:, 0:1], None, ALU.mult,
                        deps=[er_])
                eg = TT(sch, S["gts"][:], S["p8"][:], S["eb8"][:], ALU.add,
                        deps=[ep, e_eb])
                eg4 = RED(sch, S["g4"][:].rearrange("p (j u) -> p j u", u=1),
                          S["gts"][:].rearrange("p (j i) -> p j i", i=2), "sum",
                          deps=[eg])
                egm = RED(sch, S["m81"][:], S["g4"][:], "max", deps=[eg4])
                et1 = TS(sch, S["keep"][:], S["g4"][:], S["m81"][:, 0:1], None,
                         ALU.is_ge, deps=[egm])
                eg2 = STT(sch, S["msk"][:, 0:4], S["keep"][:], -1e30, S["g4"][:],
                          ALU.mult, ALU.add, deps=[et1])
                egm2 = RED(sch, S["m82"][:], S["msk"][:, 0:4], "max", deps=[eg2])
                ekp = TS(sch, S["keep"][:], S["g4"][:], S["m82"][:, 0:1], None,
                         ALU.is_ge, deps=[egm2, et1])
                ekx = COPY(sch, S["msk"][:].rearrange("p (j i) -> p j i", i=2),
                           S["keep"][:].rearrange("p (j u) -> p j u", u=1)
                           .broadcast_to([128, 4, 2]), deps=[ekp])
                emk = TS(sch, S["msk"][:], S["msk"][:], 1e30, -1e30, ALU.mult, ALU.add,
                         deps=[ekx])
                emk2 = TT(sch, S["msk"][:], S["msk"][:], S["gts"][:], ALU.add,
                          deps=[emk])
                em1_ = RED(sch, S["m81"][:], S["msk"][:], "max", deps=[emk2])
                ee1 = TS(sch, S["e1"][:], S["msk"][:], S["m81"][:, 0:1], None,
                         ALU.is_ge, deps=[em1_])
                em2_ = STT(sch, S["msk2"][:], S["e1"][:], -1e30, S["msk"][:], ALU.mult,
                           ALU.add, deps=[ee1])
                em2m = RED(sch, S["m82"][:], S["msk2"][:], "max", deps=[em2_])
                ee2 = TS(sch, S["e2"][:], S["msk2"][:], S["m82"][:, 0:1], None,
                         ALU.is_ge, deps=[em2m])
                ew1 = TT(sch, S["msk2"][:], S["p8"][:], S["e1"][:], ALU.mult,
                         deps=[ee1, ee2])
                ew1r = RED(sch, S["wout"][:, 0:1], S["msk2"][:], "sum", deps=[ew1])
                ew2 = TT(sch, S["msk2"][:], S["p8"][:], S["e2"][:], ALU.mult,
                         deps=[ew1r, ee2])
                ew2r = RED(sch, S["wout"][:, 1:2], S["msk2"][:], "sum", deps=[ew2])
                ec1 = TT(sch, S["msk2"][:], S["e1"][:], S["e2"][:], ALU.add,
                         deps=[ew2r, ee1, ee2])
                ec2 = COPY(sch, S["e12b"][:], S["msk2"][:], deps=[ec1])
                ec3 = MM(sch, P["c"][0:1, :], S["onesc"][:], S["e12b"][:], True, True,
                         deps=[ec2, e_cnt0, etc2] + ([prev_ev[0]] if prev_ev[0] else []))
                e_cnt0 = TT(sch, S["cnt"][:], S["cnt"][:], P["c"][0:1, :], ALU.add,
                            deps=[ec3])
                ee1b = COPY(sch, S["e12b"][:], S["e1"][:], deps=[e_cnt0])
                es1 = DMA(sch, "y", e1o[gsl, :], S["e12b"][:], deps=[ee1b])
                ee2b = COPY(sch, S["e12b"][:], S["e2"][:], deps=[es1])
                es2 = DMA(sch, "y", e2o[gsl, :], S["e12b"][:], deps=[ee2b])
                es3 = DMA(sch, "y", wro[gsl, :], S["wout"][:], deps=[ew2r, ee2b])
                prev_ev[0] = es3
            for j in range(NDT):
                ev = None
                for k in range(NDT):
                    d = [e_xb[k]] + ([prev_ev[0]] if prev_ev[0] else [])
                    if blk == 0 and j == 0 and k == 0:
                        d = d + exn8
                    ev = MM(sch, P["p"][:], wt("w1", k, j), xb_t[k], start=(k == 0),
                            stop=False, deps=d)
                ev = MM(sch, P["p"][:], S["sw"][:, j * 128:j * 128 + 128],
                        sb["row_ub"], False, True, deps=[ev] + ev_rows + [e_sw[-1]])
                ev2 = None
                for k in range(NDT):
                    d2 = [e_xb[k], ev]
                    if blk == 0 and j == 0 and k == 0:
                        d2 = d2 + exn8
                    ev2 = MM(sch, P["p2"][:], wt("w3", k, j), xb_t[k], start=(k == 0),
                             stop=False, deps=d2)
                ev2 = MM(sch, P["p2"][:], S["sw"][:, D + j * 128:D + j * 128 + 128],
                         sb["row_ub"], False, True, deps=[ev2] + ev_rows)
                et = ACT(sch, S["t1"][:], P["p"][:], AF.Silu, deps=[ev])
                eh = TT(sch, hb_t[j], S["t1"][:], P["p2"][:], ALU.mult, deps=[et, ev2])
                prev_ev[0] = eh
            e_st = []
            for j in range(NDT):
                ev = None
                for k in range(NDT):
                    d = [prev_ev[0]]
                    ev = MM(sch, P["p"][:], wt("w2", k, j), hb_t[k], start=(k == 0),
                            stop=(k == NDT - 1), deps=d)
                eo2 = TT(sch, x_t[j], x_t[j], P["p"][:], ALU.add, deps=[ev] + cast_evs)
                prev_ev[0] = eo2
                e_st.append(DMA(sch, "y", yT.rearrange("(a p) n -> p a n", p=128)[:, j:j + 1, sl],
                                x_t[j].rearrange("p (u n) -> p u n", u=1), deps=[eo2]))
            prev_blk = e_st + [prev_ev[0]]
        DMA(sch, "y", cnto, S["cnt"][:], deps=[e_cnt0])
        sch.build(block)
    return nc, {}



TTT_C = 64
TTT_R = 4 * TTT_C
TTT_NCH = TPC // TTT_R
TTT_LR = 0.001
TTT_CC = 2.0 / (4 * 1024)


def build_ttt_sweep():
    R = TTT_R
    NCH = TTT_NCH
    nc = bass.Bass("TRN2", target_bir_lowering=False)
    Kfm = nc.dram_tensor("Kfm", [D, TPC], BF16, kind="ExternalInput").ap()
    Krm = nc.dram_tensor("Krm", [TPC, D], BF16, kind="ExternalInput").ap()
    cVrm = nc.dram_tensor("cVrm", [TPC, D], BF16, kind="ExternalInput").ap()
    W10 = nc.dram_tensor("W10", [D, D], F32, kind="ExternalInput").ap()
    W20 = nc.dram_tensor("W20", [D, D], F32, kind="ExternalInput").ap()
    W2T0 = nc.dram_tensor("W2T0", [D, D], F32, kind="ExternalInput").ap()
    b10 = nc.dram_tensor("b10", [1, D], F32, kind="ExternalInput").ap()
    b20 = nc.dram_tensor("b20", [1, D], F32, kind="ExternalInput").ap()
    dS1p = nc.dram_tensor("dS1p", [7, D, D], F32, kind="ExternalInput").ap()
    dS2p = nc.dram_tensor("dS2p", [7, D, D], F32, kind="ExternalInput").ap()
    dS2Tp = nc.dram_tensor("dS2Tp", [7, D, D], F32, kind="ExternalInput").ap()
    db1p = nc.dram_tensor("db1p", [7, 1, D], F32, kind="ExternalInput").ap()
    db2p = nc.dram_tensor("db2p", [7, 1, D], F32, kind="ExternalInput").ap()
    DZ1p = nc.dram_tensor("DZ1p", [TPC, D], BF16, kind="ExternalInput").ap()
    Epi = nc.dram_tensor("Epi", [TPC, D], BF16, kind="ExternalInput").ap()
    S1o = nc.dram_tensor("S1o", [D, D], F32, kind="ExternalOutput").ap()
    S2o = nc.dram_tensor("S2o", [D, D], F32, kind="ExternalOutput").ap()
    S2To = nc.dram_tensor("S2To", [D, D], F32, kind="ExternalOutput").ap()
    b1o = nc.dram_tensor("b1o", [1, D], F32, kind="ExternalOutput").ap()
    b2o = nc.dram_tensor("b2o", [1, D], F32, kind="ExternalOutput").ap()
    DZ1o = nc.dram_tensor("DZ1o", [TPC, D], BF16, kind="ExternalOutput").ap()
    Eo = nc.dram_tensor("Eo", [TPC, D], BF16, kind="ExternalOutput").ap()
    w1s = nc.dram_tensor("w1s", [D, D], F32, kind="Internal")
    w2s = nc.dram_tensor("w2s", [D, D], F32, kind="Internal")
    w2ts = nc.dram_tensor("w2ts", [D, D], F32, kind="Internal")
    b1s = nc.dram_tensor("b1s", [1, D], F32, kind="Internal")
    b2s = nc.dram_tensor("b2s", [1, D], F32, kind="Internal")
    cs = consts(nc)
    step = np.arange(R) // 4
    msk = (step[:, None] < step[None, :]).astype(np.float32)
    cm0 = nc.inline_tensor(_bfa(msk[0:128]), "c_msku0").ap()
    cm1 = nc.inline_tensor(_bfa(msk[128:256]), "c_msku1").ap()
    idb = nc.inline_tensor(_bfa(np.eye(128)), "c_idb").ap()

    sch = Sched(nc)
    with ExitStack() as _st:
        S = {}
        for n, sh, dt in [
            ("w1c", [128, NDT * D], F32), ("w2c", [128, NDT * D], F32),
            ("w2tc", [128, NDT * D], F32),
            ("w1b", [128, NDT * D], BF16), ("w2b", [128, NDT * D], BF16),
            ("w2tb", [128, NDT * D], BF16),
            ("b1c", [1, D], F32), ("b2c", [1, D], F32),
            ("b1b", [1, D], BF16), ("b2b", [1, D], BF16),
            ("kf", [128, NDT * R], BF16), ("krm", [128, 2 * D], BF16),
            ("cv", [128, 2 * D], BF16),
            ("dzp", [128, 2 * D], BF16), ("ep", [128, 2 * D], BF16),
            ("h1", [128, 2 * D], BF16), ("ds", [128, 2 * D], BF16),
            ("er", [128, 2 * D], BF16),
            ("h1f", [128, NDT * R], BF16), ("ef", [128, NDT * R], BF16),
            ("g1", [128, 2 * R], BF16), ("g2", [128, 2 * R], BF16),
            ("g3", [128, 2 * R], BF16),
            ("m0", [128, R], BF16), ("m1", [128, R], BF16),
            ("gsc", [128, R], F32), ("stg", [128, D], F32), ("ident", [128, 128], BF16),
            ("onesc", [128, 1], BF16), ("onesr", [1, 128], BF16),
        ]:
            S[n] = _st.enter_context(nc.sbuf_tensor(n, sh, dt))
        P = {}
        for n, sh, dt in [("a", [128, 512], F32), ("b", [128, 512], F32),
                          ("g", [128, R], F32), ("t", [128, 128], BF16),
                          ("r", [1, 512], F32)]:
            P[n] = _st.enter_context(nc.psum_tensor("ps" + n, sh, dt))
        block = _st.enter_context(nc.Block())

        e_c = [DMA(sch, "y", S["onesc"][:], cs["ones_col"]),
               DMA(sch, "y", S["onesr"][:], cs["ones_row"]),
               DMA(sch, "y", S["m0"][:], cm0), DMA(sch, "y", S["m1"][:], cm1),
               DMA(sch, "y", S["ident"][:], idb)]

        stgv = None

        def minit(wc, w0, dsp, save):
            ev0 = DMA(sch, "y", S[wc][:].rearrange("p (k n) -> p k n", n=D),
                      w0.rearrange("(k p) n -> p k n", p=128), deps=e_c)
            last = ev0
            for i in range(7):
                for k in range(NDT):
                    el = DMA(sch, "y", S["stg"][:], dsp[i][k * 128:(k + 1) * 128, :],
                             deps=[last])
                    last = TT(sch, S[wc][:, k * D:(k + 1) * D],
                              S[wc][:, k * D:(k + 1) * D], S["stg"][:],
                              ALU.subtract, deps=[el, ev0])
            es = DMA(sch, "y", save.ap().rearrange("(k p) n -> p k n", p=128),
                     S[wc][:].rearrange("p (k n) -> p k n", n=D), deps=[last])
            return [last, es]

        def binit(bc, b0, dbp, save, after):
            ev0 = DMA(sch, "y", S[bc][:], b0, deps=e_c + after)
            last = ev0
            for i in range(7):
                el = DMA(sch, "y", S["stg"][0:1, 0:D], dbp[i], deps=[last])
                last = TT(sch, S[bc][:], S[bc][:], S["stg"][0:1, 0:D], ALU.subtract,
                          deps=[el, ev0])
            es = DMA(sch, "y", save.ap(), S[bc][:], deps=[last])
            return [last, es]

        e_m1 = minit("w1c", W10, dS1p, w1s)
        e_m2 = minit("w2c", W20, dS2p, w2s)
        e_m3 = minit("w2tc", W2T0, dS2Tp, w2ts)
        e_b1 = binit("b1c", b10, db1p, b1s, e_m1 + e_m2 + e_m3)
        e_b2 = binit("b2c", b20, db2p, b2s, e_b1)
        e_init = e_m1 + e_m2 + e_m3 + e_b1 + e_b2

        def kft(k):
            return S["kf"][:, k * R:(k + 1) * R]

        prev_ch = sch.barrier()
        for ch in range(NCH):
            rsl = slice(ch * R, (ch + 1) * R)
            e_cast = [COPY(sch, S["w1b"][:], S["w1c"][:], deps=prev_ch + e_init),
                      COPY(sch, S["w2b"][:], S["w2c"][:], deps=prev_ch + e_init),
                      COPY(sch, S["w2tb"][:], S["w2tc"][:], deps=prev_ch + e_init),
                      COPY(sch, S["b1b"][:], S["b1c"][:], deps=prev_ch + e_init),
                      COPY(sch, S["b2b"][:], S["b2c"][:], deps=prev_ch + e_init)]
            e_ld = [
                DMA(sch, "y", S["kf"][:].rearrange("p (k n) -> p k n", n=R),
                    Kfm.rearrange("(k p) n -> p k n", p=128)[:, :, rsl], deps=prev_ch),
                DMA(sch, "y", S["krm"][:].rearrange("p (t n) -> p t n", n=D),
                    Krm[rsl, :].rearrange("(t p) n -> p t n", p=128), deps=prev_ch),
                DMA(sch, "y", S["cv"][:].rearrange("p (t n) -> p t n", n=D),
                    cVrm[rsl, :].rearrange("(t p) n -> p t n", p=128), deps=prev_ch),
                DMA(sch, "y", S["dzp"][:].rearrange("p (t n) -> p t n", n=D),
                    DZ1p[rsl, :].rearrange("(t p) n -> p t n", p=128), deps=prev_ch),
                DMA(sch, "y", S["ep"][:].rearrange("p (t n) -> p t n", n=D),
                    Epi[rsl, :].rearrange("(t p) n -> p t n", p=128), deps=prev_ch),
            ]

            def gram(srcf, dst, plus1, e_src):
                evs = []
                for st in range(2):
                    ev = None
                    for k in range(NDT):
                        d = list(e_src) + (evs[-1:] if evs else list(prev_ch))
                        ev = MM(sch, P["g"][:], srcf(k)[:, st * 128:(st + 1) * 128],
                                srcf(k), start=(k == 0), stop=(k == NDT - 1), deps=d)
                    if plus1:
                        e1 = sch.em("v", lambda e, st=st, dst=dst: e.tensor_scalar(
                            S["gsc"][:], P["g"][:], 1.0, -TTT_LR, ALU.add,
                            ALU.mult), deps=[ev])
                    else:
                        e1 = TS(sch, S["gsc"][:], P["g"][:], -TTT_LR, None,
                                ALU.mult, deps=[ev])
                    e2 = TT(sch, S[dst][:, st * R:(st + 1) * R], S["gsc"][:],
                            S["m0" if st == 0 else "m1"][:], ALU.mult, deps=[e1])
                    evs.append(e2)
                return evs

            e_g1 = gram(kft, "g1", True, e_ld)

            e_h = []
            for rt in range(2):
                for jh in range(2):
                    ps = P["a"] if (rt * 2 + jh) % 2 == 0 else P["b"]
                    ev = None
                    for k in range(NDT):
                        d = [e_ld[0], e_cast[0]] + (e_h[-1:] if len(e_h) >= 2 else list(prev_ch))
                        ev = MM(sch, ps[:], kft(k)[:, rt * 128:(rt + 1) * 128],
                                S["w1b"][:, k * D + jh * 512:k * D + jh * 512 + 512],
                                start=(k == 0), stop=False, deps=d)
                    ev = MM(sch, ps[:], S["onesr"][:],
                            S["b1b"][:, jh * 512:jh * 512 + 512], False, False,
                            deps=[ev, e_cast[3]])
                    ev = MM(sch, ps[:], S["g1"][:, rt * 128:rt * 128 + 0 + 128] if False else S["g1"][:, 0 * R + rt * 128: 0 * R + rt * 128 + 128],
                            S["dzp"][:, 0 * D + jh * 512: 0 * D + jh * 512 + 512],
                            False, False, deps=[ev, e_g1[0], e_ld[3]])
                    ev = MM(sch, ps[:], S["g1"][:, 1 * R + rt * 128: 1 * R + rt * 128 + 128],
                            S["dzp"][:, 1 * D + jh * 512: 1 * D + jh * 512 + 512],
                            False, True, deps=[ev, e_g1[1]])
                    co = rt * D + jh * 512
                    eh = ACT(sch, S["h1"][:, co:co + 512], ps[:], AF.Silu, deps=[ev])
                    eg0 = ACT(sch, S["stg"][:, 0:512], ps[:], AF.Sigmoid, deps=[eh])
                    eg1 = TS(sch, S["stg"][:, 512:1024], S["stg"][:, 0:512], -1.0, 1.0,
                             ALU.mult, ALU.add, deps=[eg0])
                    eg2 = TT(sch, S["stg"][:, 512:1024], S["stg"][:, 512:1024], ps[:],
                             ALU.mult, deps=[eg1])
                    eg3 = TS(sch, S["stg"][:, 512:1024], S["stg"][:, 512:1024], 1.0,
                             None, ALU.add, deps=[eg2])
                    ed = TT(sch, S["ds"][:, co:co + 512], S["stg"][:, 0:512],
                            S["stg"][:, 512:1024], ALU.mult, deps=[eg3])
                    e_h += [eh, ed]

            e_tr = []
            for t in range(2):
                for j in range(NDT):
                    et = TRANS(sch, P["t"][:], S["h1"][:, t * D + j * 128:t * D + (j + 1) * 128],
                               S["ident"][:], deps=e_h + (e_tr[-1:] if e_tr else []))
                    e_tr.append(COPY(sch, S["h1f"][:, j * R + t * 128:j * R + (t + 1) * 128],
                                     P["t"][:], deps=[et]))

            def h1ft(j):
                return S["h1f"][:, j * R:(j + 1) * R]

            def eft(j):
                return S["ef"][:, j * R:(j + 1) * R]

            e_g2 = gram(h1ft, "g2", True, e_tr)

            e_e = []
            for rt in range(2):
                for jh in range(2):
                    ps = P["a"] if (rt * 2 + jh) % 2 == 0 else P["b"]
                    ev = None
                    for k in range(NDT):
                        d = [e_tr[-1], e_cast[1]] + (e_e[-1:] if len(e_e) >= 1 else [])
                        ev = MM(sch, ps[:], h1ft(k)[:, rt * 128:(rt + 1) * 128],
                                S["w2b"][:, k * D + jh * 512:k * D + jh * 512 + 512],
                                start=(k == 0), stop=False, deps=d)
                    ev = MM(sch, ps[:], S["onesr"][:],
                            S["b2b"][:, jh * 512:jh * 512 + 512], False, False,
                            deps=[ev, e_cast[4]])
                    ev = MM(sch, ps[:], S["g2"][:, 0 * R + rt * 128: 0 * R + rt * 128 + 128],
                            S["ep"][:, 0 * D + jh * 512: 0 * D + jh * 512 + 512],
                            False, False, deps=[ev, e_g2[0], e_ld[4]])
                    ev = MM(sch, ps[:], S["g2"][:, 1 * R + rt * 128: 1 * R + rt * 128 + 128],
                            S["ep"][:, 1 * D + jh * 512: 1 * D + jh * 512 + 512],
                            False, True, deps=[ev, e_g2[1]])
                    co = rt * D + jh * 512
                    ee = sch.em("v", lambda e, co=co, ps=ps: e.scalar_tensor_tensor(
                        S["er"][:, co:co + 512], ps[:], TTT_CC, S["cv"][:, co:co + 512],
                        ALU.mult, ALU.subtract), deps=[ev, e_ld[2]])
                    e_e.append(ee)

            e_tre = []
            for t in range(2):
                for j in range(NDT):
                    et = TRANS(sch, P["t"][:], S["er"][:, t * D + j * 128:t * D + (j + 1) * 128],
                               S["ident"][:], deps=e_e + (e_tre[-1:] if e_tre else []))
                    e_tre.append(COPY(sch, S["ef"][:, j * R + t * 128:j * R + (t + 1) * 128],
                                      P["t"][:], deps=[et]))

            e_g3 = gram(eft, "g3", False, e_tre)

            e_dz = []
            for rt in range(2):
                for jh in range(2):
                    ps = P["a"] if (rt * 2 + jh) % 2 == 0 else P["b"]
                    ev = None
                    for k in range(NDT):
                        d = [e_tre[-1], e_cast[2]] + (e_dz[-1:] if e_dz else [])
                        ev = MM(sch, ps[:], eft(k)[:, rt * 128:(rt + 1) * 128],
                                S["w2tb"][:, k * D + jh * 512:k * D + jh * 512 + 512],
                                start=(k == 0), stop=False, deps=d)
                    ev = MM(sch, ps[:], S["g3"][:, 0 * R + rt * 128: 0 * R + rt * 128 + 128],
                            S["h1"][:, 0 * D + jh * 512: 0 * D + jh * 512 + 512],
                            False, False, deps=[ev, e_g3[0]])
                    ev = MM(sch, ps[:], S["g3"][:, 1 * R + rt * 128: 1 * R + rt * 128 + 128],
                            S["h1"][:, 1 * D + jh * 512: 1 * D + jh * 512 + 512],
                            False, True, deps=[ev, e_g3[1]])
                    co = rt * D + jh * 512
                    ez = TT(sch, S["dzp"][:, co:co + 512], ps[:], S["ds"][:, co:co + 512],
                            ALU.mult, deps=[ev])
                    e_dz.append(ez)

            e_st = [
                DMA(sch, "y", DZ1o[rsl, :].rearrange("(t p) n -> p t n", p=128),
                    S["dzp"][:].rearrange("p (t n) -> p t n", n=D), deps=e_dz),
                DMA(sch, "y", Eo[rsl, :].rearrange("(t p) n -> p t n", p=128),
                    S["er"][:].rearrange("p (t n) -> p t n", n=D), deps=e_dz + e_tre),
            ]

            e_up = []

            def upd(lhs_tiles, rhs_buf, master, extra):
                evs = []
                for dt_ in range(NDT):
                    for jh in range(2):
                        ps = P["a"] if (dt_ + jh) % 2 == 0 else P["b"]
                        ev = None
                        for t in range(2):
                            d = list(extra) + (evs[-2:] if len(evs) >= 2 else list(extra))
                            ev = MM(sch, ps[:], lhs_tiles(t)[:, dt_ * 128:(dt_ + 1) * 128],
                                    rhs_buf[:, t * D + jh * 512:t * D + jh * 512 + 512],
                                    start=(t == 0), stop=(t == 1), deps=d)
                        eu = sch.em("v", lambda e, ps=ps, dt_=dt_, jh=jh, master=master:
                                    e.scalar_tensor_tensor(
                                        S[master][:, dt_ * D + jh * 512:dt_ * D + jh * 512 + 512],
                                        ps[:], -TTT_LR,
                                        S[master][:, dt_ * D + jh * 512:dt_ * D + jh * 512 + 512],
                                        ALU.mult, ALU.add), deps=[ev])
                        evs.append(eu)
                return evs

            e_up += upd(lambda t: S["krm"][:, t * D:(t + 1) * D], S["dzp"], "w1c",
                        e_dz + [e_ld[1]])
            e_up += upd(lambda t: S["h1"][:, t * D:(t + 1) * D], S["er"], "w2c",
                        e_dz + e_e + e_up[-2:])
            e_up += upd(lambda t: S["er"][:, t * D:(t + 1) * D], S["h1"], "w2tc",
                        e_dz + e_e + e_up[-2:])
            for (buf, bc) in (("dzp", "b1c"), ("er", "b2c")):
                for jh in range(2):
                    ev = None
                    for t in range(2):
                        d = e_dz + e_up[-2:]
                        ev = MM(sch, P["r"][:], S["onesc"][:],
                                S[buf][:, t * D + jh * 512:t * D + jh * 512 + 512],
                                start=(t == 0), stop=(t == 1), deps=d)
                    e_up.append(sch.em("v", lambda e, bc=bc, jh=jh: e.scalar_tensor_tensor(
                        S[bc][:, jh * 512:jh * 512 + 512], P["r"][:], -TTT_LR,
                        S[bc][:, jh * 512:jh * 512 + 512], ALU.mult, ALU.add),
                        deps=[ev]))
            prev_ch = e_up + e_st

        e_fin = prev_ch
        for (sv, mc, out) in ((w1s, "w1c", S1o), (w2s, "w2c", S2o), (w2ts, "w2tc", S2To)):
            for k in range(NDT):
                el = DMA(sch, "y", S["stg"][:], sv.ap()[k * 128:(k + 1) * 128, :],
                         deps=e_fin)
                ed = TT(sch, S["stg"][:], S["stg"][:],
                        S[mc][:, k * D:(k + 1) * D], ALU.subtract, deps=[el])
                e_fin = [DMA(sch, "y", out[k * 128:(k + 1) * 128, :], S["stg"][:],
                             deps=[ed]), ed]
        for (sv, mc, out) in ((b1s, "b1c", b1o), (b2s, "b2c", b2o)):
            el = DMA(sch, "y", S["stg"][0:1, 0:D], sv.ap(), deps=e_fin)
            ed = TT(sch, S["stg"][0:1, 0:D], S["stg"][0:1, 0:D], S[mc][:], ALU.subtract, deps=[el])
            e_fin = [DMA(sch, "y", out, S["stg"][0:1, 0:D], deps=[ed]), ed]
        sch.build(block)
    return nc, {}



def build_l1_final():
    TB = 512
    NB = TPC // TB
    nc = bass.Bass("TRN2", target_bir_lowering=False)
    xT = nc.dram_tensor("xT", [D, TPC], F32, kind="ExternalInput").ap()
    qT = nc.dram_tensor("qT", [D, TPC], BF16, kind="ExternalInput").ap()
    W10 = nc.dram_tensor("W10", [D, D], F32, kind="ExternalInput").ap()
    W20 = nc.dram_tensor("W20", [D, D], F32, kind="ExternalInput").ap()
    b10 = nc.dram_tensor("b10", [1, D], F32, kind="ExternalInput").ap()
    b20 = nc.dram_tensor("b20", [1, D], F32, kind="ExternalInput").ap()
    dS1a = nc.dram_tensor("dS1a", [8, D, D], F32, kind="ExternalInput").ap()
    dS2a = nc.dram_tensor("dS2a", [8, D, D], F32, kind="ExternalInput").ap()
    db1a = nc.dram_tensor("db1a", [8, 1, D], F32, kind="ExternalInput").ap()
    db2a = nc.dram_tensor("db2a", [8, 1, D], F32, kind="ExternalInput").ap()
    yT = nc.dram_tensor("yT", [D, TPC], F32, kind="ExternalOutput").ap()
    scr1 = nc.dram_tensor("scr1f", [NDT, 128], F32, kind="Internal")
    scr2 = nc.dram_tensor("scr2f", [NDT, 128], F32, kind="Internal")
    cs = consts(nc)
    sch = Sched(nc)
    with ExitStack() as _st:
        S = {}
        for n, sh, dt in [
            ("w1c", [128, NDT * D], F32), ("w2c", [128, NDT * D], F32),
            ("w1b", [128, NDT * D], BF16), ("w2b", [128, NDT * D], BF16),
            ("b1c", [1, D], F32), ("b2c", [1, D], F32),
            ("b1col", [128, NDT], F32), ("b2col", [128, NDT], F32),
            ("stg", [128, D], F32),
            ("x", [128, NDT * TB], F32), ("q", [128, NDT * TB], BF16),
            ("hb", [128, NDT * TB], BF16),
            ("onesr", [1, 128], BF16),
        ]:
            S[n] = _st.enter_context(nc.sbuf_tensor(n, sh, dt))
        P = {"a": _st.enter_context(nc.psum_tensor("psa", [128, TB], F32))}
        block = _st.enter_context(nc.Block())
        e_c = [DMA(sch, "y", S["onesr"][:], cs["ones_row"])]

        def minit(wc, w0, dsp):
            ev0 = DMA(sch, "y", S[wc][:].rearrange("p (k n) -> p k n", n=D),
                      w0.rearrange("(k p) n -> p k n", p=128), deps=e_c)
            last = ev0
            for i in range(8):
                for k in range(NDT):
                    el = DMA(sch, "y", S["stg"][:], dsp[i][k * 128:(k + 1) * 128, :],
                             deps=[last])
                    last = TT(sch, S[wc][:, k * D:(k + 1) * D],
                              S[wc][:, k * D:(k + 1) * D], S["stg"][:], ALU.subtract,
                              deps=[el, ev0])
            return last

        e1 = minit("w1c", W10, dS1a)
        e2 = minit("w2c", W20, dS2a)
        eb1 = DMA(sch, "y", S["b1c"][:], b10, deps=e_c)
        eb2 = DMA(sch, "y", S["b2c"][:], b20, deps=e_c)
        for i in range(8):
            el = DMA(sch, "y", S["stg"][0:1, 0:D], db1a[i], deps=[eb1, e2])
            eb1 = TT(sch, S["b1c"][:], S["b1c"][:], S["stg"][0:1, 0:D], ALU.subtract,
                     deps=[el])
            el2 = DMA(sch, "y", S["stg"][0:1, 0:D], db2a[i], deps=[eb1, eb2])
            eb2 = TT(sch, S["b2c"][:], S["b2c"][:], S["stg"][0:1, 0:D], ALU.subtract,
                     deps=[el2])
        ec1 = DMA(sch, "y", scr1.ap().rearrange("a n -> (a n)").rearrange("(u x) -> u x", u=1),
                  S["b1c"][:], deps=[eb1, eb2])
        ec2 = DMAS(sch, "y", S["b1col"][:], scr1.ap().rearrange("c p -> p c"), deps=[ec1])
        ec3 = DMA(sch, "y", scr2.ap().rearrange("a n -> (a n)").rearrange("(u x) -> u x", u=1),
                  S["b2c"][:], deps=[eb2])
        ec4 = DMAS(sch, "y", S["b2col"][:], scr2.ap().rearrange("c p -> p c"), deps=[ec3])
        e_cast = [COPY(sch, S["w1b"][:], S["w1c"][:], deps=[e1]),
                  COPY(sch, S["w2b"][:], S["w2c"][:], deps=[e2]), ec2, ec4]
        prev_blk = sch.barrier()
        for blk in range(NB):
            sl = slice(blk * TB, (blk + 1) * TB)
            e_x = DMA(sch, "y", S["x"][:].rearrange("p (a n) -> p a n", n=TB),
                      xT.rearrange("(a p) n -> p a n", p=128)[:, :, sl], deps=prev_blk)
            e_q = DMA(sch, "y", S["q"][:].rearrange("p (a n) -> p a n", n=TB),
                      qT.rearrange("(a p) n -> p a n", p=128)[:, :, sl], deps=prev_blk)
            prev = [None]
            e_h = []
            for j in range(NDT):
                ev = None
                for k in range(NDT):
                    d = [e_q] + e_cast + ([prev[0]] if prev[0] else [])
                    ev = MM(sch, P["a"][:], S["w1b"][:, (k * NDT + j) * 128:(k * NDT + j) * 128 + 128],
                            S["q"][:, k * TB:(k + 1) * TB], start=(k == 0),
                            stop=(k == NDT - 1), deps=d)
                ev = sch.em("t", lambda e, ev=ev: e.matmul(
                    P["a"][:], S["w1b"][:, 0:128], S["q"][:, 0:TB], start=False,
                    stop=True, skip_group_check=True), deps=[ev]) if False else ev
                eh = ACT(sch, S["hb"][:, j * TB:(j + 1) * TB], P["a"][:], AF.Silu,
                         bias=S["b1col"][:, j:j + 1], deps=[ev] + e_cast)
                prev[0] = eh
                e_h.append(eh)
            e_st = []
            for j in range(NDT):
                ev = None
                for k in range(NDT):
                    d = [e_h[k], prev[0]]
                    ev = MM(sch, P["a"][:], S["w2b"][:, (k * NDT + j) * 128:(k * NDT + j) * 128 + 128],
                            S["hb"][:, k * TB:(k + 1) * TB], start=(k == 0),
                            stop=(k == NDT - 1), deps=d)
                eo = sch.em("v", lambda e, j=j: e.scalar_tensor_tensor(
                    S["x"][:, j * TB:(j + 1) * TB], P["a"][:], S["b2col"][:, j:j + 1],
                    S["x"][:, j * TB:(j + 1) * TB], ALU.add, ALU.add),
                    deps=[ev, e_x] + e_cast)
                prev[0] = eo
                e_st.append(DMA(sch, "y", yT.rearrange("(a p) n -> p a n", p=128)[:, j:j + 1, sl],
                                S["x"][:, j * TB:(j + 1) * TB].rearrange("p (u n) -> p u n", u=1),
                                deps=[eo]))
            prev_blk = e_st + [prev[0]]
        sch.build(block)
    return nc, {}



def build_l1_ffn():
    TB = 256
    NB = TPC // TB
    nc = bass.Bass("TRN2", target_bir_lowering=False)
    xT = nc.dram_tensor("xT", [D, TPC], F32, kind="ExternalInput").ap()
    Wg = nc.dram_tensor("Wg", [D, 8], F32, kind="ExternalInput").ap()
    gb = nc.dram_tensor("gb", [1, 8], F32, kind="ExternalInput").ap()
    We = nc.dram_tensor("We", [8, D, D], F32, kind="ExternalInput").ap()
    be = nc.dram_tensor("be", [8, D], F32, kind="ExternalInput").ap()
    yT = nc.dram_tensor("yT", [D, TPC], F32, kind="ExternalOutput").ap()
    cs = consts(nc)
    sel = []
    for e in range(8):
        m = np.zeros((8, 128), np.float32)
        m[e, :] = 1.0
        sel.append(nc.inline_tensor(_bfa(m), f"c_sel{e}").ap())
    sch = Sched(nc)
    with ExitStack() as _st:
        S = {}
        for n, sh, dt in [
            ("we", [128, 8 * NDT * D], BF16),
            ("x", [128, NDT * TB], F32), ("xp", [128, NDT * TB], BF16),
            ("xb", [128, NDT * TB], BF16), ("sq", [128, TB], BF16),
            ("acc", [128, NDT * TB], F32),
            ("rows", [1, 6 * TB], F32), ("rowsb", [1, 2 * TB], BF16),
            ("sw", [1, 8 * D + 8], BF16),
            ("wgs", [128, NDT * 8], BF16), ("gbb", [1, 8], BF16),
            ("gb8", [128, 8], F32), ("bes", [8, D], BF16),
            ("p8", [128, 8], F32), ("m81", [128, 1], F32), ("m82", [128, 1], F32),
            ("msk", [128, 8], F32), ("wg8", [128, 8], BF16),
            ("w8", [8, TB], BF16), ("selS", [8, 8 * 128], BF16),
            ("webc", [128, TB], F32), ("webc2", [128, TB], F32),
            ("t1", [128, TB], F32), ("t12", [128, TB], F32),
            ("idb", [128, 128], BF16),
            ("onesc", [128, 1], BF16), ("onesr", [1, 128], BF16),
        ]:
            S[n] = _st.enter_context(nc.sbuf_tensor(n, sh, dt))
        P = {}
        for n, sh in [("a", [128, TB]), ("a2", [128, TB]), ("b", [128, TB]),
                      ("r1", [1, TB]), ("r2", [1, TB]), ("g", [128, 8])]:
            P[n] = _st.enter_context(nc.psum_tensor("ps" + n, sh, F32))
        P["w8"] = _st.enter_context(nc.psum_tensor("psw8", [8, 128], BF16))
        block = _st.enter_context(nc.Block())
        e_c = [DMA(sch, "y", S["onesc"][:], cs["ones_col"]),
               DMA(sch, "y", S["onesr"][:], cs["ones_row"]),
               DMA(sch, "y", S["idb"][:], cs["ident"]),
               DMA(sch, "g", S["gbb"][:], gb),
               DMA(sch, "g", S["bes"][:], be),
               DMA(sch, "g", S["wgs"][:].rearrange("p (k n) -> p k n", n=8),
                   Wg.rearrange("(k p) n -> p k n", p=128))]
        for e in range(8):
            e_c.append(DMA(sch, "y", S["selS"][:, e * 128:(e + 1) * 128], sel[e]))
            e_c.append(DMA(sch, "g",
                           S["we"][:, e * NDT * D:(e + 1) * NDT * D]
                           .rearrange("p (k j n) -> p k j n", j=NDT, n=128),
                           We[e].rearrange("(k p) (j n) -> p k j n", p=128, n=128)))

        def wt(e, k, j):
            o = e * NDT * D + (k * NDT + j) * 128
            return S["we"][:, o:o + 128]

        eg1 = MM(sch, P["g"][:], S["onesr"][:], S["gbb"][:], True, True, deps=e_c)
        e_gb8 = COPY(sch, S["gb8"][:], P["g"][:], deps=[eg1])
        e_sw = []
        prevmm = e_gb8
        for e in range(8):
            for j in range(NDT):
                ev = None
                for k in range(NDT):
                    ev = MM(sch, P["r1"][:, 0:128], S["onesc"][:], wt(e, k, j),
                            start=(k == 0), stop=(k == NDT - 1),
                            deps=list(e_c) + [prevmm])
                prevmm = COPY(sch, S["sw"][:, e * D + j * 128:e * D + j * 128 + 128],
                              P["r1"][:, 0:128], deps=[ev])
                e_sw.append(prevmm)
        ev = None
        for k in range(NDT):
            ev = MM(sch, P["r1"][:, 0:8], S["onesc"][:], S["wgs"][:, k * 8:(k + 1) * 8],
                    start=(k == 0), stop=(k == NDT - 1), deps=list(e_c) + [prevmm])
        e_swg = COPY(sch, S["sw"][:, 8 * D:8 * D + 8], P["r1"][:, 0:8], deps=[ev])

        sb = {
            "sq": S["sq"][:], "r1": P["r1"][:], "r2": P["r2"][:],
            "row_m": S["rows"][:, TB:2 * TB], "row_v": S["rows"][:, 2 * TB:3 * TB],
            "row_t": S["rows"][:, 3 * TB:4 * TB], "row_r": S["rows"][:, 4 * TB:5 * TB],
            "row_uf": S["rows"][:, 5 * TB:6 * TB],
            "row_rb": S["rowsb"][:, 0:TB], "row_ub": S["rowsb"][:, TB:2 * TB],
        }
        prev_blk = sch.barrier()
        for blk in range(NB):
            sl = slice(blk * TB, (blk + 1) * TB)
            x_t = [S["x"][:, i * TB:(i + 1) * TB] for i in range(NDT)]
            xp_t = [S["xp"][:, i * TB:(i + 1) * TB] for i in range(NDT)]
            xb_t = [S["xb"][:, i * TB:(i + 1) * TB] for i in range(NDT)]
            e_x = DMA(sch, "y", S["x"][:].rearrange("p (a n) -> p a n", n=TB),
                      xT.rearrange("(a p) n -> p a n", p=128)[:, :, sl], deps=prev_blk)
            ev_rows, cast_evs = emit_ln_stats(sch, sb, x_t, xp_t, TB, S["onesc"][:],
                                              [e_x] + prev_blk)
            e_rbc = MM(sch, P["a"][:], S["onesr"][:], sb["row_rb"], True, True,
                       deps=ev_rows + prev_blk)
            e_xb = [TT(sch, xb_t[i], x_t[i], P["a"][:], ALU.mult, deps=[e_rbc])
                    for i in range(NDT)]
            prev_ev = [e_xb[-1]]
            e_w8 = []
            for sub in range(TB // 128):
                tsl = slice(sub * 128, (sub + 1) * 128)
                ev = None
                for k in range(NDT):
                    d = [e_xb[k], prev_ev[0]]
                    ev = MM(sch, P["g"][:], xb_t[k][:, tsl], S["wgs"][:, k * 8:(k + 1) * 8],
                            start=(k == 0), stop=False, deps=d)
                ev = MM(sch, P["g"][:], sb["row_ub"][:, tsl], S["sw"][:, 8 * D:8 * D + 8],
                        False, True, deps=[ev] + ev_rows + [e_swg])
                et = sch.em("v", lambda e: e.tensor_tensor(
                    S["msk"][:], P["g"][:], S["gb8"][:], ALU.add), deps=[ev, e_gb8])
                em1 = RED(sch, S["m81"][:], S["msk"][:], "max", deps=[et], negate=True)
                ee = ACT(sch, S["p8"][:], S["msk"][:], AF.Exp, bias=S["m81"][:, 0:1],
                         deps=[em1])
                es_ = RED(sch, S["m82"][:], S["p8"][:], "sum", deps=[ee])
                er_ = RECIP(sch, S["m82"][:], S["m82"][:], deps=[es_])
                ep = TS(sch, S["p8"][:], S["p8"][:], S["m82"][:, 0:1], None, ALU.mult,
                        deps=[er_])
                ems = TS(sch, S["msk"][:], S["p8"][:], 0.1, None, ALU.is_ge, deps=[ep])
                ew = TT(sch, S["msk"][:], S["msk"][:], S["p8"][:], ALU.mult, deps=[ems])
                esm = RED(sch, S["m81"][:], S["msk"][:], "sum", deps=[ew])
                eis = TS(sch, S["m82"][:], S["m81"][:], 0.0, None, ALU.is_equal,
                         deps=[esm])
                eadd = TT(sch, S["m81"][:], S["m81"][:], S["m82"][:], ALU.add,
                          deps=[eis])
                erc = RECIP(sch, S["m81"][:], S["m81"][:], deps=[eadd])
                ewf = TS(sch, S["wg8"][:], S["msk"][:], S["m81"][:, 0:1], None,
                         ALU.mult, deps=[erc])
                etr = sch.em("t", lambda e: e.transpose(P["w8"][:], S["wg8"][:],
                                                        S["idb"][:]),
                             deps=[ewf] + e_w8[-1:])
                e_w8.append(COPY(sch, S["w8"][:, tsl], P["w8"][:].bitcast(BF16)
                                 if False else P["w8"][:], deps=[etr]))
                prev_ev[0] = e_w8[-1]
            e_z = []
            for i in range(NDT):
                e_z.append(sch.em("v", lambda e, i=i: e.memset(
                    S["acc"][:, i * TB:(i + 1) * TB], 0.0),
                    deps=prev_blk + [e_w8[-1], e_z[-1] if e_z else e_w8[-1]]))
            prev2 = [e_z[-1]]
            hist = []
            webc_hist = [None, None]
            for e in range(8):
                wpar = e % 2
                wb_t = S["webc"][:, 0:TB] if wpar == 0 else S["webc2"][:, 0:TB]
                dwb = e_w8 + ([webc_hist[wpar]] if webc_hist[wpar] else [])
                ebc = MM(sch, P["b"][:], S["selS"][:, e * 128:(e + 1) * 128],
                         S["w8"][:], True, True,
                         deps=dwb + ([ebcp, hist[-1]] if e > 0 else []))
                ebcp = ebc
                ewb = COPY(sch, wb_t, P["b"][:], deps=[ebc])
                for j in range(NDT):
                    gi = e * NDT + j
                    par = gi % 2
                    ps = P["a"] if par == 0 else P["a2"]
                    t1t = S["t1"][:] if par == 0 else S["t12"][:]
                    ev = None
                    for k in range(NDT):
                        d = [e_xb[k]]
                        if k == 0 and len(hist) >= 2:
                            d = d + [hist[-2]]
                        if k == 0 and len(hist) < 2:
                            d = d + [e_z[-1]]
                        ev = MM(sch, ps[:], wt(e, k, j), xb_t[k], start=(k == 0),
                                stop=False, deps=d)
                    ev = MM(sch, ps[:], S["sw"][:, e * D + j * 128:e * D + j * 128 + 128],
                            sb["row_ub"], False, True, deps=[ev] + ev_rows)
                    e1_ = TT(sch, t1t, ps[:], wb_t, ALU.mult,
                             deps=[ev, ewb] + (hist[-2:-1] if len(hist) >= 2 else []))
                    e2_ = sch.em("v", lambda e_, j=j, t1t=t1t: e_.tensor_tensor(
                        S["acc"][:, j * TB:(j + 1) * TB],
                        S["acc"][:, j * TB:(j + 1) * TB], t1t, ALU.add),
                        deps=[e1_, e_z[j]])
                    hist.append(e2_)
                webc_hist[wpar] = hist[-1]
                prev2[0] = hist[-1]
            e_st = []
            for j in range(NDT):
                ev = MM(sch, P["a"][:], S["bes"][:, j * 128:(j + 1) * 128], S["w8"][:],
                        True, True, deps=e_w8 + [prev2[0]] + list(e_c))
                e1_ = TT(sch, S["acc"][:, j * TB:(j + 1) * TB],
                         S["acc"][:, j * TB:(j + 1) * TB], P["a"][:], ALU.add,
                         deps=[ev, prev2[0]])
                e2_ = TT(sch, S["x"][:, j * TB:(j + 1) * TB],
                         S["x"][:, j * TB:(j + 1) * TB],
                         S["acc"][:, j * TB:(j + 1) * TB], ALU.add,
                         deps=[e1_] + cast_evs)
                prev2[0] = e2_
                e_st.append(DMA(sch, "y", yT.rearrange("(a p) n -> p a n", p=128)[:, j:j + 1, sl],
                                S["x"][:, j * TB:(j + 1) * TB].rearrange("p (u n) -> p u n", u=1),
                                deps=[e2_]))
            prev_blk = e_st + [prev2[0]]
        sch.build(block)
    return nc, {}



def build_l3_routed():
    NT = 2 * TPC // 128
    nc = bass.Bass("TRN2", target_bir_lowering=False)
    cnts = nc.dram_tensor("cnts", [8, 1, 8], F32, kind="ExternalInput").ap()
    srcmT = nc.dram_tensor("srcmT", [8, 2 * TPC], BF16, kind="ExternalInput").ap()
    wcol = nc.dram_tensor("wcol", [128, NT], F32, kind="ExternalInput").ap()
    xn8 = nc.dram_tensor("xn8", [128, NDT * 8], F32, kind="ExternalInput").ap()
    w1r = nc.dram_tensor("w1r", [8, D, 512], F32, kind="ExternalInput").ap()
    w3r = nc.dram_tensor("w3r", [8, D, 512], F32, kind="ExternalInput").ap()
    Hpart = nc.dram_tensor("Hpart", [64, 512], F32, kind="ExternalOutput").ap()
    apd = nc.dram_tensor("apd", [64, 512], BF16, kind="Internal")
    bpd = nc.dram_tensor("bpd", [64, 512], BF16, kind="Internal")
    cs = consts(nc)
    iot = nc.dram_tensor("iot", [8, 128], F32, kind="ExternalInput").ap()
    L8 = nc.inline_tensor(np.triu(np.ones((8, 8), np.float32)), "c_L8").ap()
    onesf = nc.inline_tensor(np.ones((128, 1), np.float32), "c_onesf").ap()
    E8m = np.zeros((8, 64), np.float32)
    T8m = np.zeros((8, 64), np.float32)
    for g in range(8):
        for e in range(8):
            E8m[g, g * 8 + e] = 1.0
            T8m[e, g * 8 + e] = 1.0
    E8 = nc.inline_tensor(_bfa(E8m), "c_E8").ap()
    T8 = nc.inline_tensor(_bfa(T8m), "c_T8").ap()
    sch = Sched(nc)
    with ExitStack() as _st:
        S = {}
        for n, sh, dt in [
            ("xnb", [128, NDT * 8], BF16), ("wstg", [128, 4 * 512], BF16),
            ("ap8", [8, 512], F32), ("apS", [64, 512], BF16), ("bpS", [64, 512], BF16),
            ("cntS", [8, 8], F32), ("ctot", [1, 8], F32), ("onescf", [128, 1], F32),
            ("offs", [8, 1], F32), ("prevs", [8, 1], F32), ("cnt1", [8, 1], F32),
            ("iotS", [8, 128], F32), ("iota8", [8, 128], F32),
            ("gA", [8, 128], F32), ("gB", [8, 128], F32), ("gmT", [8, 128], BF16),
            ("smT", [8, 128], BF16),
            ("p8t", [64, 128], BF16), ("p8r", [128, 64], BF16),
            ("wcS", [128, 2 * TPC // 128], F32), ("t1", [128, 512], F32), ("t2", [128, 512], F32),
            ("t3", [128, 512], F32), ("hrow", [128, 512], BF16),
            ("L8S", [8, 8], F32), ("E8S", [8, 64], BF16), ("T8S", [8, 64], BF16),
            ("scr8", [1, 8], F32),
            ("onesc", [128, 1], BF16), ("onesr", [1, 128], BF16),
        ]:
            S[n] = _st.enter_context(nc.sbuf_tensor(n, sh, dt))
        P = {}
        for n, sh in [("a", [128, 512]), ("b", [128, 512]), ("h", [64, 512]),
                      ("x", [64, 128]), ("y", [128, 64]), ("o", [8, 512]),
                      ("s", [8, 1])]:
            P[n] = _st.enter_context(nc.psum_tensor("ps" + n, sh, F32))
        block = _st.enter_context(nc.Block())
        e_c = [DMA(sch, "y", S["onesc"][:], cs["ones_col"]),
               DMA(sch, "y", S["onesr"][:], cs["ones_row"]),
               DMA(sch, "y", S["iotS"][:], iot),
               DMA(sch, "y", S["L8S"][:], L8),
               DMA(sch, "y", S["E8S"][:], E8), DMA(sch, "y", S["T8S"][:], T8),
               DMA(sch, "y", S["cntS"][:].rearrange("p (u n) -> p u n", u=1),
                   cnts),
               DMA(sch, "g", S["xnb"][:], xn8),
               DMA(sch, "y", S["onescf"][:], onesf),
               DMA(sch, "y", S["wcS"][:], wcol)]
        ec1 = MM(sch, P["s"][:, 0:1] if False else P["o"][:, 0:8], S["onesc"][0:8, :]
                 if False else S["onesc"][:, :], S["cntS"][:], True, True, deps=e_c)
        sch.steps["t"].pop()
        sch.count["t"] -= 1
        ec1 = MM(sch, P["o"][0:1, 0:8], S["onescf"][0:8, 0:1], S["cntS"][:], True, True,
                 deps=e_c)
        ec2 = COPY(sch, S["ctot"][:], P["o"][0:1, 0:8], deps=[ec1])
        eb1 = DMA(sch, "y", S["scr8"][:], S["ctot"][:], deps=[ec2]) if False else None
        scrd = nc.dram_tensor("scrd", [1, 8], F32, kind="Internal")
        eb1 = DMA(sch, "y", scrd.ap(), S["ctot"][:], deps=[ec2])
        eb2 = DMA(sch, "y", S["cnt1"][:], scrd.ap().rearrange("u n -> n u"), deps=[eb1])
        ep1 = MM(sch, P["s"][:], S["L8S"][:], S["cnt1"][:], True, True, deps=[eb2])
        ep2 = COPY(sch, S["offs"][:], P["s"][:], deps=[ep1])
        ep3 = sch.em("v", lambda e: e.tensor_tensor(
            S["prevs"][:], S["offs"][:], S["cntS"][:, 0:1] if False else S["offs"][:],
            ALU.subtract), deps=[ep2]) if False else None
        ecc = COPY(sch, S["prevs"][:], S["cnt1"][:], deps=[eb2])
        ep3 = TT(sch, S["prevs"][:], S["offs"][:], S["prevs"][:], ALU.subtract,
                 deps=[ep2, ecc])
        for (wsrc, dstd) in ((w1r, apd), (w3r, bpd)):
            prevw = [e_c[-1]]
            for g in range(8):
                el = DMA(sch, "g", S["wstg"][:].rearrange("p (k n) -> p k n", n=512)
                         [:, 0:NDT // 2, :], wsrc[g].rearrange("(k p) n -> p k n", p=128)
                         [:, 0:NDT // 2, :], deps=[prevw[0]])
                el2 = DMA(sch, "g", S["wstg"][:].rearrange("p (k n) -> p k n", n=512)
                          [:, NDT // 2:NDT, :] if False else S["wstg"][:, 4 * 512:4 * 512]
                          , None) if False else None
                ev = None
                for k in range(NDT // 2):
                    d = [el, prevw[0]]
                    ev = MM(sch, P["o"][:], S["xnb"][:, k * 8:(k + 1) * 8],
                            S["wstg"][:, k * 512:(k + 1) * 512], start=(k == 0),
                            stop=False, deps=d)
                el2 = DMA(sch, "g", S["wstg"][:].rearrange("p (k n) -> p k n", n=512),
                          wsrc[g].rearrange("(k p) n -> p k n", p=128)[:, NDT // 2:NDT, :],
                          deps=[ev])
                for k in range(NDT // 2):
                    d = [el2]
                    ev = MM(sch, P["o"][:], S["xnb"][:, (4 + k) * 8:(5 + k) * 8],
                            S["wstg"][:, k * 512:(k + 1) * 512], start=False,
                            stop=(k == NDT // 2 - 1), deps=d)
                ee = COPY(sch, S["ap8"][:], P["o"][:], deps=[ev])
                prevw[0] = DMA(sch, "g", dstd.ap()[g * 8:(g + 1) * 8, :], S["ap8"][:],
                               deps=[ee])
            sch.barrier()
        eap = DMA(sch, "y", S["apS"][:], apd.ap(), deps=sch.barrier())
        ebp = DMA(sch, "y", S["bpS"][:], bpd.ap(), deps=sch.barrier())

        eH_open = None
        prev_t = sch.barrier()
        for t in range(NT):
            tb = t * 128
            ei = TS(sch, S["iota8"][:], S["iotS"][:], float(tb), None,
                    ALU.add, deps=prev_t + e_c)
            ega = sch.em("v", lambda e: e.tensor_scalar(
                S["gA"][:], S["iota8"][:], S["prevs"][:, 0:1], None, ALU.is_ge),
                deps=[ei, ep3])
            egb = sch.em("v", lambda e: e.tensor_scalar(
                S["gB"][:], S["iota8"][:], S["offs"][:, 0:1], None, ALU.is_lt),
                deps=[ega, ep2])
            egm = TT(sch, S["gmT"][:], S["gA"][:], S["gB"][:], ALU.mult, deps=[egb])
            esm = COPY(sch, S["smT"][:], srcmT[:, tb:tb + 128], deps=prev_t) if False else \
                DMA(sch, "y", S["smT"][:], srcmT[:, tb:tb + 128], deps=prev_t)
            em1 = MM(sch, P["x"][:], S["E8S"][:], S["gmT"][:], True, True,
                     deps=[egm] + prev_t)
            em2 = MM(sch, P["y"][:] if False else P["x"][:], None, None, True, True) if False else None
            ee1 = COPY(sch, S["p8t"][:], P["x"][:], deps=[em1])
            em2 = MM(sch, P["x"][:], S["T8S"][:], S["smT"][:], True, True,
                     deps=[ee1, esm])
            ee2 = TT(sch, S["p8t"][:], S["p8t"][:], P["x"][:], ALU.mult, deps=[em2])
            em3 = MM(sch, P["y"][:], S["gmT"][:], S["E8S"][:], True, True,
                     deps=[egm] + prev_t + [ee2])
            ee3 = COPY(sch, S["p8r"][:], P["y"][:], deps=[em3])
            em4 = MM(sch, P["y"][:], S["smT"][:], S["T8S"][:], True, True,
                     deps=[ee3, esm])
            ee4 = TT(sch, S["p8r"][:], S["p8r"][:], P["y"][:], ALU.mult, deps=[em4])
            ea = MM(sch, P["a"][:], S["p8t"][:], S["apS"][:], True, True,
                    deps=[ee2, eap] + prev_t)
            eb = MM(sch, P["b"][:], S["p8t"][:], S["bpS"][:], True, True,
                    deps=[ee2, ebp, ea] + prev_t)
            e1 = sch.em("v", lambda e, t=t: e.tensor_scalar(
                S["t1"][:], P["a"][:], S["wcS"][:, t:t + 1], None, ALU.mult),
                deps=[ea] + e_c)
            e2 = ACT(sch, S["t2"][:], S["t1"][:], AF.Silu, deps=[e1])
            e3 = sch.em("v", lambda e, t=t: e.tensor_scalar(
                S["t3"][:], P["b"][:], S["wcS"][:, t:t + 1], None, ALU.mult),
                deps=[eb, e2])
            e4 = TT(sch, S["hrow"][:], S["t2"][:], S["t3"][:], ALU.mult, deps=[e3])
            eH = MM(sch, P["h"][:], S["p8r"][:], S["hrow"][:], start=(t == 0),
                    stop=(t == NT - 1), deps=[e4, ee4] +
                    ([eH_open] if eH_open else []))
            eH_open = eH
            prev_t = [e4, ee4, eH, ea, eb, egm]
        efin = COPY(sch, S["t1"][0:64, 0:512], P["h"][:], deps=[eH_open])
        DMA(sch, "y", Hpart, S["t1"][0:64, 0:512], deps=[efin])
        sch.build(block)
    return nc, {}


def build_l3_final():
    nc = bass.Bass("TRN2", target_bir_lowering=False)
    xT = nc.dram_tensor("xT", [D, TPC], F32, kind="ExternalInput").ap()
    Hparts = nc.dram_tensor("Hparts", [8, 64, 512], F32, kind="ExternalInput").ap()
    w2r = nc.dram_tensor("w2r", [8, 512, D], F32, kind="ExternalInput").ap()
    mski = nc.dram_tensor("mski", [1, 1], F32, kind="ExternalInput").ap()
    yT = nc.dram_tensor("yT", [D, TPC], F32, kind="ExternalOutput").ap()
    cs = consts(nc)
    sch = Sched(nc)
    with ExitStack() as _st:
        S = {}
        for n, sh, dt in [
            ("x", [128, NDT * TPC // 1], F32) if False else ("x", [128, NDT * TPC], F32),
            ("h64", [64, 512], F32), ("hstg", [64, 512], F32),
            ("h64t", [128, 4 * 64], BF16),
            ("w2", [128, 4 * D], BF16), ("o8", [8, D], F32), ("o8b", [128, 8], F32),
            ("ident", [128, 128], BF16), ("mcol", [128, 1], F32),
            ("mrow", [1, 1], BF16),
            ("onesc", [128, 1], BF16), ("onesr", [1, 128], BF16),
        ]:
            S[n] = _st.enter_context(nc.sbuf_tensor(n, sh, dt))
        P = {}
        for n, sh, dt in [("a", [8, 512], F32), ("t", [128, 64], F32),
                          ("tb", [128, 64], BF16), ("m", [128, 1], F32)]:
            P[n] = _st.enter_context(nc.psum_tensor("ps" + n, sh, dt))
        block = _st.enter_context(nc.Block())
        e_c = [DMA(sch, "y", S["onesc"][:], cs["ones_col"]),
               DMA(sch, "y", S["onesr"][:], cs["ones_row"]),
               DMA(sch, "y", S["ident"][:], cs["ident"]),
               DMA(sch, "g", S["mrow"][:], mski)]
        e_x = DMA(sch, "y", S["x"][:].rearrange("p (a n) -> p a n", n=TPC),
                  xT.rearrange("(a p) n -> p a n", p=128), deps=[])
        em1 = MM(sch, P["m"][:], S["onesr"][:], S["mrow"][:], True, True, deps=e_c)
        em2 = COPY(sch, S["mcol"][:], P["m"][:], deps=[em1])
        eh = DMA(sch, "y", S["h64"][:], Hparts[0], deps=e_c)
        for i in range(1, 8):
            el = DMA(sch, "y", S["hstg"][:], Hparts[i], deps=[eh])
            eh = TT(sch, S["h64"][:], S["h64"][:], S["hstg"][:], ALU.add, deps=[el])
        ehb = COPY(sch, S["hstg"][:].bitcast(BF16)[:, 0:512] if False else S["h64t"][:, 0:0 + 0] if False else S["hstg"][:], S["h64"][:], deps=[eh]) if False else None
        hb16 = _st.enter_context(nc.sbuf_tensor("hb16", [64, 512], BF16))
        ehb = COPY(sch, hb16[:], S["h64"][:], deps=[eh])
        e_tr = []
        for c4 in range(4):
            et = sch.em("t", lambda e, c4=c4: e.transpose(
                P["tb"][:, 0:64], hb16[:, c4 * 128:(c4 + 1) * 128], S["ident"][0:64, 0:64]),
                deps=[ehb] + e_tr[-1:])
            e_tr.append(COPY(sch, S["h64t"][:, c4 * 64:(c4 + 1) * 64], P["tb"][:, 0:64],
                             deps=[et]))
        e_o = []
        for half in range(2):
            ev = None
            for g in range(8):
                el = DMA(sch, "g", S["w2"][:].rearrange("p (k n) -> p k n", n=D)
                         [:, :, half * 512:(half + 1) * 512],
                         w2r[g].rearrange("(k p) n -> p k n", p=128)
                         [:, :, half * 512:(half + 1) * 512], deps=e_tr + e_o[-1:] +
                         ([ev] if ev else []))
                for kt in range(4):
                    ev = MM(sch, P["a"][:], S["h64t"][:, kt * 64 + 0:kt * 64 + 64]
                            [:, g * 8 - g * 8:8] if False else
                            S["h64t"][:, kt * 64:(kt + 1) * 64][:, 0:8] if False else
                            S["h64t"][:, kt * 64 + 0:kt * 64 + 64],
                            None, True, True) if False else ev
                    ev = MM(sch, P["a"][:],
                            S["h64t"][:, kt * 64 + g * 8:kt * 64 + g * 8 + 8],
                            S["w2"][:].rearrange("p (k n) -> p k n", n=D)
                            [:, kt, half * 512:half * 512 + 512],
                            start=(g == 0 and kt == 0), stop=(g == 7 and kt == 3),
                            deps=[el] + ([ev] if ev else []))
            e_o.append(COPY(sch, S["o8"][:, half * 512:(half + 1) * 512], P["a"][:],
                            deps=[ev]))
        e_fin = []
        o8b16 = _st.enter_context(nc.sbuf_tensor("o8b16", [8, D], BF16))
        eo_b = COPY(sch, o8b16[:], S["o8"][:], deps=e_o)
        for i in range(NDT):
            et = sch.em("t", lambda e, i=i: e.transpose(
                P["tb"][:, 0:8], o8b16[:, i * 128:(i + 1) * 128], S["ident"][0:8, 0:8]),
                deps=[eo_b] + e_fin[-1:])
            e1 = sch.em("v", lambda e, i=i: e.tensor_scalar(
                S["o8b"][:, 0:8], P["tb"][:, 0:8], S["mcol"][:, 0:1], None, ALU.mult),
                deps=[et])
            e2 = TT(sch, S["x"][:, i * TPC:i * TPC + 8], S["x"][:, i * TPC:i * TPC + 8],
                    S["o8b"][:, 0:8], ALU.add, deps=[e1, e_x, em2])
            e_fin.append(e2)
        e_st = DMA(sch, "y", yT.rearrange("(a p) n -> p a n", p=128),
                   S["x"][:].rearrange("p (a n) -> p a n", n=TPC), deps=e_fin + [e_x])
        sch.build(block)
    return nc, {}



def _get(name, builder):
    if name not in _progs:
        _progs[name] = builder()
    return _progs[name][0]


_progs = {}


def _col(v):
    return np.ascontiguousarray(np.asarray(v, np.float32).reshape(-1, 128).T)


def kernel(**inp):
    cores = list(range(8))

    def RUN(nc, ims):
        return run_bass_kernel_spmd(nc, ims, core_ids=cores).results

    def shT(a):
        return [np.ascontiguousarray(np.asarray(a)[c * TPC:(c + 1) * TPC].T)
                for c in cores]

    def shTb(a):
        return [np.ascontiguousarray(np.asarray(a)[c * TPC:(c + 1) * TPC].T)
                for c in cores]

    def unT(rs, key):
        return np.concatenate([np.asarray(r[key]).T for r in rs], 0)

    x = _f32(inp["L0_attn_x"]).reshape(TOK, D)

    nc = _get("l0p", build_l0_proj)
    rs = RUN(nc, [{"xT": s, "Wq": _f32(inp["L0_attn_Wq"]), "Wk": _f32(inp["L0_attn_Wk"]),
                   "Wv": _f32(inp["L0_attn_Wv"]), "Wb": _f32(inp["L0_attn_Wb"])}
                  for s in shT(x)])
    q0 = unT(rs, "qT"); k0 = unT(rs, "kT"); v0 = unT(rs, "vT")
    beta0 = np.concatenate([np.asarray(r["betaT"]) for r in rs], 1)

    nc = _get("l0s", build_l0_scan)
    ims = []
    for c in cores:
        kf, qf, krm, vrm, bc, br = [], [], [], [], [], []
        for i in range(4):
            bh = c * 4 + i
            b, h = bh // 8, bh % 8
            sl = slice(b * L, (b + 1) * L)
            hs = slice(h * 128, (h + 1) * 128)
            kf.append(k0[sl, hs].T); qf.append(q0[sl, hs].T)
            krm.append(k0[sl, hs]); vrm.append(v0[sl, hs])
            brow = _f32(beta0[h, b * L:(b + 1) * L])
            bc.append(brow.reshape(64, 128).T)
            br.append(brow.reshape(1, L))
        ims.append({"kf": np.ascontiguousarray(np.stack(kf)),
                    "qf": np.ascontiguousarray(np.stack(qf)),
                    "krm": np.ascontiguousarray(np.stack(krm)),
                    "vrm": np.ascontiguousarray(np.stack(vrm)),
                    "bcol": np.ascontiguousarray(np.stack(bc)),
                    "brow": np.ascontiguousarray(np.stack(br))})
    rs = RUN(nc, ims)
    o0 = np.zeros((TOK, D), bfdt)
    for c in cores:
        for i in range(4):
            bh = c * 4 + i
            b, h = bh // 8, bh % 8
            o0[b * L:(b + 1) * L, h * 128:(h + 1) * 128] = np.asarray(rs[c]["O"])[i]

    nc = _get("l0pf", build_l0_postffn)
    rs = RUN(nc, [{"xT": sx, "oT": so,
                   "lnw": _col(inp["L0_attn_ln_w"]), "lnb": _col(inp["L0_attn_ln_b"]),
                   "W1": _f32(inp["L0_ffn_W1"]), "b1c": _col(inp["L0_ffn_b1"]),
                   "thc": _col(inp["L0_ffn_quantum_phases"]),
                   "wrc": _col(inp["L0_ffn_quantum_weights_real"]),
                   "wic": _col(inp["L0_ffn_quantum_weights_imag"]),
                   "W2": _f32(inp["L0_ffn_W2"]), "b2c": _col(inp["L0_ffn_b2"])}
                  for sx, so in zip(shT(x), shTb(o0))])
    x = unT(rs, "yT").astype(np.float32)

    W_KV = _f32(inp["L1_attn_W_KV"])
    spec1 = {"projs": [("k1", D, "copy"), ("cv", D, "scale"), ("q1", D, "copy")],
             "biases": set(), "scale": {"cv": TTT_CC}}
    nc = _get("l1p", lambda: build_proj(spec1, "l1p"))
    rs = RUN(nc, [{"xT": s, "W_k1": np.ascontiguousarray(W_KV[:, :D]),
                   "W_cv": np.ascontiguousarray(W_KV[:, D:]),
                   "W_q1": _f32(inp["L1_attn_W_Q"])} for s in shT(x)])
    k1 = unT(rs, "k1T"); cv1 = unT(rs, "cvT"); q1 = unT(rs, "q1T")
    krows = np.ascontiguousarray(k1.reshape(4, L, D).transpose(1, 0, 2).reshape(TOK, D))
    cvrows = np.ascontiguousarray(cv1.reshape(4, L, D).transpose(1, 0, 2).reshape(TOK, D))
    W10 = _f32(inp["L1_attn_W1"]); W20 = _f32(inp["L1_attn_W2"])
    b10 = _f32(inp["L1_attn_b1"]).reshape(1, D)
    b20 = _f32(inp["L1_attn_b2"]).reshape(1, D)
    dz = np.zeros((TOK, D), bfdt); ep = np.zeros((TOK, D), bfdt)
    dS1 = np.zeros((8, D, D), np.float32); dS2 = np.zeros((8, D, D), np.float32)
    dS2T = np.zeros((8, D, D), np.float32)
    db1 = np.zeros((8, 1, D), np.float32); db2 = np.zeros((8, 1, D), np.float32)
    nc = _get("ttt", build_ttt_sweep)
    for sweep in range(4):
        ims = []
        for c in cores:
            sl = slice(c * TPC, (c + 1) * TPC)
            p1 = np.zeros((7, D, D), np.float32); p2 = np.zeros((7, D, D), np.float32)
            p2t = np.zeros((7, D, D), np.float32)
            pb1 = np.zeros((7, 1, D), np.float32); pb2 = np.zeros((7, 1, D), np.float32)
            p1[:c] = dS1[:c]; p2[:c] = dS2[:c]; p2t[:c] = dS2T[:c]
            pb1[:c] = db1[:c]; pb2[:c] = db2[:c]
            ims.append({"Kfm": np.ascontiguousarray(krows[sl].T),
                        "Krm": np.ascontiguousarray(krows[sl]),
                        "cVrm": np.ascontiguousarray(cvrows[sl]),
                        "W10": W10, "W20": W20,
                        "W2T0": np.ascontiguousarray(W20.T),
                        "b10": b10, "b20": b20,
                        "dS1p": p1, "dS2p": p2, "dS2Tp": p2t,
                        "db1p": pb1, "db2p": pb2,
                        "DZ1p": np.ascontiguousarray(dz[sl]),
                        "Epi": np.ascontiguousarray(ep[sl])})
        rs = RUN(nc, ims)
        for c in cores:
            sl = slice(c * TPC, (c + 1) * TPC)
            dS1[c] = np.asarray(rs[c]["S1o"])
            dS2T[c] = np.asarray(rs[c]["S2To"])
            dS2[c] = np.ascontiguousarray(np.asarray(rs[c]["S2To"]).T)
            db1[c] = np.asarray(rs[c]["b1o"]); db2[c] = np.asarray(rs[c]["b2o"])
            dz[sl] = np.asarray(rs[c]["DZ1o"]); ep[sl] = np.asarray(rs[c]["Eo"])
    nc = _get("l1f", build_l1_final)
    q1sh = [np.ascontiguousarray(q1[c * TPC:(c + 1) * TPC].T) for c in cores]
    rs = RUN(nc, [{"xT": sx, "qT": sq, "W10": W10, "W20": W20, "b10": b10,
                   "b20": b20, "dS1a": dS1, "dS2a": dS2, "db1a": db1, "db2a": db2}
                  for sx, sq in zip(shT(x), q1sh)])
    x = unT(rs, "yT").astype(np.float32)

    nc = _get("l1ffn", build_l1_ffn)
    rs = RUN(nc, [{"xT": s, "Wg": _f32(inp["L1_ffn_gate_w"]),
                   "gb": _f32(inp["L1_ffn_gate_b"]).reshape(1, 8),
                   "We": _f32(inp["L1_ffn_expert_w"]),
                   "be": _f32(inp["L1_ffn_expert_b"])} for s in shT(x)])
    x = unT(rs, "yT").astype(np.float32)

    nc = _get("wvs", build_wvsum)
    rs = RUN(nc, [{"Wv": _f32(inp["L2_attn_Wv"]),
                   "bv": _f32(inp["L2_attn_bv"]).reshape(1, -1)} for _ in cores])
    Wt = np.asarray(rs[0]["Wt"]); bt = np.asarray(rs[0]["bt"])
    spec2 = {"projs": [("q2", D, "copy"), ("k2", D, "copy"), ("vt", D, "copy")],
             "biases": {"q2", "k2", "vt"}, "gamma_beta": True}
    nc = _get("l2p", lambda: build_proj(spec2, "l2p"))
    rs = RUN(nc, [{"xT": s, "W_q2": _f32(inp["L2_attn_Wq"]),
                   "W_k2": _f32(inp["L2_attn_Wk"]), "W_vt": Wt,
                   "b_q2": _f32(inp["L2_attn_bq"]).reshape(1, D),
                   "b_k2": _f32(inp["L2_attn_bk"]).reshape(1, D), "b_vt": bt,
                   "gam": _col(inp["L2_attn_gamma"]), "betap": _col(inp["L2_attn_beta"])}
                  for s in shT(x)])
    q2 = unT(rs, "q2T"); k2 = unT(rs, "k2T"); vt2 = unT(rs, "vtT")
    nc = _get("l2s", build_l2_scan)
    ims = []
    for c in cores:
        kf, qf, vr = [], [], []
        for i in range(4):
            bh = c * 4 + i
            b, h = bh // 8, bh % 8
            sl = slice(b * L, (b + 1) * L)
            hs = slice(h * 128, (h + 1) * 128)
            kf.append(k2[sl, hs].T); qf.append(q2[sl, hs].T); vr.append(vt2[sl, hs])
        ims.append({"kf": np.ascontiguousarray(np.stack(kf)),
                    "qf": np.ascontiguousarray(np.stack(qf)),
                    "vrm": np.ascontiguousarray(np.stack(vr))})
    rs = RUN(nc, ims)
    att2 = np.zeros((TOK, D), bfdt)
    for c in cores:
        for i in range(4):
            bh = c * 4 + i
            b, h = bh // 8, bh % 8
            pv = np.asarray(rs[c]["OUT"])[i].T
            att2[b * L + h * 1024:b * L + (h + 1) * 1024, :] = pv.reshape(1024, D)
    nc = _get("l2pf", build_l2_postffn)
    ims = []
    for c, (sx, sa) in enumerate(zip(shT(x), shTb(att2))):
        if c % 2 == 1:
            xl = _col(x[c * TPC - 1])
        else:
            xl = np.zeros((128, NDT), np.float32)
        ims.append({"xT": sx, "attnT": sa,
                    "WrT": np.ascontiguousarray(_f32(inp["L2_ffn_W_r"]).T),
                    "WkT": np.ascontiguousarray(_f32(inp["L2_ffn_W_k"]).T),
                    "WvT": np.ascontiguousarray(_f32(inp["L2_ffn_W_v"]).T),
                    "mur": _col(inp["L2_ffn_mu_r"]), "muk": _col(inp["L2_ffn_mu_k"]),
                    "xlast": xl})
    rs = RUN(nc, ims)
    x = unT(rs, "yT").astype(np.float32)

    spec3 = {"projs": [("q3", D, "copy"), ("k3", D, "copy"), ("v3", D, "copy")],
             "biases": set()}
    nc = _get("l3p", lambda: build_proj(spec3, "l3p"))
    rs = RUN(nc, [{"xT": s, "W_q3": np.ascontiguousarray(_f32(inp["L3_attn_q_w"]).T),
                   "W_k3": np.ascontiguousarray(_f32(inp["L3_attn_k_w"]).T),
                   "W_v3": np.ascontiguousarray(_f32(inp["L3_attn_v_w"]).T)}
                  for s in shT(x)])
    q3 = unT(rs, "q3T"); k3 = unT(rs, "k3T"); v3 = unT(rs, "v3T")
    nc = _get("l3s", build_l3_scan)
    ims = []
    for c in cores:
        kf, qf, vr = [], [], []
        for i in range(8):
            bh = c * 8 + i
            b, h = bh // 16, bh % 16
            sl = slice(b * L, (b + 1) * L)
            hs = slice(h * 64, (h + 1) * 64)
            kf.append(k3[sl, hs].T); qf.append(q3[sl, hs].T); vr.append(v3[sl, hs])
        ims.append({"kf": np.ascontiguousarray(np.stack(kf)),
                    "qf": np.ascontiguousarray(np.stack(qf)),
                    "vrm": np.ascontiguousarray(np.stack(vr))})
    rs = RUN(nc, ims)
    att3 = np.zeros((TOK, D), bfdt)
    for c in cores:
        for i in range(8):
            bh = c * 8 + i
            b, h = bh // 16, bh % 16
            att3[b * L:(b + 1) * L, h * 64:(h + 1) * 64] = np.asarray(rs[c]["OUT"])[i].T
    nc = _get("l3pg", build_l3_postgate)
    rs = RUN(nc, [{"xT": sx, "attT": sa,
                   "WoT": np.ascontiguousarray(_f32(inp["L3_attn_out_w"]).T),
                   "ob": _f32(inp["L3_attn_out_b"]).reshape(1, D),
                   "Wg": np.ascontiguousarray(_f32(inp["L3_ffn_w_gate"]).T),
                   "ebias": _f32(inp["L3_ffn_expert_bias"]).reshape(1, 8),
                   "Sw1T": np.ascontiguousarray(_f32(inp["L3_ffn_sw1"]).T),
                   "Sw3T": np.ascontiguousarray(_f32(inp["L3_ffn_sw3"]).T),
                   "Sw2T": np.ascontiguousarray(_f32(inp["L3_ffn_sw2"]).T)}
                  for sx, sa in zip(shT(x), shTb(att3))])
    x = unT(rs, "yT").astype(np.float32)
    cnts = np.stack([np.asarray(r["cnto"]).reshape(1, 8) for r in rs])
    xn8 = np.asarray(rs[0]["xn8o"])
    e1s = [np.asarray(r["e1o"]) for r in rs]
    e2s = [np.asarray(r["e2o"]) for r in rs]
    wrs = [np.asarray(r["wro"]) for r in rs]
    nc = _get("l3r", build_l3_routed)
    ims = []
    for c in cores:
        srcm = np.zeros((2 * TPC, 8), bfdt)
        srcm[0::2] = e1s[c]; srcm[1::2] = e2s[c]
        wrow = np.zeros((2 * TPC,), np.float32)
        wrow[0::2] = wrs[c][:, 0]; wrow[1::2] = wrs[c][:, 1]
        ims.append({"cnts": cnts, "srcmT": np.ascontiguousarray(srcm.T),
                    "wcol": np.ascontiguousarray(wrow.reshape(-1, 128).T),
                    "xn8": xn8, "w1r": _f32(inp["L3_ffn_w1"]),
                    "w3r": _f32(inp["L3_ffn_w3"]),
                    "iot": np.tile(np.arange(128, dtype=np.float32), (8, 1)) + c * 2 * TPC})
    rs = RUN(nc, ims)
    Hparts = np.stack([np.asarray(r["Hpart"]) for r in rs])
    nc = _get("l3f", build_l3_final)
    rs = RUN(nc, [{"xT": s, "Hparts": Hparts, "w2r": _f32(inp["L3_ffn_w2"]),
                   "mski": np.asarray([[1.0 if c == 0 else 0.0]], np.float32)}
                  for c, s in enumerate(shT(x))])
    x = unT(rs, "yT").astype(np.float32)
    return x.reshape(4, L, D)
